# version 31
# baseline (speedup 1.0000x reference)
import contextlib
import numpy as np
import concourse.bass as bass
import concourse.mybir as mybir
from concourse.bass_utils import run_bass_kernel_spmd

F32 = mybir.dt.float32
F32R = mybir.dt.float32r
F16 = mybir.dt.float16
U8 = mybir.dt.uint8
AF = mybir.ActivationFunctionType
ALU = mybir.AluOpType

L = 2048
D = 1024
NCH = 8
TT = 512
NT = 4
EPS_LN = 1e-5
EPS_RMS = 1e-6
EPS_SSM = 1e-6


class Sem:
    def __init__(self, h):
        self.h = h
        self.n = 0


class Mem:

    LO = 17408
    HI = 229344

    def __init__(self, nc):
        self.nc = nc
        self.free_list = [(self.LO, self.HI)]
        self.live = {}
        self.peak = 0

    def alloc(self, name, shape, dtype):
        nbytes = 1
        for s in shape[1:]:
            nbytes *= s
        nbytes *= {F32: 4, F32R: 4, F16: 2, U8: 1}[dtype]
        nbytes = (nbytes + 63) & ~63
        fits = [(b - a, idx) for idx, (a, b) in enumerate(self.free_list)
                if b - a >= nbytes]
        fits.sort()
        for _, idx in fits[:1]:
            a, b = self.free_list[idx]
            if True:
                self.free_list[idx] = (a + nbytes, b)
                if self.free_list[idx][0] == self.free_list[idx][1]:
                    self.free_list.pop(idx)
                h = self.nc.alloc_sbuf_tensor_at(name, list(shape), dtype,
                                                 offset=a)
                self.live[name] = (a, nbytes)
                used = self.HI - self.LO - sum(e - s for s, e in self.free_list)
                self.peak = max(self.peak, used)
                return h
        raise RuntimeError(f"manual SBUF OOM at {name} ({nbytes}B); "
                           f"free={self.free_list}")

    def free(self, *names):
        for n in names:
            a, nbytes = self.live.pop(n)
            self.free_list.append((a, a + nbytes))
        self.free_list.sort()
        out = []
        for s, e in self.free_list:
            if out and out[-1][1] >= s:
                out[-1] = (out[-1][0], max(out[-1][1], e))
            else:
                out.append((s, e))
        self.free_list = [tuple(x) for x in out]


def build_program(debug=False):
    nc = bass.Bass("TRN2", target_bir_lowering=False)

    xTd = nc.dram_tensor("xT", [D, L], F32, kind="ExternalInput").ap()
    inwTd = nc.dram_tensor("in_wT", [D, 2 * D], F32, kind="ExternalInput").ap()
    pwTd = nc.dram_tensor("pw_wT", [D, D], F32, kind="ExternalInput").ap()
    outwTd = nc.dram_tensor("out_wT", [D, D], F32, kind="ExternalInput").ap()
    w1d = nc.dram_tensor("w1", [D, D], F32, kind="ExternalInput").ap()
    w2d = nc.dram_tensor("w2", [D, D], F32, kind="ExternalInput").ap()
    w3d = nc.dram_tensor("w3", [D, D], F32, kind="ExternalInput").ap()
    winTd = nc.dram_tensor("winT2", [D, 128], F32, kind="ExternalInput").ap()
    woutd = nc.dram_tensor("woutT2", [128, D], F32, kind="ExternalInput").ap()
    packd = nc.dram_tensor("pack", [128, 84], F32, kind="ExternalInput").ap()
    onesr32d = nc.dram_tensor("onesr32", [128, 128], F32R,
                              kind="ExternalInput").ap()
    outd = nc.dram_tensor("outT", [NT, NCH, 128, TT], F32,
                          kind="ExternalOutput").ap()

    es = contextlib.ExitStack()
    with es:
        mem = Mem(nc)

        banks = [es.enter_context(nc.psum_tensor(f"pb{i}", [128, TT], F32))
                 for i in range(8)]

        def sem(name):
            return Sem(es.enter_context(nc.semaphore(name)))

        boot = sem("boot")
        done = sem("done")
        s_t = sem("s_t")
        s_v = sem("s_v")
        s_s = sem("s_s")
        s_g = sem("s_g")
        q_sm = sem("q_sm")
        q_sm2 = sem("q_sm2")
        q_x0 = sem("q_x0")
        q_x1 = sem("q_x1")
        q_st0 = sem("q_st0")
        q_st1 = sem("q_st1")
        q_st2 = sem("q_st2")
        q_st3 = sem("q_st3")
        g_st1 = sem("g_st1")
        g_st3 = sem("g_st3")
        q_xt = [sem(f"q_xt{i}") for i in range(4)]
        g_wx = sem("g_wx")
        g_wx2 = sem("g_wx2")
        g_wz = sem("g_wz")
        g_pw = sem("g_pw")
        g_w12 = sem("g_w12")
        g_w34 = sem("g_w34")
        g_sm = sem("g_sm")
        g_x0 = sem("g_x0")
        g_x1 = sem("g_x1")
        q_dbg = sem("q_dbg")
        dbg_tensors = {}
        ALL_SEMS = [boot, done, s_t, s_v, s_s, s_g, q_sm, q_sm2, q_x0, q_x1, q_st0,
                    q_st1, q_st2, q_st3, g_st1, g_st3,
                    g_wx, g_wx2, g_wz, g_pw, g_w12, g_w34, g_sm,
                    g_x0, g_x1, q_dbg] + q_xt
        BOOT = 1 << 16

        ops = {k: [] for k in "qgtvs"}
        last_wait = {k: {} for k in "qgtvs"}

        def emit(eng, fn):
            ops[eng].append(fn)

        def _mkwaits(eng, waits):
            out = []
            for (h, v) in waits:
                if v <= 0:
                    continue
                if last_wait[eng].get(h, -1) >= v:
                    continue
                last_wait[eng][h] = v
                out.append((h, v))
            return tuple(out)

        def op(eng, build_fn, sm=None, inc=1, waits=(), drain=False):
            inline_w = None
            if eng == 'v' and s_v.n > 0:
                inline_w = (s_v.h, s_v.n)
            elif eng == 's' and s_s.n > 0:
                inline_w = (s_s.h, s_s.n)
            ws = _mkwaits(eng, waits)
            if inline_w is None and ws:
                inline_w, ws = ws[0], ws[1:]
            if sm is not None:
                sm.n += inc
            smh = sm.h if sm is not None else None

            def f(e, bf=build_fn, ws=ws, iw=inline_w, smh=smh, k=inc, dr=drain):
                for (h, v) in ws:
                    e.wait_ge(h, v)
                ins = bf(e)
                if iw is not None:
                    ins._wait_ge(iw[0], iw[1])
                if smh is not None:
                    ins.then_inc(smh, k)
                if dr:
                    e.drain()
            emit(eng, f)
            return sm.n if sm is not None else None

        def dma(eng, dst, src, sm, waits=()):
            return op(eng, lambda e, d=dst, s=src: e.dma_start(out=d, in_=s),
                      sm=sm, inc=16, waits=waits)

        def barrier():
            w = [(s_t.h, s_t.n), (s_v.h, s_v.n), (s_s.h, s_s.n)]
            if s_g.n > 0:
                w.append((s_g.h, s_g.n))
            return w

        def dbg(name, src_ap, shape, dtype, waits):
            if not debug:
                return
            d = nc.dram_tensor("dbg_" + name, list(shape), dtype,
                               kind="ExternalOutput").ap()
            dbg_tensors[name] = d
            dma('q', d, src_ap, q_dbg, waits=waits)

        class Banks:
            def __init__(self):
                self.guard = [None] * 8
                self.i = 0

            def take(self):
                i = self.i % 8
                self.i += 1
                g = self.guard[i]
                self.guard[i] = None
                return i, ([g] if g else [])

            def set(self, i, semobj, val):
                self.guard[i] = (semobj.h, val)

        BK = Banks()

        def t_group(items, waits=()):
            norm = []
            for it in items:
                if len(it) == 5:
                    norm.append(((),) + tuple(it))
                else:
                    w = _mkwaits('t', it[0])
                    norm.append((w,) + tuple(it[1:]))
            ws = _mkwaits('t', waits)
            s_t.n += 1

            def f(e, it=tuple(norm), ws=ws, smh=s_t.h):
                used_w0 = not ws or (it and not it[0][0])
                for (h, v) in (ws[1:] if used_w0 else ws):
                    e.wait_ge(h, v)
                first = True
                ins = None
                for (iw, o, lh, r, st, sp) in it:
                    for (h, v) in iw[1:]:
                        e.wait_ge(h, v)
                    ins = e.matmul(o, lh, r, start=st, stop=sp)
                    if iw:
                        ins._wait_ge(iw[0][0], iw[0][1])
                    elif first and ws:
                        ins._wait_ge(ws[0][0], ws[0][1])
                    first = False
                ins.then_inc(smh, 1)
            emit('t', f)
            return s_t.n

        def _boot_q(e, sems=tuple(s.h for s in ALL_SEMS), bh=boot.h, big=BOOT):
            for sh in sems:
                e.sem_clear(sh)
            e.sem_inc(bh, big)
        emit('q', _boot_q)
        for eng in 'gtvs':
            emit(eng, lambda e, bh=boot.h, big=BOOT: e.wait_ge(bh, big))

        class APView:
            def __init__(self, base_ap):
                self._ap = base_ap

            def ap(self):
                return self._ap

            def __getitem__(self, idx):
                return self._ap[idx]

        ones16 = mem.alloc("ones16_s", [128, 128], F16)
        onesr32 = mem.alloc("onesr32_s", [128, 128], F32R)
        onesr16 = mem.alloc("onesr16_s", [1, TT], F16)
        onesr16b = mem.alloc("onesr16b_s", [1, 128], F16)
        consts = mem.alloc("consts_s", [128, 2], F32)
        pack = mem.alloc("pack_s", [128, 84], F32)
        ssm = APView(pack.ap()[:, 0:4])
        vec = APView(pack.ap()[:, 4:60])
        dww = APView(pack.ap()[:, 60:84])
        prm = mem.alloc("prm_s", [128, 12], F32)
        cvb = mem.alloc("cvb_s", [128, 8], F32)
        e0n = mem.alloc("e0n_s", [128, 8], F32)
        e1n = mem.alloc("e1n_s", [128, 8], F32)
        winT16 = mem.alloc("winT16_s", [128, D], F16)
        wout16 = mem.alloc("wout16_s", [128, D], F16)
        woutT2f = mem.alloc("woutT2f_s", [128, D], F32)

        negmu = mem.alloc("negmu_r", [1, L], F16)
        nm2row = mem.alloc("nm2_r", [1, L], F16)
        colsum = mem.alloc("colsum_r", [1, 2 * D], F16)

        s1b = mem.alloc("s1b", [128, L], F32)
        s2b = mem.alloc("s2b", [128, L], F32)

        xTs = mem.alloc("xTs", [128, NCH * L], F16)
        x2buf = mem.alloc("x2buf", [128, 2 * L], F16)
        SxB = mem.alloc("SxB", [128, L], F32)
        Sx2B = mem.alloc("Sx2B", [128, L], F32)
        wxs = mem.alloc("wxs", [128, NCH * D], F16)
        wzs = mem.alloc("wzs", [128, NCH * D], F16)
        xt32 = mem.alloc("xt32", [128, NCH * L], F32R)

        def xT(c):
            return xTs.ap()[:, L * c:L * (c + 1)]

        def x32c(c):
            return xt32.ap()[:, L * c:L * (c + 1)]

        def wx(k):
            return wxs.ap()[:, D * k:D * (k + 1)]

        def wz(k):
            return wzs.ap()[:, D * k:D * (k + 1)]

        dma('q', xt32.ap()[:, 0:2 * L].rearrange("p (c w) -> p c w", w=L),
            xTd.bitcast(F32R)[0:256, :].rearrange("(c p) w -> p c w", p=128),
            q_xt[0])
        dma('q', onesr32[:], onesr32d, q_sm)
        dma('q', pack[:], packd, q_sm)
        op('v', lambda e: e.memset(ones16[:], 1.0), sm=s_v)
        op('v', lambda e: e.memset(onesr16[:], 1.0), sm=s_v, drain=True)
        op('v', lambda e: e.memset(onesr16b[:], 1.0), sm=s_v, drain=True)
        op('v', lambda e: e.memset(consts.ap()[:, 0:1], EPS_LN), sm=s_v,
           drain=True)
        op('v', lambda e: e.memset(consts.ap()[:, 1:2], EPS_LN * EPS_RMS),
           sm=s_v, drain=True)
        v_after_consts = s_v.n
        for i in range(1, 4):
            dma('q', xt32.ap()[:, 2 * L * i:2 * L * (i + 1)]
                .rearrange("p (c w) -> p c w", w=L),
                xTd.bitcast(F32R)[256 * i:256 * (i + 1), :]
                .rearrange("(c p) w -> p c w", p=128), q_xt[i])
        dma('q', woutT2f[:], woutd, q_sm2)
        dma('g', wxs.ap()[:, 0:4 * D].rearrange("p (k o) -> p k o", o=D),
            inwTd[0:512, 0:D].rearrange("(k p) o -> p k o", p=128), g_wx)
        dma('g', wxs.ap()[:, 4 * D:].rearrange("p (k o) -> p k o", o=D),
            inwTd[512:1024, 0:D].rearrange("(k p) o -> p k o", p=128), g_wx2)
        dma('g', wzs.ap().rearrange("p (k o) -> p k o", o=D),
            inwTd[:, D:2 * D].rearrange("(k p) o -> p k o", p=128), g_wz)
        dma('g', winT16.ap().rearrange("p (k o) -> p k o", o=128),
            winTd.rearrange("(k p) o -> p k o", p=128), g_sm)

        ns_col = lambda k: vec.ap()[:, k:k + 1]
        bz_col = lambda o: vec.ap()[:, 16 + o:17 + o]
        pwb_col = lambda e: vec.ap()[:, 32 + e:33 + e]
        fb_col = lambda o: vec.ap()[:, 40 + o:41 + o]
        outb_col = lambda o: vec.ap()[:, 48 + o:49 + o]
        c_ln = consts.ap()[:, 0:1]
        c_ee = consts.ap()[:, 1:2]

        SMALL_LD = [(q_sm.h, q_sm.n)]
        XTLD = [[(q_xt[c // 2].h, q_xt[c // 2].n)] for c in range(NCH)]

        ns16 = mem.alloc("ns16_s", [128, 8], F16)

        P_dt = prm.ap()[:, 0:1]
        P_A = prm.ap()[:, 1:2]
        P_Abar = prm.ap()[:, 2:3]
        P_Bbar = prm.ap()[:, 3:4]
        P_eA = prm.ap()[:, 4:5]
        P_t0 = prm.ap()[:, 5:6]
        P_t1 = prm.ap()[:, 6:7]
        P_t2 = prm.ap()[:, 7:8]
        P_m = prm.ap()[:, 8:9]
        w0s = dww.ap()[:, 0:8]
        w1s = dww.ap()[:, 8:16]
        w2s = dww.ap()[:, 16:24]
        op('v', lambda e: e.tensor_tensor(cvb[:], w0s, w1s, ALU.add), sm=s_v,
           waits=SMALL_LD, drain=True)
        op('v', lambda e: e.tensor_tensor(cvb[:], cvb[:], w2s, ALU.add),
           sm=s_v, drain=True)
        op('v', lambda e: e.tensor_tensor(cvb[:], cvb[:], vec.ap()[:, 8:16],
                                          ALU.mult), sm=s_v, drain=True)
        op('v', lambda e: e.tensor_tensor(cvb[:], cvb[:], vec.ap()[:, 24:32],
                                          ALU.add), sm=s_v, drain=True)
        op('v', lambda e: e.tensor_tensor(e0n[:], vec.ap()[:, 8:16], w0s,
                                          ALU.mult), sm=s_v, drain=True)
        op('v', lambda e: e.tensor_scalar(e0n[:], e0n[:], -1.0, None, ALU.mult),
           sm=s_v, drain=True)
        op('v', lambda e: e.tensor_tensor(e1n[:], vec.ap()[:, 8:16], w2s,
                                          ALU.mult), sm=s_v, drain=True)
        op('v', lambda e: e.tensor_scalar(e1n[:], e1n[:], -1.0, None, ALU.mult),
           sm=s_v, drain=True)
        op('v', lambda e: e.tensor_copy(ns16[:], vec.ap()[:, 0:8]), sm=s_v,
           drain=True)
        v_after_ns16 = s_v.n

        sq_done = []
        statgrp_after = []
        for c in range(NCH):
            w = list(XTLD[c])
            if c >= 2:
                w.append((s_t.h, statgrp_after[c - 2]))
            x2slot = x2buf.ap()[:, L * (c % 2):L * (c % 2 + 1)]
            if c % 2 == 0:
                op('s', lambda e, xc=x32c(c), d=x2slot:
                   e.activation(d, xc, AF.Square), sm=s_s, waits=w)
                sq_done.append((s_s, s_s.n))
            else:
                op('v', lambda e, xc=x32c(c), d=x2slot:
                   e.tensor_tensor(d, xc, xc, ALU.mult), sm=s_v, waits=w)
                sq_done.append((s_v, s_v.n))
            items = []
            for tt in range(NT):
                items.append((banks[tt].ap()[:, :], onesr32[:],
                              x32c(c)[:, TT * tt:TT * (tt + 1)],
                              c == 0, c == NCH - 1))
            for tt in range(NT):
                items.append((banks[4 + tt].ap()[:, :], ones16[:],
                              x2slot[:, TT * tt:TT * (tt + 1)],
                              c == 0, c == NCH - 1))
            t_group(items, waits=XTLD[c] + [(sq_done[c][0].h, sq_done[c][1]),
                                            (s_v.h, v_after_consts)]
                    + (SMALL_LD if c == 0 else []))
            statgrp_after.append(s_t.n)
        t_after_stats = s_t.n

        op('s', lambda e: e.activation(P_dt, ssm.ap()[:, 3:4], AF.Exp),
           sm=s_s, waits=SMALL_LD)
        op('s', lambda e: e.activation(P_eA, ssm.ap()[:, 0:1], AF.Exp), sm=s_s)
        s_after_pexp = s_s.n
        op('v', lambda e: e.tensor_scalar(P_A, P_eA, -1.0, None, ALU.mult),
           sm=s_v, waits=[(s_s.h, s_after_pexp)], drain=True)
        op('v', lambda e: e.tensor_tensor(P_t0, P_A, P_dt, ALU.mult), sm=s_v,
           drain=True)
        v_after_pt0 = s_v.n
        op('v', lambda e: e.tensor_single_scalar(P_m, P_eA, EPS_SSM,
                                                 ALU.is_lt), sm=s_v, drain=True)
        op('v', lambda e: e.tensor_scalar(P_t1, P_A, -1.0, 1.0, ALU.mult,
                                          ALU.add), sm=s_v, drain=True)
        op('v', lambda e: e.tensor_tensor(P_t1, P_m, P_t1, ALU.mult), sm=s_v,
           drain=True)
        op('v', lambda e: e.tensor_tensor(P_t1, P_A, P_t1, ALU.add), sm=s_v,
           drain=True)
        op('v', lambda e: e.reciprocal(P_t2, P_t1), sm=s_v, drain=True)

        for tt in range(NT):
            op('s', lambda e, b=banks[tt], d=SxB.ap()[:, TT * tt:TT * (tt + 1)]:
               e.activation(d, b.ap()[:, :], AF.Copy),
               sm=s_s, waits=[(s_t.h, t_after_stats)])
        for tt in range(NT):
            op('s', lambda e, b=banks[4 + tt],
               d=Sx2B.ap()[:, TT * tt:TT * (tt + 1)]:
               e.activation(d, b.ap()[:, :], AF.Copy), sm=s_s)
        s_after_statevac = s_s.n
        for i in range(8):
            BK.set(i, s_s, s_after_statevac)
        BK.i = 8

        SEV = [(s_s.h, s_after_statevac)]
        op('v', lambda e: e.tensor_scalar(negmu[0:1, :], SxB.ap()[0:1, :],
                                          -1.0 / D, None, ALU.mult),
           sm=s_v, waits=SEV)
        op('s', lambda e: e.activation(SxB[:], SxB[:], AF.Square, scale=1.0 / D),
           sm=s_s, waits=[(s_v.h, s_v.n)])
        op('v', lambda e: e.tensor_scalar(Sx2B[:], Sx2B[:], 1.0 / D, None,
                                          ALU.mult), sm=s_v)
        op('v', lambda e: e.tensor_tensor(Sx2B[:], Sx2B[:], SxB[:], ALU.subtract),
           sm=s_v, waits=[(s_s.h, s_s.n)])
        v_after_var = s_v.n
        op('s', lambda e: e.activation(SxB[:], Sx2B[:], AF.Ln, bias=c_ee,
                                       scale=1.0 + EPS_RMS),
           sm=s_s, waits=[(s_v.h, v_after_var)])
        op('s', lambda e: e.activation(s2b[:], SxB[:], AF.Exp, scale=-0.5),
           sm=s_s)
        s_after_s2 = s_s.n
        ts16_done = []
        for c in range(NCH):
            eng, sm_ = ('v', s_v)
            op(eng, lambda e, c=c: e.scalar_tensor_tensor(
                xT(c), x32c(c), pack.ap()[:, 4 + c:5 + c], s2b[:],
                ALU.mult, ALU.mult),
               sm=sm_, waits=[(s_s.h, s_after_s2), (s_t.h, t_after_stats)])
            ts16_done.append((sm_, sm_.n))
        op('v', lambda e: e.tensor_tensor(nm2row[0:1, :], negmu.ap()[0:1, :],
                                          s2b.ap()[0:1, :], ALU.mult), sm=s_v)
        v_after_nm2 = s_v.n
        op('s', lambda e: e.activation(P_Abar, P_t0, AF.Exp), sm=s_s,
           waits=[(s_v.h, v_after_pt0)])
        s_after_abar = s_s.n
        op('v', lambda e: e.tensor_scalar(P_t0, P_Abar, -1.0, None, ALU.add),
           sm=s_v, waits=[(s_s.h, s_after_abar)], drain=True)
        op('v', lambda e: e.tensor_tensor(P_t0, P_t0, P_t2, ALU.mult), sm=s_v,
           drain=True)
        op('v', lambda e: e.tensor_tensor(P_t1, P_dt, P_t0, ALU.subtract),
           sm=s_v, drain=True)
        op('v', lambda e: e.tensor_tensor(P_t1, P_m, P_t1, ALU.mult), sm=s_v,
           drain=True)
        op('v', lambda e: e.tensor_tensor(P_t0, P_t0, P_t1, ALU.add), sm=s_v,
           drain=True)
        op('v', lambda e: e.tensor_tensor(P_Bbar, P_t0, ssm.ap()[:, 1:2],
                                          ALU.mult), sm=s_v, drain=True)
        op('v', lambda e: e.tensor_scalar(wout16[:], woutT2f[:],
                                          ssm.ap()[:, 2:3], None, ALU.mult),
           sm=s_v, waits=[(q_sm2.h, 16)])
        op('s', lambda e: e.activation(s1b[:], Sx2B[:], AF.Ln, bias=c_ln),
           sm=s_s)
        op('s', lambda e: e.activation(s1b[:], s1b[:], AF.Exp, scale=-0.5),
           sm=s_s)
        s_after_s1 = s_s.n
        v_after_statmath = s_v.n
        dbg("s1b", s1b[:], [128, L], F32, [(s_s.h, s_s.n)])
        dbg("s2b", s2b[:], [128, L], F32, [(s_s.h, s_s.n)])
        dbg("negmu", negmu[:], [1, L], F16, [(s_v.h, s_v.n)])

        mem.free("x2buf", "SxB", "Sx2B", "xt32")
        x1buf = mem.alloc("x1buf", [128, 2 * L], F32)
        convbuf = mem.alloc("convbuf", [128, L], F32)
        xc1 = mem.alloc("xc1", [128, NCH * L], F16)
        gate16 = mem.alloc("gate16", [128, NCH * L], F16)
        pw16 = mem.alloc("pw16", [128, NCH * D], F16)
        dma('g', pw16.ap().rearrange("p (k o) -> p k o", o=D),
            pwTd.rearrange("(k p) o -> p k o", p=128), g_pw, waits=barrier())

        cs = []
        for half, wfun, gsem in ((0, wx, g_wx2), (1, wz, g_wz)):
            for j2 in range(2):
                bi, bwc = BK.take()
                items = [(banks[bi].ap()[0:1, :], ns16.ap()[:, k:k + 1],
                          wfun(k)[:, TT * j2:TT * (j2 + 1)],
                          k == 0, k == NCH - 1) for k in range(NCH)]
                t_group(items, waits=bwc + [(s_v.h, v_after_ns16),
                                            (g_wx.h, g_wx.n),
                                            (gsem.h, gsem.n)])
                cs.append((bi, half, j2, s_t.n))
        for (bi, half, j2, tcnt) in cs:
            dst = colsum.ap()[0:1, D * half + TT * j2:D * half + TT * (j2 + 1)]
            op('s', lambda e, b=banks[bi], d=dst:
               e.activation(d, b.ap()[0:1, :], AF.Copy), sm=s_s,
               waits=[(s_t.h, tcnt)])
            BK.set(bi, s_s, s_s.n)
        s_after_colsum = s_s.n

        conv_done = []
        half_pend = {}
        for o in (0, 1):
            bks = [BK.take() for _ in range(NT)]
            bw = [w for _, ws in bks for w in ws]
            items = []
            for k in range(4):
                for t in range(NT):
                    items.append((((ts16_done[k][0].h, ts16_done[k][1]),),
                                  banks[bks[t][0]].ap()[:, :],
                                  wx(k)[:, 128 * o:128 * (o + 1)],
                                  xT(k)[:, TT * t:TT * (t + 1)],
                                  k == 0, False))
            t_group(items, waits=bw + [(g_wx.h, g_wx.n)])
            half_pend[o] = bks
        for o in range(NCH):
            if o in half_pend:
                bks = half_pend[o]
                bw = []
                items = []
                for k in range(4, NCH):
                    for t in range(NT):
                        items.append((((ts16_done[k][0].h, ts16_done[k][1]),),
                                      banks[bks[t][0]].ap()[:, :],
                                      wx(k)[:, 128 * o:128 * (o + 1)],
                                      xT(k)[:, TT * t:TT * (t + 1)],
                                      False, False))
            else:
                bks = [BK.take() for _ in range(NT)]
                bw = [w for _, ws in bks for w in ws]
                items = []
                for k in range(NCH):
                    for t in range(NT):
                        items.append((((ts16_done[k][0].h, ts16_done[k][1]),),
                                      banks[bks[t][0]].ap()[:, :],
                                      wx(k)[:, 128 * o:128 * (o + 1)],
                                      xT(k)[:, TT * t:TT * (t + 1)],
                                      k == 0, False))
            for t in range(NT):
                items.append((((s_s.h, s_after_colsum),
                               (s_v.h, v_after_nm2)),
                              banks[bks[t][0]].ap()[:, :],
                              colsum.ap()[0:1, 128 * o:128 * (o + 1)],
                              nm2row.ap()[0:1, TT * t:TT * (t + 1)],
                              False, True))
            t_group(items, waits=bw + [(g_wx2.h, g_wx2.n)])
            tcnt = s_t.n
            sw2 = [(s_t.h, tcnt)]
            if len(conv_done) >= 2:
                sw2.append((s_v.h, conv_done[-2]))
            x1sl = x1buf.ap()[:, L * (o % 2):L * (o % 2 + 1)]
            for t in range(NT):
                bi = bks[t][0]
                op('s', lambda e, b=banks[bi],
                   d=x1sl[:, TT * t:TT * (t + 1)]:
                   e.copy(d, b.ap()[:, :]),
                   sm=s_s, waits=sw2)
                BK.set(bi, s_s, s_s.n)
            xcc = xc1.ap()[:, L * o:L * (o + 1)]
            x1sl = x1buf.ap()[:, L * (o % 2):L * (o % 2 + 1)]
            op('v', lambda e, o=o, x1=x1sl: e.tensor_scalar(
                convbuf[:], x1, dww.ap()[:, 8 + o:9 + o],
                cvb.ap()[:, o:o + 1], ALU.mult, ALU.add), sm=s_v,
               waits=[(s_s.h, s_s.n)])
            op('v', lambda e, o=o, x1=x1sl: e.scalar_tensor_tensor(
                convbuf.ap()[:, 1:L], x1[:, 0:L - 1],
                dww.ap()[:, o:o + 1], convbuf.ap()[:, 1:L],
                ALU.mult, ALU.add), sm=s_v)
            op('v', lambda e, o=o, x1=x1sl: e.scalar_tensor_tensor(
                convbuf.ap()[:, 0:L - 1], x1[:, 1:L],
                dww.ap()[:, 16 + o:17 + o], convbuf.ap()[:, 0:L - 1],
                ALU.mult, ALU.add), sm=s_v)
            op('v', lambda e, o=o: e.tensor_scalar(
                convbuf.ap()[:, 0:1], convbuf.ap()[:, 0:1],
                e0n.ap()[:, o:o + 1], None, ALU.add), sm=s_v, drain=True)
            op('v', lambda e, o=o: e.tensor_scalar(
                convbuf.ap()[:, L - 1:L], convbuf.ap()[:, L - 1:L],
                e1n.ap()[:, o:o + 1], None, ALU.add), sm=s_v, drain=True)
            op('v', lambda e, d=xcc: e.tensor_copy(d, convbuf[:]), sm=s_v)
            conv_done.append(s_v.n)
        t_after_inx = s_t.n
        for o in range(NCH):
            bks = [BK.take() for _ in range(NT)]
            bw = [w for _, ws in bks for w in ws]
            items = []
            for k in range(NCH):
                for t in range(NT):
                    items.append((banks[bks[t][0]].ap()[:, :],
                                  wz(k)[:, 128 * o:128 * (o + 1)],
                                  xT(k)[:, TT * t:TT * (t + 1)],
                                  k == 0, False))
            for t in range(NT):
                items.append((((s_s.h, s_after_colsum),),
                              banks[bks[t][0]].ap()[:, :],
                              colsum.ap()[0:1, D + 128 * o:D + 128 * (o + 1)],
                              nm2row.ap()[0:1, TT * t:TT * (t + 1)],
                              False, True))
            t_group(items, waits=bw)
            tcnt = s_t.n
            for t in range(NT):
                bi = bks[t][0]
                op('s', lambda e, b=banks[bi], o=o,
                   d=gate16.ap()[:, L * o + TT * t:L * o + TT * (t + 1)]:
                   e.activation(d, b.ap()[:, :], AF.Sigmoid, bias=bz_col(o)),
                   sm=s_s, waits=[(s_t.h, tcnt)])
                BK.set(bi, s_s, s_s.n)
        t_after_z = s_t.n
        s_after_gate = s_s.n
        mem.free("xTs", "wxs", "wzs", "x1buf", "convbuf", "s2b")
        xconv16 = mem.alloc("xconv16", [128, NCH * L], F16)
        w1_16 = mem.alloc("w1_16", [128, NCH * D], F16)
        w2_16 = mem.alloc("w2_16", [128, NCH * D], F16)
        dma('g', w1_16.ap().rearrange("p (k o) -> p k o", o=D),
            w1d.rearrange("(k p) o -> p k o", p=128), g_w12, waits=barrier())
        dma('g', w2_16.ap().rearrange("p (k o) -> p k o", o=D),
            w2d.rearrange("(k p) o -> p k o", p=128), g_w12)
        x16s_ = [mem.alloc(f"x16slot{i}", [128, NCH * TT], F16)
                 for i in range(2)]
        for t in (0, 1):
            dma('g', x16s_[t].ap().rearrange("p (c w) -> p c w", w=TT),
                xTd[:, TT * t:TT * (t + 1)].rearrange("(c p) w -> p c w", p=128),
                (g_x0, g_x1)[t], waits=barrier())

        PWLD = [(g_pw.h, g_pw.n)]
        for e_ in range(NCH):
            bks = [BK.take() for _ in range(NT)]
            bw = [w for _, ws in bks for w in ws]
            items = []
            for k in range(NCH):
                for t in range(NT):
                    items.append((banks[bks[t][0]].ap()[:, :],
                                  pw16.ap()[:, D * k + 128 * e_:D * k + 128 * (e_ + 1)],
                                  xc1.ap()[:, L * k + TT * t:L * k + TT * (t + 1)],
                                  k == 0, k == NCH - 1))
            t_group(items, waits=bw + PWLD + [(s_v.h, conv_done[NCH - 1])])
            tcnt = s_t.n
            for t in range(NT):
                bi = bks[t][0]
                op('s', lambda e, b=banks[bi], e_=e_,
                   d=xconv16.ap()[:, L * e_ + TT * t:L * e_ + TT * (t + 1)]:
                   e.add(d, b.ap()[:, :], pwb_col(e_)),
                   sm=s_s, waits=[(s_t.h, tcnt)])
                BK.set(bi, s_s, s_s.n)
        s_after_pw = s_s.n
        t_after_pw = s_t.n

        mem.free("pw16", "xc1")
        u_scan = mem.alloc("u_scan", [128, L], F32)
        abart = mem.alloc("abart", [128, L], F32)
        S_scan = mem.alloc("S_scan", [128, L], F32)
        S16 = mem.alloc("S16", [128, L], F16)
        w3_16 = mem.alloc("w3_16", [128, NCH * D], F16)
        outw16 = mem.alloc("outw16", [128, NCH * D], F16)
        dma('g', w3_16.ap().rearrange("p (k o) -> p k o", o=D),
            w3d.rearrange("(k p) o -> p k o", p=128), g_w34, waits=barrier())
        dma('g', outw16.ap().rearrange("p (k o) -> p k o", o=D),
            outwTd.rearrange("(k p) o -> p k o", p=128), g_w34)

        colsum116 = mem.alloc("colsum116_r", [1, D], F16)
        W12LD = [(g_w12.h, g_w12.n)]
        c1banks = []
        for j in range(2):
            bi, bw = BK.take()
            items = [(banks[bi].ap()[0:1, :], ones16.ap()[:, 0:1],
                      w1_16.ap()[:, D * k + TT * j:D * k + TT * (j + 1)],
                      k == 0, k == NCH - 1) for k in range(NCH)]
            t_group(items, waits=bw + W12LD)
            c1banks.append((bi, j, s_t.n))
        for (bi, j, tcnt) in c1banks:
            op('s', lambda e, b=banks[bi],
               d=colsum116.ap()[0:1, TT * j:TT * (j + 1)]:
               e.activation(d, b.ap()[0:1, :], AF.Copy), sm=s_s,
               waits=[(s_t.h, tcnt)])
            BK.set(bi, s_s, s_s.n)
        s_after_c1 = s_s.n
        u_rev = u_scan.ap()[64:128, ::-1]
        bks = [BK.take() for _ in range(NT)]
        bw = [w for _, ws in bks for w in ws]
        items = []
        for k in range(NCH):
            for t in range(NT):
                items.append((banks[bks[t][0]].ap()[:, :],
                              winT16.ap()[:, 128 * k:128 * (k + 1)],
                              xconv16.ap()[:, L * k + TT * t:L * k + TT * (t + 1)],
                              k == 0, k == NCH - 1))
        t_group(items, waits=bw + [(g_sm.h, g_sm.n), (s_s.h, s_after_pw)])
        tcnt = s_t.n
        for t in range(NT):
            bi = bks[t][0]
            op('v', lambda e, b=banks[bi],
               d=u_scan.ap()[0:64, TT * t:TT * (t + 1)]:
               e.tensor_scalar(d, b.ap()[0:64, :], prm.ap()[0:64, 3:4], None,
                               ALU.mult), sm=s_v, waits=[(s_t.h, tcnt)])
            op('v', lambda e, b=banks[bi],
               d=u_rev[:, TT * t:TT * (t + 1)]:
               e.tensor_scalar(d, b.ap()[64:128, :], prm.ap()[64:128, 3:4],
                               None, ALU.mult), sm=s_v)
            BK.set(bi, s_v, s_v.n)
        op('v', lambda e: e.tensor_scalar(abart[:], u_scan[:], 0.0,
                                          P_Abar, ALU.mult, ALU.add), sm=s_v)
        op('v', lambda e: e.tensor_tensor_scan(S_scan[:], abart[:], u_scan[:],
                                               0.0, ALU.mult, ALU.add), sm=s_v)
        op('s', lambda e: e.activation(S16.ap()[0:64, :], S_scan.ap()[0:64, :],
                                       AF.Copy), sm=s_s,
           waits=[(s_v.h, s_v.n)])
        s_after_s16f = s_s.n
        op('v', lambda e: e.tensor_copy(S16.ap()[64:128, :],
                                        S_scan.ap()[64:128, ::-1]), sm=s_v)
        v_after_scan = s_v.n
        dbg("uscan", u_scan[:], [128, L], F32, [(s_v.h, s_v.n)])
        dbg("sscan", S_scan[:], [128, L], F32, [(s_v.h, s_v.n)])
        dbg("prm", prm[:], [128, 12], F32, [(s_v.h, s_v.n)])

        mem.free("xconv16", "S_scan", "u_scan", "abart")
        h1buf = mem.alloc("h1buf", [128, NCH * TT], F16)
        y16all = mem.alloc("y16all", [128, NCH * L], F16)
        ybuf = mem.alloc("ybuf", [128, 4 * TT], F16)
        W34LD = [(g_w34.h, g_w34.n)]
        h1ev = {}
        h1grp0 = {}
        for o in range(NCH):
            bi, bw = BK.take()
            items = [(banks[bi].ap()[:, :],
                      w1_16.ap()[:, D * k + 128 * o:D * k + 128 * (o + 1)],
                      x16s_[0].ap()[:, TT * k:TT * (k + 1)],
                      k == 0, False) for k in range(NCH)]
            items.append((banks[bi].ap()[:, :],
                          colsum116.ap()[0:1, 128 * o:128 * (o + 1)],
                          negmu.ap()[0:1, 0:TT], False, True))
            t_group(items, waits=bw + [(g_x0.h, g_x0.n),
                                       (s_s.h, s_after_c1)])
            tcnt = s_t.n
            h1grp0[(0, o)] = tcnt
            op('v', lambda e, b=banks[bi], o=o,
               d=h1buf.ap()[:, TT * o:TT * (o + 1)],
               s1=s1b.ap()[:, 0:TT]:
               e.tensor_tensor(d, b.ap()[:, :], s1, ALU.mult),
               sm=s_v, waits=[(s_t.h, tcnt), (s_s.h, s_after_s1),
                              (s_s.h, s_after_s16f)])
            h1ev[(0, o)] = s_v.n
            BK.set(bi, s_v, s_v.n)

        jy = 0
        yslot_guard = {}
        for c in range(NCH):
            bks = [BK.take() for _ in range(NT)]
            bw = [w for _, ws in bks for w in ws]
            items = []
            for t in range(NT):
                items.append((banks[bks[t][0]].ap()[:, :],
                              wout16.ap()[:, 128 * c:128 * (c + 1)],
                              S16.ap()[:, TT * t:TT * (t + 1)], True, True))
            t_group(items, waits=bw + [(s_v.h, v_after_scan),
                                       (s_s.h, s_after_s16f)])
            tcnt = s_t.n
            for t in range(NT):
                bi = bks[t][0]
                dsl = y16all.ap()[:, L * c + TT * t:L * c + TT * (t + 1)]
                gsl = gate16.ap()[:, L * c + TT * t:L * c + TT * (t + 1)]
                if t % 2 == 0:
                    op('v', lambda e, b=banks[bi], d=dsl, g=gsl:
                       e.tensor_tensor(d, b.ap()[:, :], g, ALU.mult),
                       sm=s_v, waits=[(s_t.h, tcnt), (s_s.h, s_after_gate)])
                    BK.set(bi, s_v, s_v.n)
                else:
                    ysl = ybuf.ap()[:, TT * (jy % 4):TT * (jy % 4 + 1)]
                    sw = [(s_t.h, tcnt)]
                    if jy >= 4:
                        sw.append((s_g.h, yslot_guard[jy - 4]))
                    op('s', lambda e, b=banks[bi], d=ysl:
                       e.activation(d, b.ap()[:, :], AF.Copy),
                       sm=s_s, waits=sw)
                    BK.set(bi, s_s, s_s.n)
                    op('g', lambda e, d=dsl, y=ysl, g=gsl:
                       e.tensor_tensor(d, y, g, ALU.mult),
                       sm=s_g, waits=[(s_s.h, s_s.n), (s_s.h, s_after_gate)])
                    yslot_guard[jy] = s_g.n
                    jy += 1
        mem.free("gate16", "S16", "ybuf")
        g_after_y16 = s_g.n
        v_after_y16 = s_v.n
        dbg("y16all", y16all[:], [128, NCH * L], F16,
            [(s_v.h, v_after_y16), (s_g.h, g_after_y16)])

        alpha_b = mem.alloc("alpha_b", [128, L], F32)
        beta16 = mem.alloc("beta16_r", [1, L], F16)
        h2all = mem.alloc("h2all", [128, NCH * L], F16)

        BAR = barrier()
        op('v', lambda e: e.tensor_scalar(alpha_b[:], s1b[:], 1.0, None,
                                          ALU.add), sm=s_v, waits=BAR)
        op('v', lambda e: e.tensor_tensor(beta16[0:1, :], negmu.ap()[0:1, :],
                                          s1b.ap()[0:1, :], ALU.mult), sm=s_v)
        v_after_prep = s_v.n
        dbg("colsum1", colsum116[:], [1, D], F16, [(s_v.h, v_after_prep)])
        dbg("beta", beta16[:], [1, L], F16, [(s_v.h, v_after_prep)])

        h2grp = {}
        h1grp = dict(h1grp0)
        for t in range(NT):
            gx = (g_x0, g_x1)[t % 2]
            if t >= 2:
                dma('g', x16s_[t % 2].ap().rearrange("p (c w) -> p c w", w=TT),
                    xTd[:, TT * t:TT * (t + 1)]
                    .rearrange("(c p) w -> p c w", p=128),
                    gx, waits=[(s_t.h, h1grp[(t - 2, NCH - 1)]
                                if (t - 2, NCH - 1) in h1grp
                                else h2grp[(t - 2, NCH - 1)])])
            for o in (range(NCH) if t > 0 else ()):
                bi, bw = BK.take()
                items = [(banks[bi].ap()[:, :],
                          w1_16.ap()[:, D * k + 128 * o:D * k + 128 * (o + 1)],
                          x16s_[t % 2].ap()[:, TT * k:TT * (k + 1)],
                          k == 0, False) for k in range(NCH)]
                items.append((banks[bi].ap()[:, :],
                              colsum116.ap()[0:1, 128 * o:128 * (o + 1)],
                              negmu.ap()[0:1, TT * t:TT * (t + 1)],
                              False, True))
                t_group(items, waits=bw + [(gx.h, gx.n),
                                           (s_v.h, v_after_prep)])
                tcnt = s_t.n
                h1grp[(t, o)] = tcnt
                vw = [(s_t.h, tcnt), (s_t.h, h2grp[(t - 1, NCH - 1)])]
                op('v', lambda e, b=banks[bi], o=o, t=t,
                   d=h1buf.ap()[:, TT * o:TT * (o + 1)],
                   s1=s1b.ap()[:, TT * t:TT * (t + 1)]:
                   e.tensor_tensor(d, b.ap()[:, :], s1, ALU.mult),
                   sm=s_v, waits=vw)
                h1ev[(t, o)] = s_v.n
                BK.set(bi, s_v, s_v.n)
            for o in range(NCH):
                bi, bw = BK.take()
                items = [(banks[bi].ap()[:, :],
                          w2_16.ap()[:, D * k + 128 * o:D * k + 128 * (o + 1)],
                          h1buf.ap()[:, TT * k:TT * (k + 1)],
                          k == 0, k == NCH - 1) for k in range(NCH)]
                t_group(items, waits=bw + [(s_v.h, h1ev[(t, NCH - 1)])])
                h2grp[(t, o)] = s_t.n
                op('s', lambda e, b=banks[bi], o=o, t=t,
                   d=h2all.ap()[:, L * o + TT * t:L * o + TT * (t + 1)]:
                   e.activation(d, b.ap()[:, :], AF.Copy),
                   sm=s_s, waits=[(s_t.h, h2grp[(t, o)])])
                BK.set(bi, s_s, s_s.n)
        s_after_h2 = s_s.n
        dbg("h2all", h2all[:], [128, NCH * L], F16, [(s_s.h, s_after_h2)])

        mem.free("x16slot0", "x16slot1", "h1buf", "w1_16", "w2_16",
                 "negmu_r", "woutT2f_s")
        x32s_ = [mem.alloc(f"x32slot{i}", [128, L], F32) for i in range(2)]
        ftmp = mem.alloc("ftmp", [128, 4 * L], F32)
        vtmp = mem.alloc("vtmp", [128, 4 * TT], F32)
        outstage = mem.alloc("outstage", [128, 4 * TT], F32)

        BAR2 = barrier()
        st_q = (q_st0, q_st1, q_st2, q_st3)
        st_g = (None, g_st1, None, g_st3)
        slot_guard = {}
        j = 0
        fin_done = {}
        for o in range(NCH):
            qx = (q_x0, q_x1)[o % 2]
            if o < 2:
                w = list(BAR2)
            else:
                fs, fv = fin_done[(o - 2, NT - 1)]
                w = [(fs.h, fv)]
            dma('q', x32s_[o % 2][:], xTd[128 * o:128 * (o + 1), :], qx,
                waits=w)
            bks = [BK.take() for _ in range(NT)]
            bw = [w2 for _, ws in bks for w2 in ws]
            items = []
            for k in range(NCH):
                for t in range(NT):
                    items.append((banks[bks[t][0]].ap()[:, :],
                                  w3_16.ap()[:, D * k + 128 * o:D * k + 128 * (o + 1)],
                                  h2all.ap()[:, L * k + TT * t:L * k + TT * (t + 1)],
                                  k == 0, k == NCH - 1))
            t_group(items, waits=bw + W34LD + [(s_s.h, s_after_h2)])
            h3cnt = s_t.n
            fsl_o = ftmp.ap()[:, L * (o % 4):L * (o % 4 + 1)]
            sw = [(s_t.h, h3cnt)]
            if o >= 4:
                fs, fv = fin_done[(o - 4, NT - 1)]
                sw.append((fs.h, fv))
            for t in range(NT):
                bi = bks[t][0]
                op('s', lambda e, b=banks[bi], o=o,
                   d=fsl_o[:, TT * t:TT * (t + 1)]:
                   e.activation(d, b.ap()[:, :], AF.Tanh, bias=fb_col(o)),
                   sm=s_s, waits=sw)
                sw = []
                BK.set(bi, s_s, s_s.n)
            s_tanh_o = s_s.n
            pre_done = {}
            for t in range(NT):
                jj = o * NT + t
                vsl = vtmp.ap()[:, TT * (jj % 4):TT * (jj % 4 + 1)]
                peng, psem = ('v', s_v) if (t % 2 == 0 or o == NCH - 1) \
                    else ('g', s_g)
                pw_ = [(s_s.h, s_tanh_o), (qx.h, qx.n)]
                if jj >= 4:
                    fs2, fv2 = fin_done[divmod(jj - 4, NT)]
                    pw_.append((fs2.h, fv2))
                op(peng, lambda e, v2=vsl, o=o, t=t,
                   x=x32s_[o % 2].ap()[:, TT * t:TT * (t + 1)],
                   a=alpha_b.ap()[:, TT * t:TT * (t + 1)]:
                   e.tensor_tensor(v2, x, a, ALU.mult), sm=psem, waits=pw_)
                op(peng, lambda e, v2=vsl, f=fsl_o[:, TT * t:TT * (t + 1)]:
                   e.tensor_tensor(v2, v2, f, ALU.add), sm=psem,
                   waits=[(psem.h, psem.n)])
                pre_done[t] = (psem, psem.n)
            bks2 = [BK.take() for _ in range(NT)]
            bw2 = [w2 for _, ws in bks2 for w2 in ws]
            items = []
            for k in range(NCH):
                for t in range(NT):
                    items.append((banks[bks2[t][0]].ap()[:, :],
                                  outw16.ap()[:, D * k + 128 * o:D * k + 128 * (o + 1)],
                                  y16all.ap()[:, L * k + TT * t:L * k + TT * (t + 1)],
                                  k == 0, False))
            for t in range(NT):
                items.append((banks[bks2[t][0]].ap()[:, :], onesr16b[0:1, :],
                              beta16.ap()[0:1, TT * t:TT * (t + 1)],
                              False, True))
            t_group(items, waits=bw2 + [(s_v.h, v_after_y16),
                                        (s_g.h, g_after_y16)])
            opcnt = s_t.n
            for t in range(NT):
                bi2 = bks2[t][0]
                slot = j % 4
                osl = outstage.ap()[:, TT * slot:TT * (slot + 1)]
                ring = 'q' if (t % 2 == 0 or o == NCH - 1) else 'g'
                stsem = st_q[slot] if ring == 'q' else st_g[slot]
                sg = [slot_guard[slot]] if slot in slot_guard else []
                op('s', lambda e, b=banks[bi2], o=o, d=osl:
                   e.add(d, b.ap()[:, :], outb_col(o)),
                   sm=s_s, waits=[(s_t.h, opcnt)] + sg)
                BK.set(bi2, s_s, s_s.n)
                feng, fsem = ('g', s_g) \
                    if (t % 2 == 0 and o < NCH - 1) else ('v', s_v)
                fw = [(s_s.h, s_s.n), (pre_done[t][0].h, pre_done[t][1])]
                if feng == 'g':
                    fw.append((s_g.h, s_g.n))
                op(feng, lambda e, d=osl,
                   v2=vtmp.ap()[:, TT * (j % 4):TT * (j % 4 + 1)]:
                   e.tensor_tensor(d, d, v2, ALU.add), sm=fsem, waits=fw)
                fin_done[(o, t)] = (fsem, fsem.n)
                dma(ring, outd[t, o], osl, stsem,
                    waits=[(fin_done[(o, t)][0].h, fin_done[(o, t)][1])])
                slot_guard[slot] = (stsem.h, stsem.n)
                j += 1

        for eng in 'gtvs':
            emit(eng, lambda e, dh=done.h: e.nop().then_inc(dh, 1))

        dma_finals = [(s.h, s.n) for s in
                      [q_sm, q_sm2, q_x0, q_x1, q_st0, q_st1, q_st2, q_st3,
                       g_st1, g_st3,
                       g_wx, g_wx2, g_wz,
                       g_pw, g_w12, g_w34, g_sm, g_x0, g_x1, q_dbg] + q_xt
                      + [s_t, s_v, s_s, s_g]]

        def fin(e, a=q_st0.n, b=q_st1.n, fins=tuple(dma_finals), dh=done.h,
                bh=boot.h):
            e.wait_ge(q_st0.h, a)
            e.wait_ge(q_st1.h, b)
            e.wait_ge(dh, 4)
            for (sh, v) in fins:
                if v > 0:
                    e.wait_ge(sh, v)
            e.sem_clear(bh)
        emit('q', fin)

        blk = es.enter_context(nc.Block())

        @blk.sync
        def _(eng):
            for f in ops['q']:
                f(eng)

        @blk.gpsimd
        def _(eng):
            for f in ops['g']:
                f(eng)

        @blk.tensor
        def _(eng):
            for f in ops['t']:
                f(eng)

        @blk.vector
        def _(eng):
            for f in ops['v']:
                f(eng)

        @blk.scalar
        def _(eng):
            for f in ops['s']:
                f(eng)

    return nc


_CACHE = {}


def host_prep(inputs):
    inp = {k: np.asarray(v) for k, v in inputs.items()}
    f32 = np.float32
    f16 = np.float16

    T = np.ascontiguousarray(inp["L0_attn_T"].astype(f32, copy=False))
    in_wT = np.ascontiguousarray(inp["L0_attn_in_w"].astype(f32).T)
    pw_wT = np.ascontiguousarray(inp["L0_attn_pw_w"].astype(f32).T)
    out_wT = np.ascontiguousarray(inp["L0_attn_out_w"].astype(f32).T)
    w1 = np.ascontiguousarray(inp["L0_ffn_weight"].astype(f32))
    w2 = np.ascontiguousarray(inp["L0_ffn_aux_weight1"].astype(f32))
    w3 = np.ascontiguousarray(inp["L0_ffn_aux_weight2"].astype(f32))
    winT2 = np.ascontiguousarray(
        np.concatenate([inp["L0_attn_f_win"], inp["L0_attn_b_win"]], axis=0)
        .astype(f32).T)
    woutT2 = np.ascontiguousarray(
        np.concatenate([inp["L0_attn_f_wout"].astype(f32).T,
                        inp["L0_attn_b_wout"].astype(f32).T], axis=0))
    ssm = np.stack([
        np.concatenate([inp["L0_attn_f_logA"], inp["L0_attn_b_logA"]]),
        np.concatenate([inp["L0_attn_f_B"], inp["L0_attn_b_B"]]),
        np.concatenate([inp["L0_attn_f_C"], inp["L0_attn_b_C"]]),
        np.concatenate([np.broadcast_to(inp["L0_attn_f_logdt"], (64,)),
                        np.broadcast_to(inp["L0_attn_b_logdt"], (64,))]),
    ], axis=1).astype(f32)
    ssm = np.ascontiguousarray(ssm)

    def cols8(v):
        return np.ascontiguousarray(v.astype(f32).reshape(8, 128).T)

    vec = np.concatenate([
        cols8(inp["L0_attn_norm_scale"]),
        cols8(inp["L0_attn_in_b"][:1024]),
        cols8(inp["L0_attn_in_b"][1024:]),
        cols8(inp["L0_attn_dw_b"]),
        cols8(inp["L0_attn_pw_b"]),
        cols8(inp["L0_ffn_bias"]),
        cols8(inp["L0_attn_out_b"]),
    ], axis=1)
    dww = (inp["L0_attn_dw_w"].astype(f32).reshape(8, 128, 3)
           .transpose(1, 2, 0).reshape(128, 24))
    pack = np.ascontiguousarray(np.concatenate([ssm, vec, dww], axis=1))

    shared = dict(
        in_wT=in_wT, pw_wT=pw_wT, out_wT=out_wT, w1=w1, w2=w2, w3=w3,
        winT2=winT2, woutT2=woutT2, pack=pack,
        onesr32=np.ones((128, 128), f32))

    return shared, T


def kernel(**inputs):
    shared, T = host_prep(inputs)

    if "nc" not in _CACHE:
        _CACHE["nc"] = build_program()
    nc = _CACHE["nc"]

    in_maps = []
    for b in range(8):
        m = dict(shared)
        m["xT"] = np.ascontiguousarray(T[b].T)
        in_maps.append(m)

    res = run_bass_kernel_spmd(nc, in_maps, core_ids=list(range(8)))

    out = np.empty((8, L, D), np.float32)
    for b in range(8):
        a = res.results[b]["outT"]
        out[b] = a.transpose(0, 3, 1, 2).reshape(L, D)
    return out


if __name__ == "__main__":
    pass


# version 32
# speedup vs baseline: 1.1689x; 1.1689x over previous
import contextlib
import numpy as np
import concourse.bass as bass
import concourse.mybir as mybir
from concourse.bass_utils import run_bass_kernel_spmd

F32 = mybir.dt.float32
F32R = mybir.dt.float32r
F16 = mybir.dt.float16
U8 = mybir.dt.uint8
AF = mybir.ActivationFunctionType
ALU = mybir.AluOpType

L = 2048
D = 1024
NCH = 8
TT = 512
NT = 4
EPS_LN = 1e-5
EPS_RMS = 1e-6
EPS_SSM = 1e-6


class Sem:
    def __init__(self, h):
        self.h = h
        self.n = 0


class Mem:

    LO = 17408
    HI = 229344

    def __init__(self, nc):
        self.nc = nc
        self.free_list = [(self.LO, self.HI)]
        self.live = {}
        self.peak = 0

    def alloc(self, name, shape, dtype):
        nbytes = 1
        for s in shape[1:]:
            nbytes *= s
        nbytes *= {F32: 4, F32R: 4, F16: 2, U8: 1}[dtype]
        nbytes = (nbytes + 63) & ~63
        fits = [(b - a, idx) for idx, (a, b) in enumerate(self.free_list)
                if b - a >= nbytes]
        fits.sort()
        for _, idx in fits[:1]:
            a, b = self.free_list[idx]
            if True:
                self.free_list[idx] = (a + nbytes, b)
                if self.free_list[idx][0] == self.free_list[idx][1]:
                    self.free_list.pop(idx)
                h = self.nc.alloc_sbuf_tensor_at(name, list(shape), dtype,
                                                 offset=a)
                self.live[name] = (a, nbytes)
                used = self.HI - self.LO - sum(e - s for s, e in self.free_list)
                self.peak = max(self.peak, used)
                return h
        raise RuntimeError(f"manual SBUF OOM at {name} ({nbytes}B); "
                           f"free={self.free_list}")

    def free(self, *names):
        for n in names:
            a, nbytes = self.live.pop(n)
            self.free_list.append((a, a + nbytes))
        self.free_list.sort()
        out = []
        for s, e in self.free_list:
            if out and out[-1][1] >= s:
                out[-1] = (out[-1][0], max(out[-1][1], e))
            else:
                out.append((s, e))
        self.free_list = [tuple(x) for x in out]


def build_program(debug=False):
    nc = bass.Bass("TRN2", target_bir_lowering=False)

    xTd = nc.dram_tensor("xT", [D, L], F32, kind="ExternalInput").ap()
    inwTd = nc.dram_tensor("in_wT", [D, 2 * D], F32, kind="ExternalInput").ap()
    pwTd = nc.dram_tensor("pw_wT", [D, D], F32, kind="ExternalInput").ap()
    outwTd = nc.dram_tensor("out_wT", [D, D], F32, kind="ExternalInput").ap()
    w1d = nc.dram_tensor("w1", [D, D], F32, kind="ExternalInput").ap()
    w2d = nc.dram_tensor("w2", [D, D], F32, kind="ExternalInput").ap()
    w3d = nc.dram_tensor("w3", [D, D], F32, kind="ExternalInput").ap()
    winTd = nc.dram_tensor("winT2", [D, 128], F32, kind="ExternalInput").ap()
    woutd = nc.dram_tensor("woutT2", [128, D], F32, kind="ExternalInput").ap()
    packd = nc.dram_tensor("pack", [128, 84], F32, kind="ExternalInput").ap()
    onesr32d = nc.dram_tensor("onesr32", [128, 128], F32R,
                              kind="ExternalInput").ap()
    outd = nc.dram_tensor("outT", [NT, NCH, 128, TT], F32,
                          kind="ExternalOutput").ap()

    es = contextlib.ExitStack()
    with es:
        mem = Mem(nc)

        banks = [es.enter_context(nc.psum_tensor(f"pb{i}", [128, TT], F32))
                 for i in range(8)]

        def sem(name):
            return Sem(es.enter_context(nc.semaphore(name)))

        boot = sem("boot")
        done = sem("done")
        s_t = sem("s_t")
        s_v = sem("s_v")
        s_s = sem("s_s")
        s_g = sem("s_g")
        q_sm = sem("q_sm")
        q_sm2 = sem("q_sm2")
        q_x0 = sem("q_x0")
        q_x1 = sem("q_x1")
        q_st0 = sem("q_st0")
        q_st1 = sem("q_st1")
        q_st2 = sem("q_st2")
        q_st3 = sem("q_st3")
        g_st1 = sem("g_st1")
        g_st3 = sem("g_st3")
        q_xt = [sem(f"q_xt{i}") for i in range(4)]
        q_xtb = sem("q_xtb")
        g_wx = sem("g_wx")
        g_wx2 = sem("g_wx2")
        g_wz = sem("g_wz")
        g_pw = sem("g_pw")
        g_w12 = sem("g_w12")
        g_w34 = sem("g_w34")
        g_sm = sem("g_sm")
        g_x0 = sem("g_x0")
        g_x1 = sem("g_x1")
        q_dbg = sem("q_dbg")
        dbg_tensors = {}
        ALL_SEMS = [boot, done, s_t, s_v, s_s, s_g, q_sm, q_sm2, q_x0, q_x1, q_st0,
                    q_st1, q_st2, q_st3, g_st1, g_st3,
                    g_wx, g_wx2, g_wz, g_pw, g_w12, g_w34, g_sm,
                    g_x0, g_x1, q_dbg, q_xtb] + q_xt
        BOOT = 1 << 16

        ops = {k: [] for k in "qgtvs"}
        last_wait = {k: {} for k in "qgtvs"}

        def emit(eng, fn):
            ops[eng].append(fn)

        def _mkwaits(eng, waits):
            out = []
            for (h, v) in waits:
                if v <= 0:
                    continue
                if last_wait[eng].get(h, -1) >= v:
                    continue
                last_wait[eng][h] = v
                out.append((h, v))
            return tuple(out)

        def op(eng, build_fn, sm=None, inc=1, waits=(), drain=False):
            inline_w = None
            if eng == 'v' and s_v.n > 0:
                inline_w = (s_v.h, s_v.n)
            elif eng == 's' and s_s.n > 0:
                inline_w = (s_s.h, s_s.n)
            ws = _mkwaits(eng, waits)
            if inline_w is None and ws:
                inline_w, ws = ws[0], ws[1:]
            if sm is not None:
                sm.n += inc
            smh = sm.h if sm is not None else None

            def f(e, bf=build_fn, ws=ws, iw=inline_w, smh=smh, k=inc, dr=drain):
                for (h, v) in ws:
                    e.wait_ge(h, v)
                ins = bf(e)
                if iw is not None:
                    ins._wait_ge(iw[0], iw[1])
                if smh is not None:
                    ins.then_inc(smh, k)
                if dr:
                    e.drain()
            emit(eng, f)
            return sm.n if sm is not None else None

        def dma(eng, dst, src, sm, waits=()):
            return op(eng, lambda e, d=dst, s=src: e.dma_start(out=d, in_=s),
                      sm=sm, inc=16, waits=waits)

        def barrier():
            w = [(s_t.h, s_t.n), (s_v.h, s_v.n), (s_s.h, s_s.n)]
            if s_g.n > 0:
                w.append((s_g.h, s_g.n))
            return w

        def dbg(name, src_ap, shape, dtype, waits):
            if not debug:
                return
            d = nc.dram_tensor("dbg_" + name, list(shape), dtype,
                               kind="ExternalOutput").ap()
            dbg_tensors[name] = d
            dma('q', d, src_ap, q_dbg, waits=waits)

        class Banks:
            def __init__(self):
                self.guard = [None] * 8
                self.i = 0

            def take(self):
                i = self.i % 8
                self.i += 1
                g = self.guard[i]
                self.guard[i] = None
                return i, ([g] if g else [])

            def set(self, i, semobj, val):
                self.guard[i] = (semobj.h, val)

        BK = Banks()

        def t_group(items, waits=()):
            norm = []
            for it in items:
                if len(it) == 5:
                    norm.append(((),) + tuple(it))
                else:
                    w = _mkwaits('t', it[0])
                    norm.append((w,) + tuple(it[1:]))
            ws = _mkwaits('t', waits)
            s_t.n += 1

            def f(e, it=tuple(norm), ws=ws, smh=s_t.h):
                used_w0 = not ws or (it and not it[0][0])
                for (h, v) in (ws[1:] if used_w0 else ws):
                    e.wait_ge(h, v)
                first = True
                ins = None
                for (iw, o, lh, r, st, sp) in it:
                    for (h, v) in iw[1:]:
                        e.wait_ge(h, v)
                    ins = e.matmul(o, lh, r, start=st, stop=sp)
                    if iw:
                        ins._wait_ge(iw[0][0], iw[0][1])
                    elif first and ws:
                        ins._wait_ge(ws[0][0], ws[0][1])
                    first = False
                ins.then_inc(smh, 1)
            emit('t', f)
            return s_t.n

        FIRST_SEMS = (q_xt[0], q_sm)
        def _boot_q1(e, sems=tuple(s.h for s in FIRST_SEMS)):
            for sh in sems:
                e.sem_clear(sh)
        emit('q', _boot_q1)

        def _boot_q2(e, sems=tuple(s.h for s in ALL_SEMS
                                   if s not in FIRST_SEMS),
                     bh=boot.h, big=BOOT):
            for sh in sems:
                e.sem_clear(sh)
            e.sem_inc(bh, big)

        def emit_boot2():
            emit('q', _boot_q2)
        for eng in 'gtvs':
            emit(eng, lambda e, bh=boot.h, big=BOOT: e.wait_ge(bh, big))

        class APView:
            def __init__(self, base_ap):
                self._ap = base_ap

            def ap(self):
                return self._ap

            def __getitem__(self, idx):
                return self._ap[idx]

        ones16 = mem.alloc("ones16_s", [128, 128], F16)
        onesr32 = mem.alloc("onesr32_s", [128, 128], F32R)
        onesr16 = mem.alloc("onesr16_s", [1, TT], F16)
        onesr16b = mem.alloc("onesr16b_s", [1, 128], F16)
        consts = mem.alloc("consts_s", [128, 2], F32)
        pack = mem.alloc("pack_s", [128, 84], F32)
        ssm = APView(pack.ap()[:, 0:4])
        vec = APView(pack.ap()[:, 4:60])
        dww = APView(pack.ap()[:, 60:84])
        prm = mem.alloc("prm_s", [128, 12], F32)
        cvb = mem.alloc("cvb_s", [128, 8], F32)
        e0n = mem.alloc("e0n_s", [128, 8], F32)
        e1n = mem.alloc("e1n_s", [128, 8], F32)
        winT16 = mem.alloc("winT16_s", [128, D], F16)
        wout16 = mem.alloc("wout16_s", [128, D], F16)
        woutT2f = mem.alloc("woutT2f_s", [128, D], F32)

        negmu = mem.alloc("negmu_r", [1, L], F16)
        nm2row = mem.alloc("nm2_r", [1, L], F16)
        colsum = mem.alloc("colsum_r", [1, 2 * D], F16)

        s1b = mem.alloc("s1b", [128, L], F32)
        s2b = mem.alloc("s2b", [128, L], F32)

        xTs = mem.alloc("xTs", [128, NCH * L], F16)
        x2buf = mem.alloc("x2buf", [128, 2 * L], F16)
        SxB = mem.alloc("SxB", [128, L], F32)
        Sx2B = mem.alloc("Sx2B", [128, L], F32)
        wxs = mem.alloc("wxs", [128, NCH * D], F16)
        wzs = mem.alloc("wzs", [128, NCH * D], F16)
        xt32 = mem.alloc("xt32", [128, NCH * L], F32R)

        def xT(c):
            return xTs.ap()[:, L * c:L * (c + 1)]

        def x32c(c):
            return xt32.ap()[:, L * c:L * (c + 1)]

        def wx(k):
            return wxs.ap()[:, D * k:D * (k + 1)]

        def wz(k):
            return wzs.ap()[:, D * k:D * (k + 1)]

        dma('q', xt32.ap()[:, 0:L], xTd.bitcast(F32R)[0:128, :], q_xt[0])
        dma('q', onesr32[:], onesr32d, q_sm)
        dma('q', pack[:], packd, q_sm)
        emit_boot2()
        dma('q', xt32.ap()[:, L:2 * L], xTd.bitcast(F32R)[128:256, :], q_xtb)
        op('v', lambda e: e.memset(ones16[:], 1.0), sm=s_v)
        op('v', lambda e: e.memset(onesr16[:], 1.0), sm=s_v, drain=True)
        op('v', lambda e: e.memset(onesr16b[:], 1.0), sm=s_v, drain=True)
        op('v', lambda e: e.memset(consts.ap()[:, 0:1], EPS_LN), sm=s_v,
           drain=True)
        op('v', lambda e: e.memset(consts.ap()[:, 1:2], EPS_LN * EPS_RMS),
           sm=s_v, drain=True)
        v_after_consts = s_v.n
        for i in range(1, 4):
            dma('q', xt32.ap()[:, 2 * L * i:2 * L * (i + 1)]
                .rearrange("p (c w) -> p c w", w=L),
                xTd.bitcast(F32R)[256 * i:256 * (i + 1), :]
                .rearrange("(c p) w -> p c w", p=128), q_xt[i])
        dma('q', woutT2f[:], woutd, q_sm2)
        dma('g', wxs.ap()[:, 0:4 * D].rearrange("p (k o) -> p k o", o=D),
            inwTd[0:512, 0:D].rearrange("(k p) o -> p k o", p=128), g_wx)
        dma('g', wxs.ap()[:, 4 * D:].rearrange("p (k o) -> p k o", o=D),
            inwTd[512:1024, 0:D].rearrange("(k p) o -> p k o", p=128), g_wx2)
        dma('g', wzs.ap().rearrange("p (k o) -> p k o", o=D),
            inwTd[:, D:2 * D].rearrange("(k p) o -> p k o", p=128), g_wz)
        dma('g', winT16.ap().rearrange("p (k o) -> p k o", o=128),
            winTd.rearrange("(k p) o -> p k o", p=128), g_sm)

        ns_col = lambda k: vec.ap()[:, k:k + 1]
        bz_col = lambda o: vec.ap()[:, 16 + o:17 + o]
        pwb_col = lambda e: vec.ap()[:, 32 + e:33 + e]
        fb_col = lambda o: vec.ap()[:, 40 + o:41 + o]
        outb_col = lambda o: vec.ap()[:, 48 + o:49 + o]
        c_ln = consts.ap()[:, 0:1]
        c_ee = consts.ap()[:, 1:2]

        SMALL_LD = [(q_sm.h, q_sm.n)]
        XTLD = [[(q_xt[c // 2].h, q_xt[c // 2].n)] for c in range(NCH)]
        XTLD[1] = [(q_xtb.h, 16)]

        ns16 = mem.alloc("ns16_s", [128, 8], F16)

        P_dt = prm.ap()[:, 0:1]
        P_A = prm.ap()[:, 1:2]
        P_Abar = prm.ap()[:, 2:3]
        P_Bbar = prm.ap()[:, 3:4]
        P_eA = prm.ap()[:, 4:5]
        P_t0 = prm.ap()[:, 5:6]
        P_t1 = prm.ap()[:, 6:7]
        P_t2 = prm.ap()[:, 7:8]
        P_m = prm.ap()[:, 8:9]
        w0s = dww.ap()[:, 0:8]
        w1s = dww.ap()[:, 8:16]
        w2s = dww.ap()[:, 16:24]
        op('v', lambda e: e.tensor_tensor(cvb[:], w0s, w1s, ALU.add), sm=s_v,
           waits=SMALL_LD, drain=True)
        op('v', lambda e: e.tensor_tensor(cvb[:], cvb[:], w2s, ALU.add),
           sm=s_v, drain=True)
        op('v', lambda e: e.tensor_tensor(cvb[:], cvb[:], vec.ap()[:, 8:16],
                                          ALU.mult), sm=s_v, drain=True)
        op('v', lambda e: e.tensor_tensor(cvb[:], cvb[:], vec.ap()[:, 24:32],
                                          ALU.add), sm=s_v, drain=True)
        op('v', lambda e: e.tensor_tensor(e0n[:], vec.ap()[:, 8:16], w0s,
                                          ALU.mult), sm=s_v, drain=True)
        op('v', lambda e: e.tensor_scalar(e0n[:], e0n[:], -1.0, None, ALU.mult),
           sm=s_v, drain=True)
        op('v', lambda e: e.tensor_tensor(e1n[:], vec.ap()[:, 8:16], w2s,
                                          ALU.mult), sm=s_v, drain=True)
        op('v', lambda e: e.tensor_scalar(e1n[:], e1n[:], -1.0, None, ALU.mult),
           sm=s_v, drain=True)
        op('v', lambda e: e.tensor_copy(ns16[:], vec.ap()[:, 0:8]), sm=s_v,
           drain=True)
        v_after_ns16 = s_v.n

        sq_done = []
        statgrp_after = []
        for c in range(NCH):
            w = list(XTLD[c])
            if c >= 2:
                w.append((s_t.h, statgrp_after[c - 2]))
            x2slot = x2buf.ap()[:, L * (c % 2):L * (c % 2 + 1)]
            if c % 2 == 0:
                op('s', lambda e, xc=x32c(c), d=x2slot:
                   e.activation(d, xc, AF.Square), sm=s_s, waits=w)
                sq_done.append((s_s, s_s.n))
            else:
                op('v', lambda e, xc=x32c(c), d=x2slot:
                   e.tensor_tensor(d, xc, xc, ALU.mult), sm=s_v, waits=w)
                sq_done.append((s_v, s_v.n))
            items = []
            for tt in range(NT):
                items.append((banks[tt].ap()[:, :], onesr32[:],
                              x32c(c)[:, TT * tt:TT * (tt + 1)],
                              c == 0, c == NCH - 1))
            for tt in range(NT):
                items.append((banks[4 + tt].ap()[:, :], ones16[:],
                              x2slot[:, TT * tt:TT * (tt + 1)],
                              c == 0, c == NCH - 1))
            t_group(items, waits=XTLD[c] + [(sq_done[c][0].h, sq_done[c][1]),
                                            (s_v.h, v_after_consts)]
                    + (SMALL_LD if c == 0 else []))
            statgrp_after.append(s_t.n)
        t_after_stats = s_t.n

        op('s', lambda e: e.activation(P_dt, ssm.ap()[:, 3:4], AF.Exp),
           sm=s_s, waits=SMALL_LD)
        op('s', lambda e: e.activation(P_eA, ssm.ap()[:, 0:1], AF.Exp), sm=s_s)
        s_after_pexp = s_s.n
        op('v', lambda e: e.tensor_scalar(P_A, P_eA, -1.0, None, ALU.mult),
           sm=s_v, waits=[(s_s.h, s_after_pexp)], drain=True)
        op('v', lambda e: e.tensor_tensor(P_t0, P_A, P_dt, ALU.mult), sm=s_v,
           drain=True)
        v_after_pt0 = s_v.n
        op('v', lambda e: e.tensor_single_scalar(P_m, P_eA, EPS_SSM,
                                                 ALU.is_lt), sm=s_v, drain=True)
        op('v', lambda e: e.tensor_scalar(P_t1, P_A, -1.0, 1.0, ALU.mult,
                                          ALU.add), sm=s_v, drain=True)
        op('v', lambda e: e.tensor_tensor(P_t1, P_m, P_t1, ALU.mult), sm=s_v,
           drain=True)
        op('v', lambda e: e.tensor_tensor(P_t1, P_A, P_t1, ALU.add), sm=s_v,
           drain=True)
        op('v', lambda e: e.reciprocal(P_t2, P_t1), sm=s_v, drain=True)

        for tt in range(NT):
            op('s', lambda e, b=banks[tt], d=SxB.ap()[:, TT * tt:TT * (tt + 1)]:
               e.activation(d, b.ap()[:, :], AF.Copy),
               sm=s_s, waits=[(s_t.h, t_after_stats)])
        for tt in range(NT):
            op('s', lambda e, b=banks[4 + tt],
               d=Sx2B.ap()[:, TT * tt:TT * (tt + 1)]:
               e.activation(d, b.ap()[:, :], AF.Copy), sm=s_s)
        s_after_statevac = s_s.n
        for i in range(8):
            BK.set(i, s_s, s_after_statevac)
        BK.i = 8

        SEV = [(s_s.h, s_after_statevac)]
        op('v', lambda e: e.tensor_scalar(negmu[0:1, :], SxB.ap()[0:1, :],
                                          -1.0 / D, None, ALU.mult),
           sm=s_v, waits=SEV)
        op('s', lambda e: e.activation(SxB[:], SxB[:], AF.Square, scale=1.0 / D),
           sm=s_s, waits=[(s_v.h, s_v.n)])
        op('v', lambda e: e.tensor_scalar(Sx2B[:], Sx2B[:], 1.0 / D, None,
                                          ALU.mult), sm=s_v)
        op('v', lambda e: e.tensor_tensor(Sx2B[:], Sx2B[:], SxB[:], ALU.subtract),
           sm=s_v, waits=[(s_s.h, s_s.n)])
        v_after_var = s_v.n
        op('s', lambda e: e.activation(SxB[:], Sx2B[:], AF.Ln, bias=c_ee,
                                       scale=1.0 + EPS_RMS),
           sm=s_s, waits=[(s_v.h, v_after_var)])
        op('s', lambda e: e.activation(s2b[:], SxB[:], AF.Exp, scale=-0.5),
           sm=s_s)
        s_after_s2 = s_s.n
        ts16_done = []
        for c in range(NCH):
            eng, sm_ = ('v', s_v)
            op(eng, lambda e, c=c: e.scalar_tensor_tensor(
                xT(c), x32c(c), pack.ap()[:, 4 + c:5 + c], s2b[:],
                ALU.mult, ALU.mult),
               sm=sm_, waits=[(s_s.h, s_after_s2), (s_t.h, t_after_stats)])
            ts16_done.append((sm_, sm_.n))
        op('v', lambda e: e.tensor_tensor(nm2row[0:1, :], negmu.ap()[0:1, :],
                                          s2b.ap()[0:1, :], ALU.mult), sm=s_v)
        v_after_nm2 = s_v.n
        op('s', lambda e: e.activation(P_Abar, P_t0, AF.Exp), sm=s_s,
           waits=[(s_v.h, v_after_pt0)])
        s_after_abar = s_s.n
        op('v', lambda e: e.tensor_scalar(P_t0, P_Abar, -1.0, None, ALU.add),
           sm=s_v, waits=[(s_s.h, s_after_abar)], drain=True)
        op('v', lambda e: e.tensor_tensor(P_t0, P_t0, P_t2, ALU.mult), sm=s_v,
           drain=True)
        op('v', lambda e: e.tensor_tensor(P_t1, P_dt, P_t0, ALU.subtract),
           sm=s_v, drain=True)
        op('v', lambda e: e.tensor_tensor(P_t1, P_m, P_t1, ALU.mult), sm=s_v,
           drain=True)
        op('v', lambda e: e.tensor_tensor(P_t0, P_t0, P_t1, ALU.add), sm=s_v,
           drain=True)
        op('v', lambda e: e.tensor_tensor(P_Bbar, P_t0, ssm.ap()[:, 1:2],
                                          ALU.mult), sm=s_v, drain=True)
        op('v', lambda e: e.tensor_scalar(wout16[:], woutT2f[:],
                                          ssm.ap()[:, 2:3], None, ALU.mult),
           sm=s_v, waits=[(q_sm2.h, 16)])
        op('s', lambda e: e.activation(s1b[:], Sx2B[:], AF.Ln, bias=c_ln),
           sm=s_s)
        op('s', lambda e: e.activation(s1b[:], s1b[:], AF.Exp, scale=-0.5),
           sm=s_s)
        s_after_s1 = s_s.n
        v_after_statmath = s_v.n
        dbg("s1b", s1b[:], [128, L], F32, [(s_s.h, s_s.n)])
        dbg("s2b", s2b[:], [128, L], F32, [(s_s.h, s_s.n)])
        dbg("negmu", negmu[:], [1, L], F16, [(s_v.h, s_v.n)])

        mem.free("x2buf", "SxB", "Sx2B", "xt32")
        x1buf = mem.alloc("x1buf", [128, 2 * L], F32)
        convbuf = mem.alloc("convbuf", [128, L], F32)
        xc1 = mem.alloc("xc1", [128, NCH * L], F16)
        gate16 = mem.alloc("gate16", [128, NCH * L], F16)
        pw16 = mem.alloc("pw16", [128, NCH * D], F16)
        dma('g', pw16.ap().rearrange("p (k o) -> p k o", o=D),
            pwTd.rearrange("(k p) o -> p k o", p=128), g_pw, waits=barrier())

        cs = []
        for half, wfun, gsem in ((0, wx, g_wx2), (1, wz, g_wz)):
            for j2 in range(2):
                bi, bwc = BK.take()
                items = [(banks[bi].ap()[0:1, :], ns16.ap()[:, k:k + 1],
                          wfun(k)[:, TT * j2:TT * (j2 + 1)],
                          k == 0, k == NCH - 1) for k in range(NCH)]
                t_group(items, waits=bwc + [(s_v.h, v_after_ns16),
                                            (g_wx.h, g_wx.n),
                                            (gsem.h, gsem.n)])
                cs.append((bi, half, j2, s_t.n))
        for (bi, half, j2, tcnt) in cs:
            dst = colsum.ap()[0:1, D * half + TT * j2:D * half + TT * (j2 + 1)]
            op('s', lambda e, b=banks[bi], d=dst:
               e.activation(d, b.ap()[0:1, :], AF.Copy), sm=s_s,
               waits=[(s_t.h, tcnt)])
            BK.set(bi, s_s, s_s.n)
        s_after_colsum = s_s.n

        conv_done = []
        half_pend = {}
        for o in (0, 1):
            bks = [BK.take() for _ in range(NT)]
            bw = [w for _, ws in bks for w in ws]
            items = []
            for k in range(4):
                for t in range(NT):
                    items.append((((ts16_done[k][0].h, ts16_done[k][1]),),
                                  banks[bks[t][0]].ap()[:, :],
                                  wx(k)[:, 128 * o:128 * (o + 1)],
                                  xT(k)[:, TT * t:TT * (t + 1)],
                                  k == 0, False))
            t_group(items, waits=bw + [(g_wx.h, g_wx.n)])
            half_pend[o] = bks
        for o in range(NCH):
            if o in half_pend:
                bks = half_pend[o]
                bw = []
                items = []
                for k in range(4, NCH):
                    for t in range(NT):
                        items.append((((ts16_done[k][0].h, ts16_done[k][1]),),
                                      banks[bks[t][0]].ap()[:, :],
                                      wx(k)[:, 128 * o:128 * (o + 1)],
                                      xT(k)[:, TT * t:TT * (t + 1)],
                                      False, False))
            else:
                bks = [BK.take() for _ in range(NT)]
                bw = [w for _, ws in bks for w in ws]
                items = []
                for k in range(NCH):
                    for t in range(NT):
                        items.append((((ts16_done[k][0].h, ts16_done[k][1]),),
                                      banks[bks[t][0]].ap()[:, :],
                                      wx(k)[:, 128 * o:128 * (o + 1)],
                                      xT(k)[:, TT * t:TT * (t + 1)],
                                      k == 0, False))
            for t in range(NT):
                items.append((((s_s.h, s_after_colsum),
                               (s_v.h, v_after_nm2)),
                              banks[bks[t][0]].ap()[:, :],
                              colsum.ap()[0:1, 128 * o:128 * (o + 1)],
                              nm2row.ap()[0:1, TT * t:TT * (t + 1)],
                              False, True))
            t_group(items, waits=bw + [(g_wx2.h, g_wx2.n)])
            tcnt = s_t.n
            sw2 = [(s_t.h, tcnt)]
            if len(conv_done) >= 2:
                sw2.append((s_v.h, conv_done[-2]))
            x1sl = x1buf.ap()[:, L * (o % 2):L * (o % 2 + 1)]
            for t in range(NT):
                bi = bks[t][0]
                op('s', lambda e, b=banks[bi],
                   d=x1sl[:, TT * t:TT * (t + 1)]:
                   e.copy(d, b.ap()[:, :]),
                   sm=s_s, waits=sw2)
                BK.set(bi, s_s, s_s.n)
            xcc = xc1.ap()[:, L * o:L * (o + 1)]
            x1sl = x1buf.ap()[:, L * (o % 2):L * (o % 2 + 1)]
            op('v', lambda e, o=o, x1=x1sl: e.tensor_scalar(
                convbuf[:], x1, dww.ap()[:, 8 + o:9 + o],
                cvb.ap()[:, o:o + 1], ALU.mult, ALU.add), sm=s_v,
               waits=[(s_s.h, s_s.n)])
            op('v', lambda e, o=o, x1=x1sl: e.scalar_tensor_tensor(
                convbuf.ap()[:, 1:L], x1[:, 0:L - 1],
                dww.ap()[:, o:o + 1], convbuf.ap()[:, 1:L],
                ALU.mult, ALU.add), sm=s_v)
            op('v', lambda e, o=o, x1=x1sl: e.scalar_tensor_tensor(
                convbuf.ap()[:, 0:L - 1], x1[:, 1:L],
                dww.ap()[:, 16 + o:17 + o], convbuf.ap()[:, 0:L - 1],
                ALU.mult, ALU.add), sm=s_v)
            op('v', lambda e, o=o: e.tensor_scalar(
                convbuf.ap()[:, 0:1], convbuf.ap()[:, 0:1],
                e0n.ap()[:, o:o + 1], None, ALU.add), sm=s_v, drain=True)
            op('v', lambda e, o=o: e.tensor_scalar(
                convbuf.ap()[:, L - 1:L], convbuf.ap()[:, L - 1:L],
                e1n.ap()[:, o:o + 1], None, ALU.add), sm=s_v, drain=True)
            op('v', lambda e, d=xcc: e.tensor_copy(d, convbuf[:]), sm=s_v)
            conv_done.append(s_v.n)
        t_after_inx = s_t.n
        for o in range(NCH):
            bks = [BK.take() for _ in range(NT)]
            bw = [w for _, ws in bks for w in ws]
            items = []
            for k in range(NCH):
                for t in range(NT):
                    items.append((banks[bks[t][0]].ap()[:, :],
                                  wz(k)[:, 128 * o:128 * (o + 1)],
                                  xT(k)[:, TT * t:TT * (t + 1)],
                                  k == 0, False))
            for t in range(NT):
                items.append((((s_s.h, s_after_colsum),),
                              banks[bks[t][0]].ap()[:, :],
                              colsum.ap()[0:1, D + 128 * o:D + 128 * (o + 1)],
                              nm2row.ap()[0:1, TT * t:TT * (t + 1)],
                              False, True))
            t_group(items, waits=bw)
            tcnt = s_t.n
            for t in range(NT):
                bi = bks[t][0]
                op('s', lambda e, b=banks[bi], o=o,
                   d=gate16.ap()[:, L * o + TT * t:L * o + TT * (t + 1)]:
                   e.activation(d, b.ap()[:, :], AF.Sigmoid, bias=bz_col(o)),
                   sm=s_s, waits=[(s_t.h, tcnt)])
                BK.set(bi, s_s, s_s.n)
        t_after_z = s_t.n
        s_after_gate = s_s.n
        mem.free("xTs", "wxs", "wzs", "x1buf", "convbuf", "s2b")
        xconv16 = mem.alloc("xconv16", [128, NCH * L], F16)
        w1_16 = mem.alloc("w1_16", [128, NCH * D], F16)
        w2_16 = mem.alloc("w2_16", [128, NCH * D], F16)
        dma('g', w1_16.ap().rearrange("p (k o) -> p k o", o=D),
            w1d.rearrange("(k p) o -> p k o", p=128), g_w12, waits=barrier())
        dma('g', w2_16.ap().rearrange("p (k o) -> p k o", o=D),
            w2d.rearrange("(k p) o -> p k o", p=128), g_w12)
        x16s_ = [mem.alloc(f"x16slot{i}", [128, NCH * TT], F16)
                 for i in range(2)]
        for t in (0, 1):
            dma('g', x16s_[t].ap().rearrange("p (c w) -> p c w", w=TT),
                xTd[:, TT * t:TT * (t + 1)].rearrange("(c p) w -> p c w", p=128),
                (g_x0, g_x1)[t], waits=barrier())

        PWLD = [(g_pw.h, g_pw.n)]
        for e_ in range(NCH):
            bks = [BK.take() for _ in range(NT)]
            bw = [w for _, ws in bks for w in ws]
            items = []
            for k in range(NCH):
                for t in range(NT):
                    items.append((banks[bks[t][0]].ap()[:, :],
                                  pw16.ap()[:, D * k + 128 * e_:D * k + 128 * (e_ + 1)],
                                  xc1.ap()[:, L * k + TT * t:L * k + TT * (t + 1)],
                                  k == 0, k == NCH - 1))
            t_group(items, waits=bw + PWLD + [(s_v.h, conv_done[NCH - 1])])
            tcnt = s_t.n
            for t in range(NT):
                bi = bks[t][0]
                op('s', lambda e, b=banks[bi], e_=e_,
                   d=xconv16.ap()[:, L * e_ + TT * t:L * e_ + TT * (t + 1)]:
                   e.add(d, b.ap()[:, :], pwb_col(e_)),
                   sm=s_s, waits=[(s_t.h, tcnt)])
                BK.set(bi, s_s, s_s.n)
        s_after_pw = s_s.n
        t_after_pw = s_t.n

        mem.free("pw16", "xc1")
        u_scan = mem.alloc("u_scan", [128, L], F32)
        abart = mem.alloc("abart", [128, L], F32)
        S_scan = mem.alloc("S_scan", [128, L], F32)
        S16 = mem.alloc("S16", [128, L], F16)
        w3_16 = mem.alloc("w3_16", [128, NCH * D], F16)
        outw16 = mem.alloc("outw16", [128, NCH * D], F16)
        dma('g', w3_16.ap().rearrange("p (k o) -> p k o", o=D),
            w3d.rearrange("(k p) o -> p k o", p=128), g_w34, waits=barrier())
        dma('g', outw16.ap().rearrange("p (k o) -> p k o", o=D),
            outwTd.rearrange("(k p) o -> p k o", p=128), g_w34)

        colsum116 = mem.alloc("colsum116_r", [1, D], F16)
        W12LD = [(g_w12.h, g_w12.n)]
        c1banks = []
        for j in range(2):
            bi, bw = BK.take()
            items = [(banks[bi].ap()[0:1, :], ones16.ap()[:, 0:1],
                      w1_16.ap()[:, D * k + TT * j:D * k + TT * (j + 1)],
                      k == 0, k == NCH - 1) for k in range(NCH)]
            t_group(items, waits=bw + W12LD)
            c1banks.append((bi, j, s_t.n))
        for (bi, j, tcnt) in c1banks:
            op('s', lambda e, b=banks[bi],
               d=colsum116.ap()[0:1, TT * j:TT * (j + 1)]:
               e.activation(d, b.ap()[0:1, :], AF.Copy), sm=s_s,
               waits=[(s_t.h, tcnt)])
            BK.set(bi, s_s, s_s.n)
        s_after_c1 = s_s.n
        u_rev = u_scan.ap()[64:128, ::-1]
        bks = [BK.take() for _ in range(NT)]
        bw = [w for _, ws in bks for w in ws]
        items = []
        for k in range(NCH):
            for t in range(NT):
                items.append((banks[bks[t][0]].ap()[:, :],
                              winT16.ap()[:, 128 * k:128 * (k + 1)],
                              xconv16.ap()[:, L * k + TT * t:L * k + TT * (t + 1)],
                              k == 0, k == NCH - 1))
        t_group(items, waits=bw + [(g_sm.h, g_sm.n), (s_s.h, s_after_pw)])
        tcnt = s_t.n
        for t in range(NT):
            bi = bks[t][0]
            op('v', lambda e, b=banks[bi],
               d=u_scan.ap()[0:64, TT * t:TT * (t + 1)]:
               e.tensor_scalar(d, b.ap()[0:64, :], prm.ap()[0:64, 3:4], None,
                               ALU.mult), sm=s_v, waits=[(s_t.h, tcnt)])
            op('v', lambda e, b=banks[bi],
               d=u_rev[:, TT * t:TT * (t + 1)]:
               e.tensor_scalar(d, b.ap()[64:128, :], prm.ap()[64:128, 3:4],
                               None, ALU.mult), sm=s_v)
            BK.set(bi, s_v, s_v.n)
        op('v', lambda e: e.tensor_scalar(abart[:], u_scan[:], 0.0,
                                          P_Abar, ALU.mult, ALU.add), sm=s_v)
        op('v', lambda e: e.tensor_tensor_scan(S_scan[:], abart[:], u_scan[:],
                                               0.0, ALU.mult, ALU.add), sm=s_v)
        op('s', lambda e: e.activation(S16.ap()[0:64, :], S_scan.ap()[0:64, :],
                                       AF.Copy), sm=s_s,
           waits=[(s_v.h, s_v.n)])
        s_after_s16f = s_s.n
        op('v', lambda e: e.tensor_copy(S16.ap()[64:128, :],
                                        S_scan.ap()[64:128, ::-1]), sm=s_v)
        v_after_scan = s_v.n
        dbg("uscan", u_scan[:], [128, L], F32, [(s_v.h, s_v.n)])
        dbg("sscan", S_scan[:], [128, L], F32, [(s_v.h, s_v.n)])
        dbg("prm", prm[:], [128, 12], F32, [(s_v.h, s_v.n)])

        mem.free("xconv16", "S_scan", "u_scan", "abart")
        h1buf = mem.alloc("h1buf", [128, NCH * TT], F16)
        y16all = mem.alloc("y16all", [128, NCH * L], F16)
        ybuf = mem.alloc("ybuf", [128, 4 * TT], F16)
        W34LD = [(g_w34.h, g_w34.n)]
        h1ev = {}
        h1grp0 = {}
        for o in range(NCH):
            bi, bw = BK.take()
            items = [(banks[bi].ap()[:, :],
                      w1_16.ap()[:, D * k + 128 * o:D * k + 128 * (o + 1)],
                      x16s_[0].ap()[:, TT * k:TT * (k + 1)],
                      k == 0, False) for k in range(NCH)]
            items.append((banks[bi].ap()[:, :],
                          colsum116.ap()[0:1, 128 * o:128 * (o + 1)],
                          negmu.ap()[0:1, 0:TT], False, True))
            t_group(items, waits=bw + [(g_x0.h, g_x0.n),
                                       (s_s.h, s_after_c1)])
            tcnt = s_t.n
            h1grp0[(0, o)] = tcnt
            op('v', lambda e, b=banks[bi], o=o,
               d=h1buf.ap()[:, TT * o:TT * (o + 1)],
               s1=s1b.ap()[:, 0:TT]:
               e.tensor_tensor(d, b.ap()[:, :], s1, ALU.mult),
               sm=s_v, waits=[(s_t.h, tcnt), (s_s.h, s_after_s1),
                              (s_s.h, s_after_s16f)])
            h1ev[(0, o)] = s_v.n
            BK.set(bi, s_v, s_v.n)

        jy = 0
        yslot_guard = {}
        for c in range(NCH):
            bks = [BK.take() for _ in range(NT)]
            bw = [w for _, ws in bks for w in ws]
            items = []
            for t in range(NT):
                items.append((banks[bks[t][0]].ap()[:, :],
                              wout16.ap()[:, 128 * c:128 * (c + 1)],
                              S16.ap()[:, TT * t:TT * (t + 1)], True, True))
            t_group(items, waits=bw + [(s_v.h, v_after_scan),
                                       (s_s.h, s_after_s16f)])
            tcnt = s_t.n
            for t in range(NT):
                bi = bks[t][0]
                dsl = y16all.ap()[:, L * c + TT * t:L * c + TT * (t + 1)]
                gsl = gate16.ap()[:, L * c + TT * t:L * c + TT * (t + 1)]
                if t % 2 == 0:
                    op('v', lambda e, b=banks[bi], d=dsl, g=gsl:
                       e.tensor_tensor(d, b.ap()[:, :], g, ALU.mult),
                       sm=s_v, waits=[(s_t.h, tcnt), (s_s.h, s_after_gate)])
                    BK.set(bi, s_v, s_v.n)
                else:
                    ysl = ybuf.ap()[:, TT * (jy % 4):TT * (jy % 4 + 1)]
                    sw = [(s_t.h, tcnt)]
                    if jy >= 4:
                        sw.append((s_g.h, yslot_guard[jy - 4]))
                    op('s', lambda e, b=banks[bi], d=ysl:
                       e.activation(d, b.ap()[:, :], AF.Copy),
                       sm=s_s, waits=sw)
                    BK.set(bi, s_s, s_s.n)
                    op('g', lambda e, d=dsl, y=ysl, g=gsl:
                       e.tensor_tensor(d, y, g, ALU.mult),
                       sm=s_g, waits=[(s_s.h, s_s.n), (s_s.h, s_after_gate)])
                    yslot_guard[jy] = s_g.n
                    jy += 1
        mem.free("gate16", "S16", "ybuf")
        g_after_y16 = s_g.n
        v_after_y16 = s_v.n
        dbg("y16all", y16all[:], [128, NCH * L], F16,
            [(s_v.h, v_after_y16), (s_g.h, g_after_y16)])

        alpha_b = mem.alloc("alpha_b", [128, L], F32)
        beta16 = mem.alloc("beta16_r", [1, L], F16)
        h2all = mem.alloc("h2all", [128, NCH * L], F16)

        BAR = barrier()
        op('v', lambda e: e.tensor_scalar(alpha_b[:], s1b[:], 1.0, None,
                                          ALU.add), sm=s_v, waits=BAR)
        op('v', lambda e: e.tensor_tensor(beta16[0:1, :], negmu.ap()[0:1, :],
                                          s1b.ap()[0:1, :], ALU.mult), sm=s_v)
        v_after_prep = s_v.n
        dbg("colsum1", colsum116[:], [1, D], F16, [(s_v.h, v_after_prep)])
        dbg("beta", beta16[:], [1, L], F16, [(s_v.h, v_after_prep)])

        h2grp = {}
        h1grp = dict(h1grp0)
        for t in range(NT):
            gx = (g_x0, g_x1)[t % 2]
            if t >= 2:
                dma('g', x16s_[t % 2].ap().rearrange("p (c w) -> p c w", w=TT),
                    xTd[:, TT * t:TT * (t + 1)]
                    .rearrange("(c p) w -> p c w", p=128),
                    gx, waits=[(s_t.h, h1grp[(t - 2, NCH - 1)]
                                if (t - 2, NCH - 1) in h1grp
                                else h2grp[(t - 2, NCH - 1)])])
            for o in (range(NCH) if t > 0 else ()):
                bi, bw = BK.take()
                items = [(banks[bi].ap()[:, :],
                          w1_16.ap()[:, D * k + 128 * o:D * k + 128 * (o + 1)],
                          x16s_[t % 2].ap()[:, TT * k:TT * (k + 1)],
                          k == 0, False) for k in range(NCH)]
                items.append((banks[bi].ap()[:, :],
                              colsum116.ap()[0:1, 128 * o:128 * (o + 1)],
                              negmu.ap()[0:1, TT * t:TT * (t + 1)],
                              False, True))
                t_group(items, waits=bw + [(gx.h, gx.n),
                                           (s_v.h, v_after_prep)])
                tcnt = s_t.n
                h1grp[(t, o)] = tcnt
                vw = [(s_t.h, tcnt), (s_t.h, h2grp[(t - 1, NCH - 1)])]
                op('v', lambda e, b=banks[bi], o=o, t=t,
                   d=h1buf.ap()[:, TT * o:TT * (o + 1)],
                   s1=s1b.ap()[:, TT * t:TT * (t + 1)]:
                   e.tensor_tensor(d, b.ap()[:, :], s1, ALU.mult),
                   sm=s_v, waits=vw)
                h1ev[(t, o)] = s_v.n
                BK.set(bi, s_v, s_v.n)
            for o in range(NCH):
                bi, bw = BK.take()
                items = [(banks[bi].ap()[:, :],
                          w2_16.ap()[:, D * k + 128 * o:D * k + 128 * (o + 1)],
                          h1buf.ap()[:, TT * k:TT * (k + 1)],
                          k == 0, k == NCH - 1) for k in range(NCH)]
                t_group(items, waits=bw + [(s_v.h, h1ev[(t, NCH - 1)])])
                h2grp[(t, o)] = s_t.n
                op('s', lambda e, b=banks[bi], o=o, t=t,
                   d=h2all.ap()[:, L * o + TT * t:L * o + TT * (t + 1)]:
                   e.activation(d, b.ap()[:, :], AF.Copy),
                   sm=s_s, waits=[(s_t.h, h2grp[(t, o)])])
                BK.set(bi, s_s, s_s.n)
        s_after_h2 = s_s.n
        dbg("h2all", h2all[:], [128, NCH * L], F16, [(s_s.h, s_after_h2)])

        mem.free("x16slot0", "x16slot1", "h1buf", "w1_16", "w2_16",
                 "negmu_r", "woutT2f_s")
        x32s_ = [mem.alloc(f"x32slot{i}", [128, L], F32) for i in range(2)]
        ftmp = mem.alloc("ftmp", [128, 4 * L], F32)
        vtmp = mem.alloc("vtmp", [128, 4 * TT], F32)
        outstage = mem.alloc("outstage", [128, 4 * TT], F32)

        BAR2 = barrier()
        st_q = (q_st0, q_st1, q_st2, q_st3)
        st_g = (None, g_st1, None, g_st3)
        slot_guard = {}
        j = 0
        fin_done = {}
        for o in range(NCH):
            qx = (q_x0, q_x1)[o % 2]
            if o < 2:
                w = list(BAR2)
            else:
                fs, fv = fin_done[(o - 2, NT - 1)]
                w = [(fs.h, fv)]
            dma('q', x32s_[o % 2][:], xTd[128 * o:128 * (o + 1), :], qx,
                waits=w)
            bks = [BK.take() for _ in range(NT)]
            bw = [w2 for _, ws in bks for w2 in ws]
            items = []
            for k in range(NCH):
                for t in range(NT):
                    items.append((banks[bks[t][0]].ap()[:, :],
                                  w3_16.ap()[:, D * k + 128 * o:D * k + 128 * (o + 1)],
                                  h2all.ap()[:, L * k + TT * t:L * k + TT * (t + 1)],
                                  k == 0, k == NCH - 1))
            t_group(items, waits=bw + W34LD + [(s_s.h, s_after_h2)])
            h3cnt = s_t.n
            fsl_o = ftmp.ap()[:, L * (o % 4):L * (o % 4 + 1)]
            sw = [(s_t.h, h3cnt)]
            if o >= 4:
                fs, fv = fin_done[(o - 4, NT - 1)]
                sw.append((fs.h, fv))
            for t in range(NT):
                bi = bks[t][0]
                op('s', lambda e, b=banks[bi], o=o,
                   d=fsl_o[:, TT * t:TT * (t + 1)]:
                   e.activation(d, b.ap()[:, :], AF.Tanh, bias=fb_col(o)),
                   sm=s_s, waits=sw)
                sw = []
                BK.set(bi, s_s, s_s.n)
            s_tanh_o = s_s.n
            pre_done = {}
            for t in range(NT):
                jj = o * NT + t
                vsl = vtmp.ap()[:, TT * (jj % 4):TT * (jj % 4 + 1)]
                peng, psem = ('v', s_v) if (t % 2 == 0 or o == NCH - 1) \
                    else ('g', s_g)
                pw_ = [(s_s.h, s_tanh_o), (qx.h, qx.n)]
                if jj >= 4:
                    fs2, fv2 = fin_done[divmod(jj - 4, NT)]
                    pw_.append((fs2.h, fv2))
                op(peng, lambda e, v2=vsl, o=o, t=t,
                   x=x32s_[o % 2].ap()[:, TT * t:TT * (t + 1)],
                   a=alpha_b.ap()[:, TT * t:TT * (t + 1)]:
                   e.tensor_tensor(v2, x, a, ALU.mult), sm=psem, waits=pw_)
                op(peng, lambda e, v2=vsl, f=fsl_o[:, TT * t:TT * (t + 1)]:
                   e.tensor_tensor(v2, v2, f, ALU.add), sm=psem,
                   waits=[(psem.h, psem.n)])
                pre_done[t] = (psem, psem.n)
            bks2 = [BK.take() for _ in range(NT)]
            bw2 = [w2 for _, ws in bks2 for w2 in ws]
            items = []
            for k in range(NCH):
                for t in range(NT):
                    items.append((banks[bks2[t][0]].ap()[:, :],
                                  outw16.ap()[:, D * k + 128 * o:D * k + 128 * (o + 1)],
                                  y16all.ap()[:, L * k + TT * t:L * k + TT * (t + 1)],
                                  k == 0, False))
            for t in range(NT):
                items.append((banks[bks2[t][0]].ap()[:, :], onesr16b[0:1, :],
                              beta16.ap()[0:1, TT * t:TT * (t + 1)],
                              False, True))
            t_group(items, waits=bw2 + [(s_v.h, v_after_y16),
                                        (s_g.h, g_after_y16)])
            opcnt = s_t.n
            for t in range(NT):
                bi2 = bks2[t][0]
                slot = j % 4
                osl = outstage.ap()[:, TT * slot:TT * (slot + 1)]
                ring = 'q' if (t % 2 == 0 or o == NCH - 1) else 'g'
                stsem = st_q[slot] if ring == 'q' else st_g[slot]
                sg = [slot_guard[slot]] if slot in slot_guard else []
                op('s', lambda e, b=banks[bi2], o=o, d=osl:
                   e.add(d, b.ap()[:, :], outb_col(o)),
                   sm=s_s, waits=[(s_t.h, opcnt)] + sg)
                BK.set(bi2, s_s, s_s.n)
                feng, fsem = ('g', s_g) \
                    if (t % 2 == 0 and o < NCH - 1) else ('v', s_v)
                fw = [(s_s.h, s_s.n), (pre_done[t][0].h, pre_done[t][1])]
                if feng == 'g':
                    fw.append((s_g.h, s_g.n))
                op(feng, lambda e, d=osl,
                   v2=vtmp.ap()[:, TT * (j % 4):TT * (j % 4 + 1)]:
                   e.tensor_tensor(d, d, v2, ALU.add), sm=fsem, waits=fw)
                fin_done[(o, t)] = (fsem, fsem.n)
                dma(ring, outd[t, o], osl, stsem,
                    waits=[(fin_done[(o, t)][0].h, fin_done[(o, t)][1])])
                slot_guard[slot] = (stsem.h, stsem.n)
                j += 1

        for eng in 'gtvs':
            emit(eng, lambda e, dh=done.h: e.nop().then_inc(dh, 1))

        dma_finals = [(s.h, s.n) for s in
                      [q_sm, q_sm2, q_x0, q_x1, q_st0, q_st1, q_st2, q_st3,
                       g_st1, g_st3,
                       g_wx, g_wx2, g_wz,
                       g_pw, g_w12, g_w34, g_sm, g_x0, g_x1, q_dbg, q_xtb]
                      + q_xt + [s_t, s_v, s_s, s_g]]

        def fin(e, a=q_st0.n, b=q_st1.n, fins=tuple(dma_finals), dh=done.h,
                bh=boot.h):
            e.wait_ge(q_st0.h, a)
            e.wait_ge(q_st1.h, b)
            e.wait_ge(dh, 4)
            for (sh, v) in fins:
                if v > 0:
                    e.wait_ge(sh, v)
            e.sem_clear(bh)
        emit('q', fin)

        blk = es.enter_context(nc.Block())

        @blk.sync
        def _(eng):
            for f in ops['q']:
                f(eng)

        @blk.gpsimd
        def _(eng):
            for f in ops['g']:
                f(eng)

        @blk.tensor
        def _(eng):
            for f in ops['t']:
                f(eng)

        @blk.vector
        def _(eng):
            for f in ops['v']:
                f(eng)

        @blk.scalar
        def _(eng):
            for f in ops['s']:
                f(eng)

    return nc


_CACHE = {}


def host_prep(inputs):
    inp = {k: np.asarray(v) for k, v in inputs.items()}
    f32 = np.float32
    f16 = np.float16

    T = np.ascontiguousarray(inp["L0_attn_T"].astype(f32, copy=False))
    in_wT = np.ascontiguousarray(inp["L0_attn_in_w"].astype(f32).T)
    pw_wT = np.ascontiguousarray(inp["L0_attn_pw_w"].astype(f32).T)
    out_wT = np.ascontiguousarray(inp["L0_attn_out_w"].astype(f32).T)
    w1 = np.ascontiguousarray(inp["L0_ffn_weight"].astype(f32))
    w2 = np.ascontiguousarray(inp["L0_ffn_aux_weight1"].astype(f32))
    w3 = np.ascontiguousarray(inp["L0_ffn_aux_weight2"].astype(f32))
    winT2 = np.ascontiguousarray(
        np.concatenate([inp["L0_attn_f_win"], inp["L0_attn_b_win"]], axis=0)
        .astype(f32).T)
    woutT2 = np.ascontiguousarray(
        np.concatenate([inp["L0_attn_f_wout"].astype(f32).T,
                        inp["L0_attn_b_wout"].astype(f32).T], axis=0))
    ssm = np.stack([
        np.concatenate([inp["L0_attn_f_logA"], inp["L0_attn_b_logA"]]),
        np.concatenate([inp["L0_attn_f_B"], inp["L0_attn_b_B"]]),
        np.concatenate([inp["L0_attn_f_C"], inp["L0_attn_b_C"]]),
        np.concatenate([np.broadcast_to(inp["L0_attn_f_logdt"], (64,)),
                        np.broadcast_to(inp["L0_attn_b_logdt"], (64,))]),
    ], axis=1).astype(f32)
    ssm = np.ascontiguousarray(ssm)

    def cols8(v):
        return np.ascontiguousarray(v.astype(f32).reshape(8, 128).T)

    vec = np.concatenate([
        cols8(inp["L0_attn_norm_scale"]),
        cols8(inp["L0_attn_in_b"][:1024]),
        cols8(inp["L0_attn_in_b"][1024:]),
        cols8(inp["L0_attn_dw_b"]),
        cols8(inp["L0_attn_pw_b"]),
        cols8(inp["L0_ffn_bias"]),
        cols8(inp["L0_attn_out_b"]),
    ], axis=1)
    dww = (inp["L0_attn_dw_w"].astype(f32).reshape(8, 128, 3)
           .transpose(1, 2, 0).reshape(128, 24))
    pack = np.ascontiguousarray(np.concatenate([ssm, vec, dww], axis=1))

    shared = dict(
        in_wT=in_wT, pw_wT=pw_wT, out_wT=out_wT, w1=w1, w2=w2, w3=w3,
        winT2=winT2, woutT2=woutT2, pack=pack,
        onesr32=np.ones((128, 128), f32))

    return shared, T


def kernel(**inputs):
    shared, T = host_prep(inputs)

    if "nc" not in _CACHE:
        _CACHE["nc"] = build_program()
    nc = _CACHE["nc"]

    in_maps = []
    for b in range(8):
        m = dict(shared)
        m["xT"] = np.ascontiguousarray(T[b].T)
        in_maps.append(m)

    res = run_bass_kernel_spmd(nc, in_maps, core_ids=list(range(8)))

    out = np.empty((8, L, D), np.float32)
    for b in range(8):
        a = res.results[b]["outT"]
        out[b] = a.transpose(0, 3, 1, 2).reshape(L, D)
    return out


if __name__ == "__main__":
    pass


# version 33 (final)
# speedup vs baseline: 1.1919x; 1.0197x over previous
import contextlib
import numpy as np
import concourse.bass as bass
import concourse.mybir as mybir
from concourse.bass_utils import run_bass_kernel_spmd

F32 = mybir.dt.float32
F32R = mybir.dt.float32r
F16 = mybir.dt.float16
U8 = mybir.dt.uint8
AF = mybir.ActivationFunctionType
ALU = mybir.AluOpType

L = 2048
D = 1024
NCH = 8
TT = 512
NT = 4
EPS_LN = 1e-5
EPS_RMS = 1e-6
EPS_SSM = 1e-6


class Sem:
    def __init__(self, h):
        self.h = h
        self.n = 0


class Mem:

    LO = 17408
    HI = 229344

    def __init__(self, nc):
        self.nc = nc
        self.free_list = [(self.LO, self.HI)]
        self.live = {}
        self.peak = 0

    def alloc(self, name, shape, dtype):
        nbytes = 1
        for s in shape[1:]:
            nbytes *= s
        nbytes *= {F32: 4, F32R: 4, F16: 2, U8: 1}[dtype]
        nbytes = (nbytes + 63) & ~63
        fits = [(b - a, idx) for idx, (a, b) in enumerate(self.free_list)
                if b - a >= nbytes]
        fits.sort()
        for _, idx in fits[:1]:
            a, b = self.free_list[idx]
            if True:
                self.free_list[idx] = (a + nbytes, b)
                if self.free_list[idx][0] == self.free_list[idx][1]:
                    self.free_list.pop(idx)
                h = self.nc.alloc_sbuf_tensor_at(name, list(shape), dtype,
                                                 offset=a)
                self.live[name] = (a, nbytes)
                used = self.HI - self.LO - sum(e - s for s, e in self.free_list)
                self.peak = max(self.peak, used)
                return h
        raise RuntimeError(f"manual SBUF OOM at {name} ({nbytes}B); "
                           f"free={self.free_list}")

    def free(self, *names):
        for n in names:
            a, nbytes = self.live.pop(n)
            self.free_list.append((a, a + nbytes))
        self.free_list.sort()
        out = []
        for s, e in self.free_list:
            if out and out[-1][1] >= s:
                out[-1] = (out[-1][0], max(out[-1][1], e))
            else:
                out.append((s, e))
        self.free_list = [tuple(x) for x in out]


def build_program(debug=False):
    nc = bass.Bass("TRN2", target_bir_lowering=False)

    xTd = nc.dram_tensor("xT", [D, L], F32, kind="ExternalInput").ap()
    inwTd = nc.dram_tensor("in_wT", [D, 2 * D], F32, kind="ExternalInput").ap()
    pwTd = nc.dram_tensor("pw_wT", [D, D], F32, kind="ExternalInput").ap()
    outwTd = nc.dram_tensor("out_wT", [D, D], F32, kind="ExternalInput").ap()
    w1d = nc.dram_tensor("w1", [D, D], F32, kind="ExternalInput").ap()
    w2d = nc.dram_tensor("w2", [D, D], F32, kind="ExternalInput").ap()
    w3d = nc.dram_tensor("w3", [D, D], F32, kind="ExternalInput").ap()
    winTd = nc.dram_tensor("winT2", [D, 128], F32, kind="ExternalInput").ap()
    woutd = nc.dram_tensor("woutT2", [128, D], F32, kind="ExternalInput").ap()
    packd = nc.dram_tensor("pack", [128, 84], F32, kind="ExternalInput").ap()
    onesr32d = nc.dram_tensor("onesr32", [128, 128], F32R,
                              kind="ExternalInput").ap()
    outd = nc.dram_tensor("outT", [NT, NCH, 128, TT], F32,
                          kind="ExternalOutput").ap()

    es = contextlib.ExitStack()
    with es:
        mem = Mem(nc)

        banks = [es.enter_context(nc.psum_tensor(f"pb{i}", [128, TT], F32))
                 for i in range(8)]

        def sem(name):
            return Sem(es.enter_context(nc.semaphore(name)))

        boot = sem("boot")
        done = sem("done")
        s_t = sem("s_t")
        s_v = sem("s_v")
        s_s = sem("s_s")
        s_g = sem("s_g")
        q_sm = sem("q_sm")
        q_sm2 = sem("q_sm2")
        q_x0 = sem("q_x0")
        q_x1 = sem("q_x1")
        q_x2 = sem("q_x2")
        q_x3 = sem("q_x3")
        q_st0 = sem("q_st0")
        q_st1 = sem("q_st1")
        q_st2 = sem("q_st2")
        q_st3 = sem("q_st3")
        g_st1 = sem("g_st1")
        g_st3 = sem("g_st3")
        q_xt = [sem(f"q_xt{i}") for i in range(4)]
        q_xtb = sem("q_xtb")
        g_wx = sem("g_wx")
        g_wx2 = sem("g_wx2")
        g_wz = sem("g_wz")
        g_pw = sem("g_pw")
        g_w12 = sem("g_w12")
        g_w34 = sem("g_w34")
        g_sm = sem("g_sm")
        g_x0 = sem("g_x0")
        g_x1 = sem("g_x1")
        q_dbg = sem("q_dbg")
        dbg_tensors = {}
        ALL_SEMS = [boot, done, s_t, s_v, s_s, s_g, q_sm, q_sm2, q_x0, q_x1,
                    q_x2, q_x3, q_st0,
                    q_st1, q_st2, q_st3, g_st1, g_st3,
                    g_wx, g_wx2, g_wz, g_pw, g_w12, g_w34, g_sm,
                    g_x0, g_x1, q_dbg, q_xtb] + q_xt
        BOOT = 1 << 16

        ops = {k: [] for k in "qgtvs"}
        last_wait = {k: {} for k in "qgtvs"}

        def emit(eng, fn):
            ops[eng].append(fn)

        def _mkwaits(eng, waits):
            out = []
            for (h, v) in waits:
                if v <= 0:
                    continue
                if last_wait[eng].get(h, -1) >= v:
                    continue
                last_wait[eng][h] = v
                out.append((h, v))
            return tuple(out)

        def op(eng, build_fn, sm=None, inc=1, waits=(), drain=False):
            inline_w = None
            if eng == 'v' and s_v.n > 0:
                inline_w = (s_v.h, s_v.n)
            elif eng == 's' and s_s.n > 0:
                inline_w = (s_s.h, s_s.n)
            ws = _mkwaits(eng, waits)
            if inline_w is None and ws:
                inline_w, ws = ws[0], ws[1:]
            if sm is not None:
                sm.n += inc
            smh = sm.h if sm is not None else None

            def f(e, bf=build_fn, ws=ws, iw=inline_w, smh=smh, k=inc, dr=drain):
                for (h, v) in ws:
                    e.wait_ge(h, v)
                ins = bf(e)
                if iw is not None:
                    ins._wait_ge(iw[0], iw[1])
                if smh is not None:
                    ins.then_inc(smh, k)
                if dr:
                    e.drain()
            emit(eng, f)
            return sm.n if sm is not None else None

        def dma(eng, dst, src, sm, waits=()):
            return op(eng, lambda e, d=dst, s=src: e.dma_start(out=d, in_=s),
                      sm=sm, inc=16, waits=waits)

        def barrier():
            w = [(s_t.h, s_t.n), (s_v.h, s_v.n), (s_s.h, s_s.n)]
            if s_g.n > 0:
                w.append((s_g.h, s_g.n))
            return w

        def dbg(name, src_ap, shape, dtype, waits):
            if not debug:
                return
            d = nc.dram_tensor("dbg_" + name, list(shape), dtype,
                               kind="ExternalOutput").ap()
            dbg_tensors[name] = d
            dma('q', d, src_ap, q_dbg, waits=waits)

        class Banks:
            def __init__(self):
                self.guard = [None] * 8
                self.i = 0

            def take(self):
                i = self.i % 8
                self.i += 1
                g = self.guard[i]
                self.guard[i] = None
                return i, ([g] if g else [])

            def set(self, i, semobj, val):
                self.guard[i] = (semobj.h, val)

        BK = Banks()

        def t_group(items, waits=()):
            norm = []
            for it in items:
                if len(it) == 5:
                    norm.append(((),) + tuple(it))
                else:
                    w = _mkwaits('t', it[0])
                    norm.append((w,) + tuple(it[1:]))
            ws = _mkwaits('t', waits)
            s_t.n += 1

            def f(e, it=tuple(norm), ws=ws, smh=s_t.h):
                used_w0 = not ws or (it and not it[0][0])
                for (h, v) in (ws[1:] if used_w0 else ws):
                    e.wait_ge(h, v)
                first = True
                ins = None
                for (iw, o, lh, r, st, sp) in it:
                    for (h, v) in iw[1:]:
                        e.wait_ge(h, v)
                    ins = e.matmul(o, lh, r, start=st, stop=sp)
                    if iw:
                        ins._wait_ge(iw[0][0], iw[0][1])
                    elif first and ws:
                        ins._wait_ge(ws[0][0], ws[0][1])
                    first = False
                ins.then_inc(smh, 1)
            emit('t', f)
            return s_t.n

        FIRST_SEMS = (q_xt[0], q_sm)
        def _boot_q1(e, sems=tuple(s.h for s in FIRST_SEMS)):
            for sh in sems:
                e.sem_clear(sh)
        emit('q', _boot_q1)

        def _boot_q2(e, sems=tuple(s.h for s in ALL_SEMS
                                   if s not in FIRST_SEMS),
                     bh=boot.h, big=BOOT):
            for sh in sems:
                e.sem_clear(sh)
            e.sem_inc(bh, big)

        def emit_boot2():
            emit('q', _boot_q2)
        for eng in 'gtvs':
            emit(eng, lambda e, bh=boot.h, big=BOOT: e.wait_ge(bh, big))

        class APView:
            def __init__(self, base_ap):
                self._ap = base_ap

            def ap(self):
                return self._ap

            def __getitem__(self, idx):
                return self._ap[idx]

        ones16 = mem.alloc("ones16_s", [128, 128], F16)
        onesr32 = mem.alloc("onesr32_s", [128, 128], F32R)
        onesr16 = mem.alloc("onesr16_s", [1, TT], F16)
        onesr16b = mem.alloc("onesr16b_s", [1, 128], F16)
        consts = mem.alloc("consts_s", [128, 2], F32)
        pack = mem.alloc("pack_s", [128, 84], F32)
        ssm = APView(pack.ap()[:, 0:4])
        vec = APView(pack.ap()[:, 4:60])
        dww = APView(pack.ap()[:, 60:84])
        prm = mem.alloc("prm_s", [128, 12], F32)
        cvb = mem.alloc("cvb_s", [128, 8], F32)
        e0n = mem.alloc("e0n_s", [128, 8], F32)
        e1n = mem.alloc("e1n_s", [128, 8], F32)
        winT16 = mem.alloc("winT16_s", [128, D], F16)
        wout16 = mem.alloc("wout16_s", [128, D], F16)
        woutT2f = mem.alloc("woutT2f_s", [128, D], F32)

        negmu = mem.alloc("negmu_r", [1, L], F16)
        nm2row = mem.alloc("nm2_r", [1, L], F16)
        colsum = mem.alloc("colsum_r", [1, 2 * D], F16)

        s1b = mem.alloc("s1b", [128, L], F32)
        s2b = mem.alloc("s2b", [128, L], F32)

        xTs = mem.alloc("xTs", [128, NCH * L], F16)
        x2buf = mem.alloc("x2buf", [128, 2 * L], F16)
        SxB = mem.alloc("SxB", [128, L], F32)
        Sx2B = mem.alloc("Sx2B", [128, L], F32)
        wxs = mem.alloc("wxs", [128, NCH * D], F16)
        wzs = mem.alloc("wzs", [128, NCH * D], F16)
        xt32 = mem.alloc("xt32", [128, NCH * L], F32R)

        def xT(c):
            return xTs.ap()[:, L * c:L * (c + 1)]

        def x32c(c):
            return xt32.ap()[:, L * c:L * (c + 1)]

        def wx(k):
            return wxs.ap()[:, D * k:D * (k + 1)]

        def wz(k):
            return wzs.ap()[:, D * k:D * (k + 1)]

        dma('q', xt32.ap()[:, 0:L], xTd.bitcast(F32R)[0:128, :], q_xt[0])
        dma('q', onesr32[:], onesr32d, q_sm)
        dma('q', pack[:], packd, q_sm)
        emit_boot2()
        dma('q', xt32.ap()[:, L:2 * L], xTd.bitcast(F32R)[128:256, :], q_xtb)
        op('v', lambda e: e.memset(ones16[:], 1.0), sm=s_v)
        op('v', lambda e: e.memset(onesr16[:], 1.0), sm=s_v, drain=True)
        op('v', lambda e: e.memset(onesr16b[:], 1.0), sm=s_v, drain=True)
        op('v', lambda e: e.memset(consts.ap()[:, 0:1], EPS_LN), sm=s_v,
           drain=True)
        op('v', lambda e: e.memset(consts.ap()[:, 1:2], EPS_LN * EPS_RMS),
           sm=s_v, drain=True)
        v_after_consts = s_v.n
        for i in range(1, 4):
            dma('q', xt32.ap()[:, 2 * L * i:2 * L * (i + 1)]
                .rearrange("p (c w) -> p c w", w=L),
                xTd.bitcast(F32R)[256 * i:256 * (i + 1), :]
                .rearrange("(c p) w -> p c w", p=128), q_xt[i])
        dma('q', woutT2f[:], woutd, q_sm2)
        dma('g', wxs.ap()[:, 0:4 * D].rearrange("p (k o) -> p k o", o=D),
            inwTd[0:512, 0:D].rearrange("(k p) o -> p k o", p=128), g_wx)
        dma('g', wxs.ap()[:, 4 * D:].rearrange("p (k o) -> p k o", o=D),
            inwTd[512:1024, 0:D].rearrange("(k p) o -> p k o", p=128), g_wx2)
        dma('g', wzs.ap().rearrange("p (k o) -> p k o", o=D),
            inwTd[:, D:2 * D].rearrange("(k p) o -> p k o", p=128), g_wz)
        dma('g', winT16.ap().rearrange("p (k o) -> p k o", o=128),
            winTd.rearrange("(k p) o -> p k o", p=128), g_sm)

        ns_col = lambda k: vec.ap()[:, k:k + 1]
        bz_col = lambda o: vec.ap()[:, 16 + o:17 + o]
        pwb_col = lambda e: vec.ap()[:, 32 + e:33 + e]
        fb_col = lambda o: vec.ap()[:, 40 + o:41 + o]
        outb_col = lambda o: vec.ap()[:, 48 + o:49 + o]
        c_ln = consts.ap()[:, 0:1]
        c_ee = consts.ap()[:, 1:2]

        SMALL_LD = [(q_sm.h, q_sm.n)]
        XTLD = [[(q_xt[c // 2].h, q_xt[c // 2].n)] for c in range(NCH)]
        XTLD[1] = [(q_xtb.h, 16)]

        ns16 = mem.alloc("ns16_s", [128, 8], F16)

        P_dt = prm.ap()[:, 0:1]
        P_A = prm.ap()[:, 1:2]
        P_Abar = prm.ap()[:, 2:3]
        P_Bbar = prm.ap()[:, 3:4]
        P_eA = prm.ap()[:, 4:5]
        P_t0 = prm.ap()[:, 5:6]
        P_t1 = prm.ap()[:, 6:7]
        P_t2 = prm.ap()[:, 7:8]
        P_m = prm.ap()[:, 8:9]
        w0s = dww.ap()[:, 0:8]
        w1s = dww.ap()[:, 8:16]
        w2s = dww.ap()[:, 16:24]
        op('v', lambda e: e.tensor_tensor(cvb[:], w0s, w1s, ALU.add), sm=s_v,
           waits=SMALL_LD, drain=True)
        op('v', lambda e: e.tensor_tensor(cvb[:], cvb[:], w2s, ALU.add),
           sm=s_v, drain=True)
        op('v', lambda e: e.tensor_tensor(cvb[:], cvb[:], vec.ap()[:, 8:16],
                                          ALU.mult), sm=s_v, drain=True)
        op('v', lambda e: e.tensor_tensor(cvb[:], cvb[:], vec.ap()[:, 24:32],
                                          ALU.add), sm=s_v, drain=True)
        op('v', lambda e: e.tensor_tensor(e0n[:], vec.ap()[:, 8:16], w0s,
                                          ALU.mult), sm=s_v, drain=True)
        op('v', lambda e: e.tensor_scalar(e0n[:], e0n[:], -1.0, None, ALU.mult),
           sm=s_v, drain=True)
        op('v', lambda e: e.tensor_tensor(e1n[:], vec.ap()[:, 8:16], w2s,
                                          ALU.mult), sm=s_v, drain=True)
        op('v', lambda e: e.tensor_scalar(e1n[:], e1n[:], -1.0, None, ALU.mult),
           sm=s_v, drain=True)
        op('v', lambda e: e.tensor_copy(ns16[:], vec.ap()[:, 0:8]), sm=s_v,
           drain=True)
        v_after_ns16 = s_v.n

        sq_done = []
        statgrp_after = []
        for c in range(NCH):
            w = list(XTLD[c])
            if c >= 2:
                w.append((s_t.h, statgrp_after[c - 2]))
            x2slot = x2buf.ap()[:, L * (c % 2):L * (c % 2 + 1)]
            if c % 2 == 0:
                op('s', lambda e, xc=x32c(c), d=x2slot:
                   e.activation(d, xc, AF.Square), sm=s_s, waits=w)
                sq_done.append((s_s, s_s.n))
            else:
                op('v', lambda e, xc=x32c(c), d=x2slot:
                   e.tensor_tensor(d, xc, xc, ALU.mult), sm=s_v, waits=w)
                sq_done.append((s_v, s_v.n))
            items = []
            for tt in range(NT):
                items.append((banks[tt].ap()[:, :], onesr32[:],
                              x32c(c)[:, TT * tt:TT * (tt + 1)],
                              c == 0, c == NCH - 1))
            for tt in range(NT):
                items.append((banks[4 + tt].ap()[:, :], ones16[:],
                              x2slot[:, TT * tt:TT * (tt + 1)],
                              c == 0, c == NCH - 1))
            t_group(items, waits=XTLD[c] + [(sq_done[c][0].h, sq_done[c][1]),
                                            (s_v.h, v_after_consts)]
                    + (SMALL_LD if c == 0 else []))
            statgrp_after.append(s_t.n)
        t_after_stats = s_t.n

        op('s', lambda e: e.activation(P_dt, ssm.ap()[:, 3:4], AF.Exp),
           sm=s_s, waits=SMALL_LD)
        op('s', lambda e: e.activation(P_eA, ssm.ap()[:, 0:1], AF.Exp), sm=s_s)
        s_after_pexp = s_s.n
        op('v', lambda e: e.tensor_scalar(P_A, P_eA, -1.0, None, ALU.mult),
           sm=s_v, waits=[(s_s.h, s_after_pexp)], drain=True)
        op('v', lambda e: e.tensor_tensor(P_t0, P_A, P_dt, ALU.mult), sm=s_v,
           drain=True)
        v_after_pt0 = s_v.n
        op('v', lambda e: e.tensor_single_scalar(P_m, P_eA, EPS_SSM,
                                                 ALU.is_lt), sm=s_v, drain=True)
        op('v', lambda e: e.tensor_scalar(P_t1, P_A, -1.0, 1.0, ALU.mult,
                                          ALU.add), sm=s_v, drain=True)
        op('v', lambda e: e.tensor_tensor(P_t1, P_m, P_t1, ALU.mult), sm=s_v,
           drain=True)
        op('v', lambda e: e.tensor_tensor(P_t1, P_A, P_t1, ALU.add), sm=s_v,
           drain=True)
        op('v', lambda e: e.reciprocal(P_t2, P_t1), sm=s_v, drain=True)

        for tt in range(NT):
            op('s', lambda e, b=banks[tt], d=SxB.ap()[:, TT * tt:TT * (tt + 1)]:
               e.activation(d, b.ap()[:, :], AF.Copy),
               sm=s_s, waits=[(s_t.h, t_after_stats)])
        for tt in range(NT):
            op('s', lambda e, b=banks[4 + tt],
               d=Sx2B.ap()[:, TT * tt:TT * (tt + 1)]:
               e.activation(d, b.ap()[:, :], AF.Copy), sm=s_s)
        s_after_statevac = s_s.n
        for i in range(8):
            BK.set(i, s_s, s_after_statevac)
        BK.i = 8

        SEV = [(s_s.h, s_after_statevac)]
        op('v', lambda e: e.tensor_scalar(negmu[0:1, :], SxB.ap()[0:1, :],
                                          -1.0 / D, None, ALU.mult),
           sm=s_v, waits=SEV)
        op('s', lambda e: e.activation(SxB[:], SxB[:], AF.Square, scale=1.0 / D),
           sm=s_s, waits=[(s_v.h, s_v.n)])
        op('v', lambda e: e.tensor_scalar(Sx2B[:], Sx2B[:], 1.0 / D, None,
                                          ALU.mult), sm=s_v)
        op('v', lambda e: e.tensor_tensor(Sx2B[:], Sx2B[:], SxB[:], ALU.subtract),
           sm=s_v, waits=[(s_s.h, s_s.n)])
        v_after_var = s_v.n
        op('s', lambda e: e.activation(SxB[:], Sx2B[:], AF.Ln, bias=c_ee,
                                       scale=1.0 + EPS_RMS),
           sm=s_s, waits=[(s_v.h, v_after_var)])
        op('s', lambda e: e.activation(s2b[:], SxB[:], AF.Exp, scale=-0.5),
           sm=s_s)
        s_after_s2 = s_s.n
        ts16_done = []
        for c in range(NCH):
            eng, sm_ = ('v', s_v)
            op(eng, lambda e, c=c: e.scalar_tensor_tensor(
                xT(c), x32c(c), pack.ap()[:, 4 + c:5 + c], s2b[:],
                ALU.mult, ALU.mult),
               sm=sm_, waits=[(s_s.h, s_after_s2), (s_t.h, t_after_stats)])
            ts16_done.append((sm_, sm_.n))
        op('v', lambda e: e.tensor_tensor(nm2row[0:1, :], negmu.ap()[0:1, :],
                                          s2b.ap()[0:1, :], ALU.mult), sm=s_v)
        v_after_nm2 = s_v.n
        op('s', lambda e: e.activation(P_Abar, P_t0, AF.Exp), sm=s_s,
           waits=[(s_v.h, v_after_pt0)])
        s_after_abar = s_s.n
        op('v', lambda e: e.tensor_scalar(P_t0, P_Abar, -1.0, None, ALU.add),
           sm=s_v, waits=[(s_s.h, s_after_abar)], drain=True)
        op('v', lambda e: e.tensor_tensor(P_t0, P_t0, P_t2, ALU.mult), sm=s_v,
           drain=True)
        op('v', lambda e: e.tensor_tensor(P_t1, P_dt, P_t0, ALU.subtract),
           sm=s_v, drain=True)
        op('v', lambda e: e.tensor_tensor(P_t1, P_m, P_t1, ALU.mult), sm=s_v,
           drain=True)
        op('v', lambda e: e.tensor_tensor(P_t0, P_t0, P_t1, ALU.add), sm=s_v,
           drain=True)
        op('v', lambda e: e.tensor_tensor(P_Bbar, P_t0, ssm.ap()[:, 1:2],
                                          ALU.mult), sm=s_v, drain=True)
        op('v', lambda e: e.tensor_scalar(wout16[:], woutT2f[:],
                                          ssm.ap()[:, 2:3], None, ALU.mult),
           sm=s_v, waits=[(q_sm2.h, 16)])
        op('s', lambda e: e.activation(s1b[:], Sx2B[:], AF.Ln, bias=c_ln),
           sm=s_s)
        op('s', lambda e: e.activation(s1b[:], s1b[:], AF.Exp, scale=-0.5),
           sm=s_s)
        s_after_s1 = s_s.n
        v_after_statmath = s_v.n
        dbg("s1b", s1b[:], [128, L], F32, [(s_s.h, s_s.n)])
        dbg("s2b", s2b[:], [128, L], F32, [(s_s.h, s_s.n)])
        dbg("negmu", negmu[:], [1, L], F16, [(s_v.h, s_v.n)])

        mem.free("x2buf", "SxB", "Sx2B", "xt32")
        x1buf = mem.alloc("x1buf", [128, 2 * L], F32)
        convbuf = mem.alloc("convbuf", [128, L], F32)
        xc1 = mem.alloc("xc1", [128, NCH * L], F16)
        gate16 = mem.alloc("gate16", [128, NCH * L], F16)
        pw16 = mem.alloc("pw16", [128, NCH * D], F16)
        dma('g', pw16.ap().rearrange("p (k o) -> p k o", o=D),
            pwTd.rearrange("(k p) o -> p k o", p=128), g_pw, waits=barrier())

        cs = []
        for half, wfun, gsem in ((0, wx, g_wx2), (1, wz, g_wz)):
            for j2 in range(2):
                bi, bwc = BK.take()
                items = [(banks[bi].ap()[0:1, :], ns16.ap()[:, k:k + 1],
                          wfun(k)[:, TT * j2:TT * (j2 + 1)],
                          k == 0, k == NCH - 1) for k in range(NCH)]
                t_group(items, waits=bwc + [(s_v.h, v_after_ns16),
                                            (g_wx.h, g_wx.n),
                                            (gsem.h, gsem.n)])
                cs.append((bi, half, j2, s_t.n))
        for (bi, half, j2, tcnt) in cs:
            dst = colsum.ap()[0:1, D * half + TT * j2:D * half + TT * (j2 + 1)]
            op('s', lambda e, b=banks[bi], d=dst:
               e.activation(d, b.ap()[0:1, :], AF.Copy), sm=s_s,
               waits=[(s_t.h, tcnt)])
            BK.set(bi, s_s, s_s.n)
        s_after_colsum = s_s.n

        conv_done = []
        half_pend = {}
        for o in (0, 1):
            bks = [BK.take() for _ in range(NT)]
            bw = [w for _, ws in bks for w in ws]
            items = []
            for k in range(4):
                for t in range(NT):
                    items.append((((ts16_done[k][0].h, ts16_done[k][1]),),
                                  banks[bks[t][0]].ap()[:, :],
                                  wx(k)[:, 128 * o:128 * (o + 1)],
                                  xT(k)[:, TT * t:TT * (t + 1)],
                                  k == 0, False))
            t_group(items, waits=bw + [(g_wx.h, g_wx.n)])
            half_pend[o] = bks
        for o in range(NCH):
            if o in half_pend:
                bks = half_pend[o]
                bw = []
                items = []
                for k in range(4, NCH):
                    for t in range(NT):
                        items.append((((ts16_done[k][0].h, ts16_done[k][1]),),
                                      banks[bks[t][0]].ap()[:, :],
                                      wx(k)[:, 128 * o:128 * (o + 1)],
                                      xT(k)[:, TT * t:TT * (t + 1)],
                                      False, False))
            else:
                bks = [BK.take() for _ in range(NT)]
                bw = [w for _, ws in bks for w in ws]
                items = []
                for k in range(NCH):
                    for t in range(NT):
                        items.append((((ts16_done[k][0].h, ts16_done[k][1]),),
                                      banks[bks[t][0]].ap()[:, :],
                                      wx(k)[:, 128 * o:128 * (o + 1)],
                                      xT(k)[:, TT * t:TT * (t + 1)],
                                      k == 0, False))
            for t in range(NT):
                items.append((((s_s.h, s_after_colsum),
                               (s_v.h, v_after_nm2)),
                              banks[bks[t][0]].ap()[:, :],
                              colsum.ap()[0:1, 128 * o:128 * (o + 1)],
                              nm2row.ap()[0:1, TT * t:TT * (t + 1)],
                              False, True))
            t_group(items, waits=bw + [(g_wx2.h, g_wx2.n)])
            tcnt = s_t.n
            sw2 = [(s_t.h, tcnt)]
            if len(conv_done) >= 2:
                sw2.append((s_v.h, conv_done[-2]))
            x1sl = x1buf.ap()[:, L * (o % 2):L * (o % 2 + 1)]
            for t in range(NT):
                bi = bks[t][0]
                op('s', lambda e, b=banks[bi],
                   d=x1sl[:, TT * t:TT * (t + 1)]:
                   e.copy(d, b.ap()[:, :]),
                   sm=s_s, waits=sw2)
                BK.set(bi, s_s, s_s.n)
            xcc = xc1.ap()[:, L * o:L * (o + 1)]
            x1sl = x1buf.ap()[:, L * (o % 2):L * (o % 2 + 1)]
            op('v', lambda e, o=o, x1=x1sl: e.tensor_scalar(
                convbuf[:], x1, dww.ap()[:, 8 + o:9 + o],
                cvb.ap()[:, o:o + 1], ALU.mult, ALU.add), sm=s_v,
               waits=[(s_s.h, s_s.n)])
            op('v', lambda e, o=o, x1=x1sl: e.scalar_tensor_tensor(
                convbuf.ap()[:, 1:L], x1[:, 0:L - 1],
                dww.ap()[:, o:o + 1], convbuf.ap()[:, 1:L],
                ALU.mult, ALU.add), sm=s_v)
            op('v', lambda e, o=o, x1=x1sl: e.scalar_tensor_tensor(
                convbuf.ap()[:, 0:L - 1], x1[:, 1:L],
                dww.ap()[:, 16 + o:17 + o], convbuf.ap()[:, 0:L - 1],
                ALU.mult, ALU.add), sm=s_v)
            op('v', lambda e, o=o: e.tensor_scalar(
                convbuf.ap()[:, 0:1], convbuf.ap()[:, 0:1],
                e0n.ap()[:, o:o + 1], None, ALU.add), sm=s_v, drain=True)
            op('v', lambda e, o=o: e.tensor_scalar(
                convbuf.ap()[:, L - 1:L], convbuf.ap()[:, L - 1:L],
                e1n.ap()[:, o:o + 1], None, ALU.add), sm=s_v, drain=True)
            op('v', lambda e, d=xcc: e.tensor_copy(d, convbuf[:]), sm=s_v)
            conv_done.append(s_v.n)
        t_after_inx = s_t.n
        for o in range(NCH):
            bks = [BK.take() for _ in range(NT)]
            bw = [w for _, ws in bks for w in ws]
            items = []
            for k in range(NCH):
                for t in range(NT):
                    items.append((banks[bks[t][0]].ap()[:, :],
                                  wz(k)[:, 128 * o:128 * (o + 1)],
                                  xT(k)[:, TT * t:TT * (t + 1)],
                                  k == 0, False))
            for t in range(NT):
                items.append((((s_s.h, s_after_colsum),),
                              banks[bks[t][0]].ap()[:, :],
                              colsum.ap()[0:1, D + 128 * o:D + 128 * (o + 1)],
                              nm2row.ap()[0:1, TT * t:TT * (t + 1)],
                              False, True))
            t_group(items, waits=bw)
            tcnt = s_t.n
            for t in range(NT):
                bi = bks[t][0]
                op('s', lambda e, b=banks[bi], o=o,
                   d=gate16.ap()[:, L * o + TT * t:L * o + TT * (t + 1)]:
                   e.activation(d, b.ap()[:, :], AF.Sigmoid, bias=bz_col(o)),
                   sm=s_s, waits=[(s_t.h, tcnt)])
                BK.set(bi, s_s, s_s.n)
        t_after_z = s_t.n
        s_after_gate = s_s.n
        mem.free("xTs", "wxs", "wzs", "x1buf", "convbuf", "s2b")
        xconv16 = mem.alloc("xconv16", [128, NCH * L], F16)
        w1_16 = mem.alloc("w1_16", [128, NCH * D], F16)
        w2_16 = mem.alloc("w2_16", [128, NCH * D], F16)
        dma('g', w1_16.ap().rearrange("p (k o) -> p k o", o=D),
            w1d.rearrange("(k p) o -> p k o", p=128), g_w12, waits=barrier())
        dma('g', w2_16.ap().rearrange("p (k o) -> p k o", o=D),
            w2d.rearrange("(k p) o -> p k o", p=128), g_w12)
        x16s_ = [mem.alloc(f"x16slot{i}", [128, NCH * TT], F16)
                 for i in range(2)]
        for t in (0, 1):
            dma('g', x16s_[t].ap().rearrange("p (c w) -> p c w", w=TT),
                xTd[:, TT * t:TT * (t + 1)].rearrange("(c p) w -> p c w", p=128),
                (g_x0, g_x1)[t], waits=barrier())

        PWLD = [(g_pw.h, g_pw.n)]
        for e_ in range(NCH):
            bks = [BK.take() for _ in range(NT)]
            bw = [w for _, ws in bks for w in ws]
            items = []
            for k in range(NCH):
                for t in range(NT):
                    items.append((banks[bks[t][0]].ap()[:, :],
                                  pw16.ap()[:, D * k + 128 * e_:D * k + 128 * (e_ + 1)],
                                  xc1.ap()[:, L * k + TT * t:L * k + TT * (t + 1)],
                                  k == 0, k == NCH - 1))
            t_group(items, waits=bw + PWLD + [(s_v.h, conv_done[NCH - 1])])
            tcnt = s_t.n
            for t in range(NT):
                bi = bks[t][0]
                op('s', lambda e, b=banks[bi], e_=e_,
                   d=xconv16.ap()[:, L * e_ + TT * t:L * e_ + TT * (t + 1)]:
                   e.add(d, b.ap()[:, :], pwb_col(e_)),
                   sm=s_s, waits=[(s_t.h, tcnt)])
                BK.set(bi, s_s, s_s.n)
        s_after_pw = s_s.n
        t_after_pw = s_t.n

        mem.free("pw16", "xc1")
        u_scan = mem.alloc("u_scan", [128, L], F32)
        abart = mem.alloc("abart", [128, L], F32)
        S_scan = mem.alloc("S_scan", [128, L], F32)
        S16 = mem.alloc("S16", [128, L], F16)
        w3_16 = mem.alloc("w3_16", [128, NCH * D], F16)
        outw16 = mem.alloc("outw16", [128, NCH * D], F16)
        dma('g', w3_16.ap().rearrange("p (k o) -> p k o", o=D),
            w3d.rearrange("(k p) o -> p k o", p=128), g_w34, waits=barrier())
        dma('g', outw16.ap().rearrange("p (k o) -> p k o", o=D),
            outwTd.rearrange("(k p) o -> p k o", p=128), g_w34)

        colsum116 = mem.alloc("colsum116_r", [1, D], F16)
        W12LD = [(g_w12.h, g_w12.n)]
        c1banks = []
        for j in range(2):
            bi, bw = BK.take()
            items = [(banks[bi].ap()[0:1, :], ones16.ap()[:, 0:1],
                      w1_16.ap()[:, D * k + TT * j:D * k + TT * (j + 1)],
                      k == 0, k == NCH - 1) for k in range(NCH)]
            t_group(items, waits=bw + W12LD)
            c1banks.append((bi, j, s_t.n))
        for (bi, j, tcnt) in c1banks:
            op('s', lambda e, b=banks[bi],
               d=colsum116.ap()[0:1, TT * j:TT * (j + 1)]:
               e.activation(d, b.ap()[0:1, :], AF.Copy), sm=s_s,
               waits=[(s_t.h, tcnt)])
            BK.set(bi, s_s, s_s.n)
        s_after_c1 = s_s.n
        u_rev = u_scan.ap()[64:128, ::-1]
        bks = [BK.take() for _ in range(NT)]
        bw = [w for _, ws in bks for w in ws]
        items = []
        for k in range(NCH):
            for t in range(NT):
                items.append((banks[bks[t][0]].ap()[:, :],
                              winT16.ap()[:, 128 * k:128 * (k + 1)],
                              xconv16.ap()[:, L * k + TT * t:L * k + TT * (t + 1)],
                              k == 0, k == NCH - 1))
        t_group(items, waits=bw + [(g_sm.h, g_sm.n), (s_s.h, s_after_pw)])
        tcnt = s_t.n
        for t in range(NT):
            bi = bks[t][0]
            op('v', lambda e, b=banks[bi],
               d=u_scan.ap()[0:64, TT * t:TT * (t + 1)]:
               e.tensor_scalar(d, b.ap()[0:64, :], prm.ap()[0:64, 3:4], None,
                               ALU.mult), sm=s_v, waits=[(s_t.h, tcnt)])
            op('v', lambda e, b=banks[bi],
               d=u_rev[:, TT * t:TT * (t + 1)]:
               e.tensor_scalar(d, b.ap()[64:128, :], prm.ap()[64:128, 3:4],
                               None, ALU.mult), sm=s_v)
            BK.set(bi, s_v, s_v.n)
        op('v', lambda e: e.tensor_scalar(abart[:], u_scan[:], 0.0,
                                          P_Abar, ALU.mult, ALU.add), sm=s_v)
        op('v', lambda e: e.tensor_tensor_scan(S_scan[:], abart[:], u_scan[:],
                                               0.0, ALU.mult, ALU.add), sm=s_v)
        op('s', lambda e: e.activation(S16.ap()[0:64, :], S_scan.ap()[0:64, :],
                                       AF.Copy), sm=s_s,
           waits=[(s_v.h, s_v.n)])
        s_after_s16f = s_s.n
        op('v', lambda e: e.tensor_copy(S16.ap()[64:128, :],
                                        S_scan.ap()[64:128, ::-1]), sm=s_v)
        v_after_scan = s_v.n
        dbg("uscan", u_scan[:], [128, L], F32, [(s_v.h, s_v.n)])
        dbg("sscan", S_scan[:], [128, L], F32, [(s_v.h, s_v.n)])
        dbg("prm", prm[:], [128, 12], F32, [(s_v.h, s_v.n)])

        mem.free("xconv16", "S_scan", "u_scan", "abart")
        h1buf = mem.alloc("h1buf", [128, NCH * TT], F16)
        y16all = mem.alloc("y16all", [128, NCH * L], F16)
        ybuf = mem.alloc("ybuf", [128, 4 * TT], F16)
        W34LD = [(g_w34.h, g_w34.n)]
        h1ev = {}
        h1grp0 = {}
        for o in range(NCH):
            bi, bw = BK.take()
            items = [(banks[bi].ap()[:, :],
                      w1_16.ap()[:, D * k + 128 * o:D * k + 128 * (o + 1)],
                      x16s_[0].ap()[:, TT * k:TT * (k + 1)],
                      k == 0, False) for k in range(NCH)]
            items.append((banks[bi].ap()[:, :],
                          colsum116.ap()[0:1, 128 * o:128 * (o + 1)],
                          negmu.ap()[0:1, 0:TT], False, True))
            t_group(items, waits=bw + [(g_x0.h, g_x0.n),
                                       (s_s.h, s_after_c1)])
            tcnt = s_t.n
            h1grp0[(0, o)] = tcnt
            op('v', lambda e, b=banks[bi], o=o,
               d=h1buf.ap()[:, TT * o:TT * (o + 1)],
               s1=s1b.ap()[:, 0:TT]:
               e.tensor_tensor(d, b.ap()[:, :], s1, ALU.mult),
               sm=s_v, waits=[(s_t.h, tcnt), (s_s.h, s_after_s1),
                              (s_s.h, s_after_s16f)])
            h1ev[(0, o)] = s_v.n
            BK.set(bi, s_v, s_v.n)

        jy = 0
        yslot_guard = {}
        for c in range(NCH):
            bks = [BK.take() for _ in range(NT)]
            bw = [w for _, ws in bks for w in ws]
            items = []
            for t in range(NT):
                items.append((banks[bks[t][0]].ap()[:, :],
                              wout16.ap()[:, 128 * c:128 * (c + 1)],
                              S16.ap()[:, TT * t:TT * (t + 1)], True, True))
            t_group(items, waits=bw + [(s_v.h, v_after_scan),
                                       (s_s.h, s_after_s16f)])
            tcnt = s_t.n
            for t in range(NT):
                bi = bks[t][0]
                dsl = y16all.ap()[:, L * c + TT * t:L * c + TT * (t + 1)]
                gsl = gate16.ap()[:, L * c + TT * t:L * c + TT * (t + 1)]
                if t % 2 == 0:
                    op('v', lambda e, b=banks[bi], d=dsl, g=gsl:
                       e.tensor_tensor(d, b.ap()[:, :], g, ALU.mult),
                       sm=s_v, waits=[(s_t.h, tcnt), (s_s.h, s_after_gate)])
                    BK.set(bi, s_v, s_v.n)
                else:
                    ysl = ybuf.ap()[:, TT * (jy % 4):TT * (jy % 4 + 1)]
                    sw = [(s_t.h, tcnt)]
                    if jy >= 4:
                        sw.append((s_g.h, yslot_guard[jy - 4]))
                    op('s', lambda e, b=banks[bi], d=ysl:
                       e.activation(d, b.ap()[:, :], AF.Copy),
                       sm=s_s, waits=sw)
                    BK.set(bi, s_s, s_s.n)
                    op('g', lambda e, d=dsl, y=ysl, g=gsl:
                       e.tensor_tensor(d, y, g, ALU.mult),
                       sm=s_g, waits=[(s_s.h, s_s.n), (s_s.h, s_after_gate)])
                    yslot_guard[jy] = s_g.n
                    jy += 1
        mem.free("gate16", "S16", "ybuf")
        g_after_y16 = s_g.n
        v_after_y16 = s_v.n
        dbg("y16all", y16all[:], [128, NCH * L], F16,
            [(s_v.h, v_after_y16), (s_g.h, g_after_y16)])

        alpha_b = mem.alloc("alpha_b", [128, L], F32)
        beta16 = mem.alloc("beta16_r", [1, L], F16)
        h2all = mem.alloc("h2all", [128, NCH * L], F16)

        BAR = barrier()
        op('v', lambda e: e.tensor_scalar(alpha_b[:], s1b[:], 1.0, None,
                                          ALU.add), sm=s_v, waits=BAR)
        op('v', lambda e: e.tensor_tensor(beta16[0:1, :], negmu.ap()[0:1, :],
                                          s1b.ap()[0:1, :], ALU.mult), sm=s_v)
        v_after_prep = s_v.n
        dbg("colsum1", colsum116[:], [1, D], F16, [(s_v.h, v_after_prep)])
        dbg("beta", beta16[:], [1, L], F16, [(s_v.h, v_after_prep)])

        h2grp = {}
        h1grp = dict(h1grp0)
        for t in range(NT):
            gx = (g_x0, g_x1)[t % 2]
            if t >= 2:
                dma('g', x16s_[t % 2].ap().rearrange("p (c w) -> p c w", w=TT),
                    xTd[:, TT * t:TT * (t + 1)]
                    .rearrange("(c p) w -> p c w", p=128),
                    gx, waits=[(s_t.h, h1grp[(t - 2, NCH - 1)]
                                if (t - 2, NCH - 1) in h1grp
                                else h2grp[(t - 2, NCH - 1)])])
            for o in (range(NCH) if t > 0 else ()):
                bi, bw = BK.take()
                items = [(banks[bi].ap()[:, :],
                          w1_16.ap()[:, D * k + 128 * o:D * k + 128 * (o + 1)],
                          x16s_[t % 2].ap()[:, TT * k:TT * (k + 1)],
                          k == 0, False) for k in range(NCH)]
                items.append((banks[bi].ap()[:, :],
                              colsum116.ap()[0:1, 128 * o:128 * (o + 1)],
                              negmu.ap()[0:1, TT * t:TT * (t + 1)],
                              False, True))
                t_group(items, waits=bw + [(gx.h, gx.n),
                                           (s_v.h, v_after_prep)])
                tcnt = s_t.n
                h1grp[(t, o)] = tcnt
                vw = [(s_t.h, tcnt), (s_t.h, h2grp[(t - 1, NCH - 1)])]
                op('v', lambda e, b=banks[bi], o=o, t=t,
                   d=h1buf.ap()[:, TT * o:TT * (o + 1)],
                   s1=s1b.ap()[:, TT * t:TT * (t + 1)]:
                   e.tensor_tensor(d, b.ap()[:, :], s1, ALU.mult),
                   sm=s_v, waits=vw)
                h1ev[(t, o)] = s_v.n
                BK.set(bi, s_v, s_v.n)
            for o in range(NCH):
                bi, bw = BK.take()
                items = [(banks[bi].ap()[:, :],
                          w2_16.ap()[:, D * k + 128 * o:D * k + 128 * (o + 1)],
                          h1buf.ap()[:, TT * k:TT * (k + 1)],
                          k == 0, k == NCH - 1) for k in range(NCH)]
                t_group(items, waits=bw + [(s_v.h, h1ev[(t, NCH - 1)])])
                h2grp[(t, o)] = s_t.n
                op('s', lambda e, b=banks[bi], o=o, t=t,
                   d=h2all.ap()[:, L * o + TT * t:L * o + TT * (t + 1)]:
                   e.activation(d, b.ap()[:, :], AF.Copy),
                   sm=s_s, waits=[(s_t.h, h2grp[(t, o)])])
                BK.set(bi, s_s, s_s.n)
        s_after_h2 = s_s.n
        dbg("h2all", h2all[:], [128, NCH * L], F16, [(s_s.h, s_after_h2)])

        mem.free("x16slot0", "x16slot1", "h1buf", "w1_16", "w2_16",
                 "negmu_r", "woutT2f_s")
        x32s_ = [mem.alloc(f"x32slot{i}", [128, L], F32) for i in range(3)]
        ftmp = mem.alloc("ftmp", [128, 4 * L], F32)
        vtmp = mem.alloc("vtmp", [128, 4 * TT], F32)
        outstage = mem.alloc("outstage", [128, 4 * TT], F32)

        BAR2 = barrier()
        st_q = (q_st0, q_st1, q_st2, q_st3)
        st_g = (None, g_st1, None, g_st3)
        slot_guard = {}
        j = 0
        fin_done = {}
        for o in range(NCH):
            qx = (q_x0, q_x1, q_x2)[o % 3]
            if o < 3:
                w = list(BAR2)
            else:
                fs, fv = fin_done[(o - 3, NT - 1)]
                w = [(fs.h, fv)]
            dma('q', x32s_[o % 3][:], xTd[128 * o:128 * (o + 1), :], qx,
                waits=w)
            bks = [BK.take() for _ in range(NT)]
            bw = [w2 for _, ws in bks for w2 in ws]
            items = []
            for k in range(NCH):
                for t in range(NT):
                    items.append((banks[bks[t][0]].ap()[:, :],
                                  w3_16.ap()[:, D * k + 128 * o:D * k + 128 * (o + 1)],
                                  h2all.ap()[:, L * k + TT * t:L * k + TT * (t + 1)],
                                  k == 0, k == NCH - 1))
            t_group(items, waits=bw + W34LD + [(s_s.h, s_after_h2)])
            h3cnt = s_t.n
            fsl_o = ftmp.ap()[:, L * (o % 4):L * (o % 4 + 1)]
            sw = [(s_t.h, h3cnt)]
            if o >= 4:
                fs, fv = fin_done[(o - 4, NT - 1)]
                sw.append((fs.h, fv))
            for t in range(NT):
                bi = bks[t][0]
                op('s', lambda e, b=banks[bi], o=o,
                   d=fsl_o[:, TT * t:TT * (t + 1)]:
                   e.activation(d, b.ap()[:, :], AF.Tanh, bias=fb_col(o)),
                   sm=s_s, waits=sw)
                sw = []
                BK.set(bi, s_s, s_s.n)
            s_tanh_o = s_s.n
            pre_done = {}
            for t in range(NT):
                jj = o * NT + t
                vsl = vtmp.ap()[:, TT * (jj % 4):TT * (jj % 4 + 1)]
                peng, psem = ('v', s_v) if (t % 2 == 0 or o == NCH - 1) \
                    else ('g', s_g)
                pw_ = [(s_s.h, s_tanh_o), (qx.h, qx.n)]
                if jj >= 4:
                    fs2, fv2 = fin_done[divmod(jj - 4, NT)]
                    pw_.append((fs2.h, fv2))
                op(peng, lambda e, v2=vsl, o=o, t=t,
                   x=x32s_[o % 3].ap()[:, TT * t:TT * (t + 1)],
                   a=alpha_b.ap()[:, TT * t:TT * (t + 1)]:
                   e.tensor_tensor(v2, x, a, ALU.mult), sm=psem, waits=pw_)
                op(peng, lambda e, v2=vsl, f=fsl_o[:, TT * t:TT * (t + 1)]:
                   e.tensor_tensor(v2, v2, f, ALU.add), sm=psem,
                   waits=[(psem.h, psem.n)])
                pre_done[t] = (psem, psem.n)
            bks2 = [BK.take() for _ in range(NT)]
            bw2 = [w2 for _, ws in bks2 for w2 in ws]
            items = []
            for k in range(NCH):
                for t in range(NT):
                    items.append((banks[bks2[t][0]].ap()[:, :],
                                  outw16.ap()[:, D * k + 128 * o:D * k + 128 * (o + 1)],
                                  y16all.ap()[:, L * k + TT * t:L * k + TT * (t + 1)],
                                  k == 0, False))
            for t in range(NT):
                items.append((banks[bks2[t][0]].ap()[:, :], onesr16b[0:1, :],
                              beta16.ap()[0:1, TT * t:TT * (t + 1)],
                              False, True))
            t_group(items, waits=bw2 + [(s_v.h, v_after_y16),
                                        (s_g.h, g_after_y16)])
            opcnt = s_t.n
            for t in range(NT):
                bi2 = bks2[t][0]
                slot = j % 4
                osl = outstage.ap()[:, TT * slot:TT * (slot + 1)]
                ring = 'q' if (t % 2 == 0 or o == NCH - 1) else 'g'
                stsem = st_q[slot] if ring == 'q' else st_g[slot]
                sg = [slot_guard[slot]] if slot in slot_guard else []
                op('s', lambda e, b=banks[bi2], o=o, d=osl:
                   e.add(d, b.ap()[:, :], outb_col(o)),
                   sm=s_s, waits=[(s_t.h, opcnt)] + sg)
                BK.set(bi2, s_s, s_s.n)
                feng, fsem = ('g', s_g) \
                    if (t % 2 == 0 and o < NCH - 1) else ('v', s_v)
                fw = [(s_s.h, s_s.n), (pre_done[t][0].h, pre_done[t][1])]
                if feng == 'g':
                    fw.append((s_g.h, s_g.n))
                op(feng, lambda e, d=osl,
                   v2=vtmp.ap()[:, TT * (j % 4):TT * (j % 4 + 1)]:
                   e.tensor_tensor(d, d, v2, ALU.add), sm=fsem, waits=fw)
                fin_done[(o, t)] = (fsem, fsem.n)
                dma(ring, outd[t, o], osl, stsem,
                    waits=[(fin_done[(o, t)][0].h, fin_done[(o, t)][1])])
                slot_guard[slot] = (stsem.h, stsem.n)
                j += 1

        for eng in 'gtvs':
            emit(eng, lambda e, dh=done.h: e.nop().then_inc(dh, 1))

        dma_finals = [(s.h, s.n) for s in
                      [q_sm, q_sm2, q_x0, q_x1, q_x2, q_x3,
                       q_st0, q_st1, q_st2, q_st3,
                       g_st1, g_st3,
                       g_wx, g_wx2, g_wz,
                       g_pw, g_w12, g_w34, g_sm, g_x0, g_x1, q_dbg, q_xtb]
                      + q_xt + [s_t, s_v, s_s, s_g]]

        def fin(e, a=q_st0.n, b=q_st1.n, fins=tuple(dma_finals), dh=done.h,
                bh=boot.h):
            e.wait_ge(q_st0.h, a)
            e.wait_ge(q_st1.h, b)
            e.wait_ge(dh, 4)
            for (sh, v) in fins:
                if v > 0:
                    e.wait_ge(sh, v)
            e.sem_clear(bh)
        emit('q', fin)

        blk = es.enter_context(nc.Block())

        @blk.sync
        def _(eng):
            for f in ops['q']:
                f(eng)

        @blk.gpsimd
        def _(eng):
            for f in ops['g']:
                f(eng)

        @blk.tensor
        def _(eng):
            for f in ops['t']:
                f(eng)

        @blk.vector
        def _(eng):
            for f in ops['v']:
                f(eng)

        @blk.scalar
        def _(eng):
            for f in ops['s']:
                f(eng)

    return nc


_CACHE = {}


def host_prep(inputs):
    inp = {k: np.asarray(v) for k, v in inputs.items()}
    f32 = np.float32
    f16 = np.float16

    T = np.ascontiguousarray(inp["L0_attn_T"].astype(f32, copy=False))
    in_wT = np.ascontiguousarray(inp["L0_attn_in_w"].astype(f32).T)
    pw_wT = np.ascontiguousarray(inp["L0_attn_pw_w"].astype(f32).T)
    out_wT = np.ascontiguousarray(inp["L0_attn_out_w"].astype(f32).T)
    w1 = np.ascontiguousarray(inp["L0_ffn_weight"].astype(f32))
    w2 = np.ascontiguousarray(inp["L0_ffn_aux_weight1"].astype(f32))
    w3 = np.ascontiguousarray(inp["L0_ffn_aux_weight2"].astype(f32))
    winT2 = np.ascontiguousarray(
        np.concatenate([inp["L0_attn_f_win"], inp["L0_attn_b_win"]], axis=0)
        .astype(f32).T)
    woutT2 = np.ascontiguousarray(
        np.concatenate([inp["L0_attn_f_wout"].astype(f32).T,
                        inp["L0_attn_b_wout"].astype(f32).T], axis=0))
    ssm = np.stack([
        np.concatenate([inp["L0_attn_f_logA"], inp["L0_attn_b_logA"]]),
        np.concatenate([inp["L0_attn_f_B"], inp["L0_attn_b_B"]]),
        np.concatenate([inp["L0_attn_f_C"], inp["L0_attn_b_C"]]),
        np.concatenate([np.broadcast_to(inp["L0_attn_f_logdt"], (64,)),
                        np.broadcast_to(inp["L0_attn_b_logdt"], (64,))]),
    ], axis=1).astype(f32)
    ssm = np.ascontiguousarray(ssm)

    def cols8(v):
        return np.ascontiguousarray(v.astype(f32).reshape(8, 128).T)

    vec = np.concatenate([
        cols8(inp["L0_attn_norm_scale"]),
        cols8(inp["L0_attn_in_b"][:1024]),
        cols8(inp["L0_attn_in_b"][1024:]),
        cols8(inp["L0_attn_dw_b"]),
        cols8(inp["L0_attn_pw_b"]),
        cols8(inp["L0_ffn_bias"]),
        cols8(inp["L0_attn_out_b"]),
    ], axis=1)
    dww = (inp["L0_attn_dw_w"].astype(f32).reshape(8, 128, 3)
           .transpose(1, 2, 0).reshape(128, 24))
    pack = np.ascontiguousarray(np.concatenate([ssm, vec, dww], axis=1))

    shared = dict(
        in_wT=in_wT, pw_wT=pw_wT, out_wT=out_wT, w1=w1, w2=w2, w3=w3,
        winT2=winT2, woutT2=woutT2, pack=pack,
        onesr32=np.ones((128, 128), f32))

    return shared, T


def kernel(**inputs):
    shared, T = host_prep(inputs)

    if "nc" not in _CACHE:
        _CACHE["nc"] = build_program()
    nc = _CACHE["nc"]

    in_maps = []
    for b in range(8):
        m = dict(shared)
        m["xT"] = np.ascontiguousarray(T[b].T)
        in_maps.append(m)

    res = run_bass_kernel_spmd(nc, in_maps, core_ids=list(range(8)))

    out = np.empty((8, L, D), np.float32)
    for b in range(8):
        a = res.results[b]["outT"]
        out[b] = a.transpose(0, 3, 1, 2).reshape(L, D)
    return out


if __name__ == "__main__":
    pass


# version 34
# speedup vs baseline: 1.2002x; 1.0069x over previous
import contextlib
import numpy as np
import concourse.bass as bass
import concourse.mybir as mybir
from concourse.bass_utils import run_bass_kernel_spmd

F32 = mybir.dt.float32
F32R = mybir.dt.float32r
F16 = mybir.dt.float16
U8 = mybir.dt.uint8
AF = mybir.ActivationFunctionType
ALU = mybir.AluOpType

L = 2048
D = 1024
NCH = 8
TT = 512
NT = 4
EPS_LN = 1e-5
EPS_RMS = 1e-6
EPS_SSM = 1e-6


class Sem:
    def __init__(self, h):
        self.h = h
        self.n = 0


class Mem:

    LO = 17408
    HI = 229344

    def __init__(self, nc):
        self.nc = nc
        self.free_list = [(self.LO, self.HI)]
        self.live = {}
        self.peak = 0

    def alloc(self, name, shape, dtype):
        nbytes = 1
        for s in shape[1:]:
            nbytes *= s
        nbytes *= {F32: 4, F32R: 4, F16: 2, U8: 1}[dtype]
        nbytes = (nbytes + 63) & ~63
        fits = [(b - a, idx) for idx, (a, b) in enumerate(self.free_list)
                if b - a >= nbytes]
        fits.sort()
        for _, idx in fits[:1]:
            a, b = self.free_list[idx]
            if True:
                self.free_list[idx] = (a + nbytes, b)
                if self.free_list[idx][0] == self.free_list[idx][1]:
                    self.free_list.pop(idx)
                h = self.nc.alloc_sbuf_tensor_at(name, list(shape), dtype,
                                                 offset=a)
                self.live[name] = (a, nbytes)
                used = self.HI - self.LO - sum(e - s for s, e in self.free_list)
                self.peak = max(self.peak, used)
                return h
        raise RuntimeError(f"manual SBUF OOM at {name} ({nbytes}B); "
                           f"free={self.free_list}")

    def free(self, *names):
        for n in names:
            a, nbytes = self.live.pop(n)
            self.free_list.append((a, a + nbytes))
        self.free_list.sort()
        out = []
        for s, e in self.free_list:
            if out and out[-1][1] >= s:
                out[-1] = (out[-1][0], max(out[-1][1], e))
            else:
                out.append((s, e))
        self.free_list = [tuple(x) for x in out]


def build_program(debug=False):
    nc = bass.Bass("TRN2", target_bir_lowering=False)

    xTd = nc.dram_tensor("xT", [D, L], F32, kind="ExternalInput").ap()
    inwTd = nc.dram_tensor("in_wT", [D, 2 * D], F32, kind="ExternalInput").ap()
    pwTd = nc.dram_tensor("pw_wT", [D, D], F32, kind="ExternalInput").ap()
    outwTd = nc.dram_tensor("out_wT", [D, D], F32, kind="ExternalInput").ap()
    w1d = nc.dram_tensor("w1", [D, D], F32, kind="ExternalInput").ap()
    w2d = nc.dram_tensor("w2", [D, D], F32, kind="ExternalInput").ap()
    w3d = nc.dram_tensor("w3", [D, D], F32, kind="ExternalInput").ap()
    winTd = nc.dram_tensor("winT2", [D, 128], F32, kind="ExternalInput").ap()
    woutd = nc.dram_tensor("woutT2", [128, D], F32, kind="ExternalInput").ap()
    packd = nc.dram_tensor("pack", [128, 84], F32, kind="ExternalInput").ap()
    onesr32d = nc.dram_tensor("onesr32", [128, 128], F32R,
                              kind="ExternalInput").ap()
    outd = nc.dram_tensor("outT", [NT, NCH, 128, TT], F32,
                          kind="ExternalOutput").ap()

    es = contextlib.ExitStack()
    with es:
        mem = Mem(nc)

        banks = [es.enter_context(nc.psum_tensor(f"pb{i}", [128, TT], F32))
                 for i in range(8)]

        def sem(name):
            return Sem(es.enter_context(nc.semaphore(name)))

        boot = sem("boot")
        done = sem("done")
        s_t = sem("s_t")
        s_v = sem("s_v")
        s_s = sem("s_s")
        s_g = sem("s_g")
        q_sm = sem("q_sm")
        q_sm2 = sem("q_sm2")
        q_x0 = sem("q_x0")
        q_x1 = sem("q_x1")
        q_x2 = sem("q_x2")
        q_x3 = sem("q_x3")
        q_st0 = sem("q_st0")
        q_st1 = sem("q_st1")
        q_st2 = sem("q_st2")
        q_st3 = sem("q_st3")
        g_st1 = sem("g_st1")
        g_st3 = sem("g_st3")
        q_xt = [sem(f"q_xt{i}") for i in range(4)]
        q_xtb = sem("q_xtb")
        g_wx = sem("g_wx")
        g_wx2 = sem("g_wx2")
        g_wz = sem("g_wz")
        g_pw = sem("g_pw")
        g_w12 = sem("g_w12")
        g_w34 = sem("g_w34")
        g_sm = sem("g_sm")
        g_x0 = sem("g_x0")
        g_x1 = sem("g_x1")
        q_dbg = sem("q_dbg")
        dbg_tensors = {}
        ALL_SEMS = [boot, done, s_t, s_v, s_s, s_g, q_sm, q_sm2, q_x0, q_x1,
                    q_x2, q_x3, q_st0,
                    q_st1, q_st2, q_st3, g_st1, g_st3,
                    g_wx, g_wx2, g_wz, g_pw, g_w12, g_w34, g_sm,
                    g_x0, g_x1, q_dbg, q_xtb] + q_xt
        BOOT = 1 << 16

        ops = {k: [] for k in "qgtvs"}
        last_wait = {k: {} for k in "qgtvs"}

        def emit(eng, fn):
            ops[eng].append(fn)

        def _mkwaits(eng, waits):
            out = []
            for (h, v) in waits:
                if v <= 0:
                    continue
                if last_wait[eng].get(h, -1) >= v:
                    continue
                last_wait[eng][h] = v
                out.append((h, v))
            return tuple(out)

        def op(eng, build_fn, sm=None, inc=1, waits=(), drain=False):
            inline_w = None
            if eng == 'v' and s_v.n > 0:
                inline_w = (s_v.h, s_v.n)
            elif eng == 's' and s_s.n > 0:
                inline_w = (s_s.h, s_s.n)
            ws = _mkwaits(eng, waits)
            if inline_w is None and ws:
                inline_w, ws = ws[0], ws[1:]
            if sm is not None:
                sm.n += inc
            smh = sm.h if sm is not None else None

            def f(e, bf=build_fn, ws=ws, iw=inline_w, smh=smh, k=inc, dr=drain):
                for (h, v) in ws:
                    e.wait_ge(h, v)
                ins = bf(e)
                if iw is not None:
                    ins._wait_ge(iw[0], iw[1])
                if smh is not None:
                    ins.then_inc(smh, k)
                if dr:
                    e.drain()
            emit(eng, f)
            return sm.n if sm is not None else None

        def dma(eng, dst, src, sm, waits=()):
            return op(eng, lambda e, d=dst, s=src: e.dma_start(out=d, in_=s),
                      sm=sm, inc=16, waits=waits)

        def barrier():
            w = [(s_t.h, s_t.n), (s_v.h, s_v.n), (s_s.h, s_s.n)]
            if s_g.n > 0:
                w.append((s_g.h, s_g.n))
            return w

        def dbg(name, src_ap, shape, dtype, waits):
            if not debug:
                return
            d = nc.dram_tensor("dbg_" + name, list(shape), dtype,
                               kind="ExternalOutput").ap()
            dbg_tensors[name] = d
            dma('q', d, src_ap, q_dbg, waits=waits)

        class Banks:
            def __init__(self):
                self.guard = [None] * 8
                self.i = 0

            def take(self):
                i = self.i % 8
                self.i += 1
                g = self.guard[i]
                self.guard[i] = None
                return i, ([g] if g else [])

            def set(self, i, semobj, val):
                self.guard[i] = (semobj.h, val)

        BK = Banks()

        def t_group(items, waits=()):
            norm = []
            for it in items:
                if len(it) == 5:
                    norm.append(((),) + tuple(it))
                else:
                    w = _mkwaits('t', it[0])
                    norm.append((w,) + tuple(it[1:]))
            ws = _mkwaits('t', waits)
            s_t.n += 1

            def f(e, it=tuple(norm), ws=ws, smh=s_t.h):
                used_w0 = not ws or (it and not it[0][0])
                for (h, v) in (ws[1:] if used_w0 else ws):
                    e.wait_ge(h, v)
                first = True
                ins = None
                for (iw, o, lh, r, st, sp) in it:
                    for (h, v) in iw[1:]:
                        e.wait_ge(h, v)
                    ins = e.matmul(o, lh, r, start=st, stop=sp)
                    if iw:
                        ins._wait_ge(iw[0][0], iw[0][1])
                    elif first and ws:
                        ins._wait_ge(ws[0][0], ws[0][1])
                    first = False
                ins.then_inc(smh, 1)
            emit('t', f)
            return s_t.n

        FIRST_SEMS = (q_xt[0], q_sm)
        def _boot_q1(e, sems=tuple(s.h for s in FIRST_SEMS)):
            for sh in sems:
                e.sem_clear(sh)
        emit('q', _boot_q1)

        def _boot_q2(e, sems=tuple(s.h for s in ALL_SEMS
                                   if s not in FIRST_SEMS),
                     bh=boot.h, big=BOOT):
            for sh in sems:
                e.sem_clear(sh)
            e.sem_inc(bh, big)

        def emit_boot2():
            emit('q', _boot_q2)
        for eng in 'gtvs':
            emit(eng, lambda e, bh=boot.h, big=BOOT: e.wait_ge(bh, big))

        class APView:
            def __init__(self, base_ap):
                self._ap = base_ap

            def ap(self):
                return self._ap

            def __getitem__(self, idx):
                return self._ap[idx]

        ones16 = mem.alloc("ones16_s", [128, 128], F16)
        onesr32 = mem.alloc("onesr32_s", [128, 128], F32R)
        onesr16 = mem.alloc("onesr16_s", [1, TT], F16)
        onesr16b = mem.alloc("onesr16b_s", [1, 128], F16)
        consts = mem.alloc("consts_s", [128, 2], F32)
        pack = mem.alloc("pack_s", [128, 84], F32)
        ssm = APView(pack.ap()[:, 0:4])
        vec = APView(pack.ap()[:, 4:60])
        dww = APView(pack.ap()[:, 60:84])
        prm = mem.alloc("prm_s", [128, 12], F32)
        cvb = mem.alloc("cvb_s", [128, 8], F32)
        e0n = mem.alloc("e0n_s", [128, 8], F32)
        e1n = mem.alloc("e1n_s", [128, 8], F32)
        winT16 = mem.alloc("winT16_s", [128, D], F16)
        wout16 = mem.alloc("wout16_s", [128, D], F16)
        woutT2f = mem.alloc("woutT2f_s", [128, D], F32)

        negmu = mem.alloc("negmu_r", [1, L], F16)
        nm2row = mem.alloc("nm2_r", [1, L], F16)
        colsum = mem.alloc("colsum_r", [1, 2 * D], F16)

        s1b = mem.alloc("s1b", [128, L], F32)
        s2b = mem.alloc("s2b", [128, L], F32)

        xTs = mem.alloc("xTs", [128, NCH * L], F16)
        x2buf = mem.alloc("x2buf", [128, 2 * L], F16)
        SxB = mem.alloc("SxB", [128, L], F32)
        Sx2B = mem.alloc("Sx2B", [128, L], F32)
        wxs = mem.alloc("wxs", [128, NCH * D], F16)
        wzs = mem.alloc("wzs", [128, NCH * D], F16)
        xt32 = mem.alloc("xt32", [128, NCH * L], F32R)

        def xT(c):
            return xTs.ap()[:, L * c:L * (c + 1)]

        def x32c(c):
            return xt32.ap()[:, L * c:L * (c + 1)]

        def wx(k):
            return wxs.ap()[:, D * k:D * (k + 1)]

        def wz(k):
            return wzs.ap()[:, D * k:D * (k + 1)]

        dma('q', xt32.ap()[:, 0:L], xTd.bitcast(F32R)[0:128, :], q_xt[0])
        dma('q', onesr32[:], onesr32d, q_sm)
        dma('q', pack[:], packd, q_sm)
        emit_boot2()
        dma('q', xt32.ap()[:, L:2 * L], xTd.bitcast(F32R)[128:256, :], q_xtb)
        op('v', lambda e: e.memset(ones16[:], 1.0), sm=s_v)
        op('v', lambda e: e.memset(onesr16[:], 1.0), sm=s_v, drain=True)
        op('v', lambda e: e.memset(onesr16b[:], 1.0), sm=s_v, drain=True)
        op('v', lambda e: e.memset(consts.ap()[:, 0:1], EPS_LN), sm=s_v,
           drain=True)
        op('v', lambda e: e.memset(consts.ap()[:, 1:2], EPS_LN * EPS_RMS),
           sm=s_v, drain=True)
        v_after_consts = s_v.n
        for i in range(1, 4):
            dma('q', xt32.ap()[:, 2 * L * i:2 * L * (i + 1)]
                .rearrange("p (c w) -> p c w", w=L),
                xTd.bitcast(F32R)[256 * i:256 * (i + 1), :]
                .rearrange("(c p) w -> p c w", p=128), q_xt[i])
        dma('q', woutT2f[:], woutd, q_sm2)
        dma('g', wxs.ap()[:, 0:4 * D].rearrange("p (k o) -> p k o", o=D),
            inwTd[0:512, 0:D].rearrange("(k p) o -> p k o", p=128), g_wx)
        dma('g', wxs.ap()[:, 4 * D:].rearrange("p (k o) -> p k o", o=D),
            inwTd[512:1024, 0:D].rearrange("(k p) o -> p k o", p=128), g_wx2)
        dma('g', wzs.ap().rearrange("p (k o) -> p k o", o=D),
            inwTd[:, D:2 * D].rearrange("(k p) o -> p k o", p=128), g_wz)
        dma('g', winT16.ap().rearrange("p (k o) -> p k o", o=128),
            winTd.rearrange("(k p) o -> p k o", p=128), g_sm)

        ns_col = lambda k: vec.ap()[:, k:k + 1]
        bz_col = lambda o: vec.ap()[:, 16 + o:17 + o]
        pwb_col = lambda e: vec.ap()[:, 32 + e:33 + e]
        fb_col = lambda o: vec.ap()[:, 40 + o:41 + o]
        outb_col = lambda o: vec.ap()[:, 48 + o:49 + o]
        c_ln = consts.ap()[:, 0:1]
        c_ee = consts.ap()[:, 1:2]

        SMALL_LD = [(q_sm.h, q_sm.n)]
        XTLD = [[(q_xt[c // 2].h, q_xt[c // 2].n)] for c in range(NCH)]
        XTLD[1] = [(q_xtb.h, 16)]

        ns16 = mem.alloc("ns16_s", [128, 8], F16)

        P_dt = prm.ap()[:, 0:1]
        P_A = prm.ap()[:, 1:2]
        P_Abar = prm.ap()[:, 2:3]
        P_Bbar = prm.ap()[:, 3:4]
        P_eA = prm.ap()[:, 4:5]
        P_t0 = prm.ap()[:, 5:6]
        P_t1 = prm.ap()[:, 6:7]
        P_t2 = prm.ap()[:, 7:8]
        P_m = prm.ap()[:, 8:9]
        w0s = dww.ap()[:, 0:8]
        w1s = dww.ap()[:, 8:16]
        w2s = dww.ap()[:, 16:24]
        op('v', lambda e: e.tensor_tensor(cvb[:], w0s, w1s, ALU.add), sm=s_v,
           waits=SMALL_LD, drain=True)
        op('v', lambda e: e.tensor_tensor(cvb[:], cvb[:], w2s, ALU.add),
           sm=s_v, drain=True)
        op('v', lambda e: e.tensor_tensor(cvb[:], cvb[:], vec.ap()[:, 8:16],
                                          ALU.mult), sm=s_v, drain=True)
        op('v', lambda e: e.tensor_tensor(cvb[:], cvb[:], vec.ap()[:, 24:32],
                                          ALU.add), sm=s_v, drain=True)
        op('v', lambda e: e.tensor_tensor(e0n[:], vec.ap()[:, 8:16], w0s,
                                          ALU.mult), sm=s_v, drain=True)
        op('v', lambda e: e.tensor_scalar(e0n[:], e0n[:], -1.0, None, ALU.mult),
           sm=s_v, drain=True)
        op('v', lambda e: e.tensor_tensor(e1n[:], vec.ap()[:, 8:16], w2s,
                                          ALU.mult), sm=s_v, drain=True)
        op('v', lambda e: e.tensor_scalar(e1n[:], e1n[:], -1.0, None, ALU.mult),
           sm=s_v, drain=True)
        op('v', lambda e: e.tensor_copy(ns16[:], vec.ap()[:, 0:8]), sm=s_v,
           drain=True)
        v_after_ns16 = s_v.n

        sq_done = []
        statgrp_after = []
        for c in range(NCH):
            w = list(XTLD[c])
            if c >= 2:
                w.append((s_t.h, statgrp_after[c - 2]))
            x2slot = x2buf.ap()[:, L * (c % 2):L * (c % 2 + 1)]
            if c % 2 == 0:
                op('s', lambda e, xc=x32c(c), d=x2slot:
                   e.activation(d, xc, AF.Square), sm=s_s, waits=w)
                sq_done.append((s_s, s_s.n))
            else:
                op('v', lambda e, xc=x32c(c), d=x2slot:
                   e.tensor_tensor(d, xc, xc, ALU.mult), sm=s_v, waits=w)
                sq_done.append((s_v, s_v.n))
            items = []
            for tt in range(NT):
                items.append((banks[tt].ap()[:, :], onesr32[:],
                              x32c(c)[:, TT * tt:TT * (tt + 1)],
                              c == 0, c == NCH - 1))
            for tt in range(NT):
                items.append((banks[4 + tt].ap()[:, :], ones16[:],
                              x2slot[:, TT * tt:TT * (tt + 1)],
                              c == 0, c == NCH - 1))
            t_group(items, waits=XTLD[c] + [(sq_done[c][0].h, sq_done[c][1]),
                                            (s_v.h, v_after_consts)]
                    + (SMALL_LD if c == 0 else []))
            statgrp_after.append(s_t.n)
        t_after_stats = s_t.n

        op('s', lambda e: e.activation(P_dt, ssm.ap()[:, 3:4], AF.Exp),
           sm=s_s, waits=SMALL_LD)
        op('s', lambda e: e.activation(P_eA, ssm.ap()[:, 0:1], AF.Exp), sm=s_s)
        s_after_pexp = s_s.n
        op('v', lambda e: e.tensor_scalar(P_A, P_eA, -1.0, None, ALU.mult),
           sm=s_v, waits=[(s_s.h, s_after_pexp)], drain=True)
        op('v', lambda e: e.tensor_tensor(P_t0, P_A, P_dt, ALU.mult), sm=s_v,
           drain=True)
        v_after_pt0 = s_v.n
        op('v', lambda e: e.tensor_single_scalar(P_m, P_eA, EPS_SSM,
                                                 ALU.is_lt), sm=s_v, drain=True)
        op('v', lambda e: e.tensor_scalar(P_t1, P_A, -1.0, 1.0, ALU.mult,
                                          ALU.add), sm=s_v, drain=True)
        op('v', lambda e: e.tensor_tensor(P_t1, P_m, P_t1, ALU.mult), sm=s_v,
           drain=True)
        op('v', lambda e: e.tensor_tensor(P_t1, P_A, P_t1, ALU.add), sm=s_v,
           drain=True)
        op('v', lambda e: e.reciprocal(P_t2, P_t1), sm=s_v, drain=True)

        for tt in range(NT):
            op('s', lambda e, b=banks[tt], d=SxB.ap()[:, TT * tt:TT * (tt + 1)]:
               e.activation(d, b.ap()[:, :], AF.Copy),
               sm=s_s, waits=[(s_t.h, t_after_stats)])
        for tt in range(NT):
            op('s', lambda e, b=banks[4 + tt],
               d=Sx2B.ap()[:, TT * tt:TT * (tt + 1)]:
               e.activation(d, b.ap()[:, :], AF.Copy), sm=s_s)
        s_after_statevac = s_s.n
        for i in range(8):
            BK.set(i, s_s, s_after_statevac)
        BK.i = 8

        SEV = [(s_s.h, s_after_statevac)]
        op('v', lambda e: e.tensor_scalar(negmu[0:1, :], SxB.ap()[0:1, :],
                                          -1.0 / D, None, ALU.mult),
           sm=s_v, waits=SEV)
        op('s', lambda e: e.activation(SxB[:], SxB[:], AF.Square, scale=1.0 / D),
           sm=s_s, waits=[(s_v.h, s_v.n)])
        op('v', lambda e: e.tensor_scalar(Sx2B[:], Sx2B[:], 1.0 / D, None,
                                          ALU.mult), sm=s_v)
        op('v', lambda e: e.tensor_tensor(Sx2B[:], Sx2B[:], SxB[:], ALU.subtract),
           sm=s_v, waits=[(s_s.h, s_s.n)])
        v_after_var = s_v.n
        op('s', lambda e: e.activation(SxB[:], Sx2B[:], AF.Ln, bias=c_ee,
                                       scale=1.0 + EPS_RMS),
           sm=s_s, waits=[(s_v.h, v_after_var)])
        op('s', lambda e: e.activation(s2b[:], SxB[:], AF.Exp, scale=-0.5),
           sm=s_s)
        s_after_s2 = s_s.n
        ts16_done = []
        for c in range(NCH):
            eng, sm_ = ('v', s_v)
            op(eng, lambda e, c=c: e.scalar_tensor_tensor(
                xT(c), x32c(c), pack.ap()[:, 4 + c:5 + c], s2b[:],
                ALU.mult, ALU.mult),
               sm=sm_, waits=[(s_s.h, s_after_s2), (s_t.h, t_after_stats)])
            ts16_done.append((sm_, sm_.n))
        op('v', lambda e: e.tensor_tensor(nm2row[0:1, :], negmu.ap()[0:1, :],
                                          s2b.ap()[0:1, :], ALU.mult), sm=s_v)
        v_after_nm2 = s_v.n
        op('s', lambda e: e.activation(P_Abar, P_t0, AF.Exp), sm=s_s,
           waits=[(s_v.h, v_after_pt0)])
        s_after_abar = s_s.n
        op('v', lambda e: e.tensor_scalar(P_t0, P_Abar, -1.0, None, ALU.add),
           sm=s_v, waits=[(s_s.h, s_after_abar)], drain=True)
        op('v', lambda e: e.tensor_tensor(P_t0, P_t0, P_t2, ALU.mult), sm=s_v,
           drain=True)
        op('v', lambda e: e.tensor_tensor(P_t1, P_dt, P_t0, ALU.subtract),
           sm=s_v, drain=True)
        op('v', lambda e: e.tensor_tensor(P_t1, P_m, P_t1, ALU.mult), sm=s_v,
           drain=True)
        op('v', lambda e: e.tensor_tensor(P_t0, P_t0, P_t1, ALU.add), sm=s_v,
           drain=True)
        op('v', lambda e: e.tensor_tensor(P_Bbar, P_t0, ssm.ap()[:, 1:2],
                                          ALU.mult), sm=s_v, drain=True)
        op('v', lambda e: e.tensor_scalar(wout16[:], woutT2f[:],
                                          ssm.ap()[:, 2:3], None, ALU.mult),
           sm=s_v, waits=[(q_sm2.h, 16)])
        op('s', lambda e: e.activation(s1b[:], Sx2B[:], AF.Ln, bias=c_ln),
           sm=s_s)
        op('s', lambda e: e.activation(s1b[:], s1b[:], AF.Exp, scale=-0.5),
           sm=s_s)
        s_after_s1 = s_s.n
        v_after_statmath = s_v.n
        dbg("s1b", s1b[:], [128, L], F32, [(s_s.h, s_s.n)])
        dbg("s2b", s2b[:], [128, L], F32, [(s_s.h, s_s.n)])
        dbg("negmu", negmu[:], [1, L], F16, [(s_v.h, s_v.n)])

        mem.free("x2buf", "SxB", "Sx2B", "xt32")
        x1buf = mem.alloc("x1buf", [128, 2 * L], F32)
        convbuf = mem.alloc("convbuf", [128, L], F32)
        xc1 = mem.alloc("xc1", [128, NCH * L], F16)
        gate16 = mem.alloc("gate16", [128, NCH * L], F16)
        pw16 = mem.alloc("pw16", [128, NCH * D], F16)
        dma('g', pw16.ap().rearrange("p (k o) -> p k o", o=D),
            pwTd.rearrange("(k p) o -> p k o", p=128), g_pw, waits=barrier())

        cs = []
        for half, wfun, gsem in ((0, wx, g_wx2), (1, wz, g_wz)):
            for j2 in range(2):
                bi, bwc = BK.take()
                items = [(banks[bi].ap()[0:1, :], ns16.ap()[:, k:k + 1],
                          wfun(k)[:, TT * j2:TT * (j2 + 1)],
                          k == 0, k == NCH - 1) for k in range(NCH)]
                t_group(items, waits=bwc + [(s_v.h, v_after_ns16),
                                            (g_wx.h, g_wx.n),
                                            (gsem.h, gsem.n)])
                cs.append((bi, half, j2, s_t.n))
        for (bi, half, j2, tcnt) in cs:
            dst = colsum.ap()[0:1, D * half + TT * j2:D * half + TT * (j2 + 1)]
            op('s', lambda e, b=banks[bi], d=dst:
               e.activation(d, b.ap()[0:1, :], AF.Copy), sm=s_s,
               waits=[(s_t.h, tcnt)])
            BK.set(bi, s_s, s_s.n)
        s_after_colsum = s_s.n

        conv_done = []
        half_pend = {}
        for o in (0, 1):
            bks = [BK.take() for _ in range(NT)]
            bw = [w for _, ws in bks for w in ws]
            items = []
            for k in range(4):
                for t in range(NT):
                    items.append((((ts16_done[k][0].h, ts16_done[k][1]),),
                                  banks[bks[t][0]].ap()[:, :],
                                  wx(k)[:, 128 * o:128 * (o + 1)],
                                  xT(k)[:, TT * t:TT * (t + 1)],
                                  k == 0, False))
            t_group(items, waits=bw + [(g_wx.h, g_wx.n)])
            half_pend[o] = bks
        for o in range(NCH):
            if o in half_pend:
                bks = half_pend[o]
                bw = []
                items = []
                for k in range(4, NCH):
                    for t in range(NT):
                        items.append((((ts16_done[k][0].h, ts16_done[k][1]),),
                                      banks[bks[t][0]].ap()[:, :],
                                      wx(k)[:, 128 * o:128 * (o + 1)],
                                      xT(k)[:, TT * t:TT * (t + 1)],
                                      False, False))
            else:
                bks = [BK.take() for _ in range(NT)]
                bw = [w for _, ws in bks for w in ws]
                items = []
                for k in range(NCH):
                    for t in range(NT):
                        items.append((((ts16_done[k][0].h, ts16_done[k][1]),),
                                      banks[bks[t][0]].ap()[:, :],
                                      wx(k)[:, 128 * o:128 * (o + 1)],
                                      xT(k)[:, TT * t:TT * (t + 1)],
                                      k == 0, False))
            for t in range(NT):
                items.append((((s_s.h, s_after_colsum),
                               (s_v.h, v_after_nm2)),
                              banks[bks[t][0]].ap()[:, :],
                              colsum.ap()[0:1, 128 * o:128 * (o + 1)],
                              nm2row.ap()[0:1, TT * t:TT * (t + 1)],
                              False, True))
            t_group(items, waits=bw + [(g_wx2.h, g_wx2.n)])
            tcnt = s_t.n
            sw2 = [(s_t.h, tcnt)]
            if len(conv_done) >= 2:
                sw2.append((s_v.h, conv_done[-2]))
            x1sl = x1buf.ap()[:, L * (o % 2):L * (o % 2 + 1)]
            for t in range(NT):
                bi = bks[t][0]
                op('s', lambda e, b=banks[bi],
                   d=x1sl[:, TT * t:TT * (t + 1)]:
                   e.copy(d, b.ap()[:, :]),
                   sm=s_s, waits=sw2)
                BK.set(bi, s_s, s_s.n)
            xcc = xc1.ap()[:, L * o:L * (o + 1)]
            x1sl = x1buf.ap()[:, L * (o % 2):L * (o % 2 + 1)]
            op('v', lambda e, o=o, x1=x1sl: e.tensor_scalar(
                convbuf[:], x1, dww.ap()[:, 8 + o:9 + o],
                cvb.ap()[:, o:o + 1], ALU.mult, ALU.add), sm=s_v,
               waits=[(s_s.h, s_s.n)])
            op('v', lambda e, o=o, x1=x1sl: e.scalar_tensor_tensor(
                convbuf.ap()[:, 1:L], x1[:, 0:L - 1],
                dww.ap()[:, o:o + 1], convbuf.ap()[:, 1:L],
                ALU.mult, ALU.add), sm=s_v)
            op('v', lambda e, o=o, x1=x1sl: e.scalar_tensor_tensor(
                convbuf.ap()[:, 0:L - 1], x1[:, 1:L],
                dww.ap()[:, 16 + o:17 + o], convbuf.ap()[:, 0:L - 1],
                ALU.mult, ALU.add), sm=s_v)
            op('v', lambda e, o=o: e.tensor_scalar(
                convbuf.ap()[:, 0:1], convbuf.ap()[:, 0:1],
                e0n.ap()[:, o:o + 1], None, ALU.add), sm=s_v, drain=True)
            op('v', lambda e, o=o: e.tensor_scalar(
                convbuf.ap()[:, L - 1:L], convbuf.ap()[:, L - 1:L],
                e1n.ap()[:, o:o + 1], None, ALU.add), sm=s_v, drain=True)
            op('v', lambda e, d=xcc: e.tensor_copy(d, convbuf[:]), sm=s_v)
            conv_done.append(s_v.n)
        t_after_inx = s_t.n
        for o in range(NCH):
            bks = [BK.take() for _ in range(NT)]
            bw = [w for _, ws in bks for w in ws]
            items = []
            for k in range(NCH):
                for t in range(NT):
                    items.append((banks[bks[t][0]].ap()[:, :],
                                  wz(k)[:, 128 * o:128 * (o + 1)],
                                  xT(k)[:, TT * t:TT * (t + 1)],
                                  k == 0, False))
            for t in range(NT):
                items.append((((s_s.h, s_after_colsum),),
                              banks[bks[t][0]].ap()[:, :],
                              colsum.ap()[0:1, D + 128 * o:D + 128 * (o + 1)],
                              nm2row.ap()[0:1, TT * t:TT * (t + 1)],
                              False, True))
            t_group(items, waits=bw)
            tcnt = s_t.n
            for t in range(NT):
                bi = bks[t][0]
                op('s', lambda e, b=banks[bi], o=o,
                   d=gate16.ap()[:, L * o + TT * t:L * o + TT * (t + 1)]:
                   e.activation(d, b.ap()[:, :], AF.Sigmoid, bias=bz_col(o)),
                   sm=s_s, waits=[(s_t.h, tcnt)])
                BK.set(bi, s_s, s_s.n)
        t_after_z = s_t.n
        s_after_gate = s_s.n
        mem.free("xTs", "wxs", "wzs", "x1buf", "convbuf", "s2b")
        xconv16 = mem.alloc("xconv16", [128, NCH * L], F16)
        w1_16 = mem.alloc("w1_16", [128, NCH * D], F16)
        w2_16 = mem.alloc("w2_16", [128, NCH * D], F16)
        dma('g', w1_16.ap().rearrange("p (k o) -> p k o", o=D),
            w1d.rearrange("(k p) o -> p k o", p=128), g_w12, waits=barrier())
        dma('g', w2_16.ap().rearrange("p (k o) -> p k o", o=D),
            w2d.rearrange("(k p) o -> p k o", p=128), g_w12)
        x16s_ = [mem.alloc(f"x16slot{i}", [128, NCH * TT], F16)
                 for i in range(2)]
        for t in (0, 1):
            dma('g', x16s_[t].ap().rearrange("p (c w) -> p c w", w=TT),
                xTd[:, TT * t:TT * (t + 1)].rearrange("(c p) w -> p c w", p=128),
                (g_x0, g_x1)[t], waits=barrier())

        PWLD = [(g_pw.h, g_pw.n)]
        for e_ in range(NCH):
            bks = [BK.take() for _ in range(NT)]
            bw = [w for _, ws in bks for w in ws]
            items = []
            for k in range(NCH):
                for t in range(NT):
                    items.append((banks[bks[t][0]].ap()[:, :],
                                  pw16.ap()[:, D * k + 128 * e_:D * k + 128 * (e_ + 1)],
                                  xc1.ap()[:, L * k + TT * t:L * k + TT * (t + 1)],
                                  k == 0, k == NCH - 1))
            t_group(items, waits=bw + PWLD + [(s_v.h, conv_done[NCH - 1])])
            tcnt = s_t.n
            for t in range(NT):
                bi = bks[t][0]
                op('s', lambda e, b=banks[bi], e_=e_,
                   d=xconv16.ap()[:, L * e_ + TT * t:L * e_ + TT * (t + 1)]:
                   e.add(d, b.ap()[:, :], pwb_col(e_)),
                   sm=s_s, waits=[(s_t.h, tcnt)])
                BK.set(bi, s_s, s_s.n)
        s_after_pw = s_s.n
        t_after_pw = s_t.n

        mem.free("pw16", "xc1")
        u_scan = mem.alloc("u_scan", [128, L], F32)
        abart = mem.alloc("abart", [128, L], F32)
        S_scan = mem.alloc("S_scan", [128, L], F32)
        S16 = mem.alloc("S16", [128, L], F16)
        w3_16 = mem.alloc("w3_16", [128, NCH * D], F16)
        outw16 = mem.alloc("outw16", [128, NCH * D], F16)
        dma('g', w3_16.ap().rearrange("p (k o) -> p k o", o=D),
            w3d.rearrange("(k p) o -> p k o", p=128), g_w34, waits=barrier())
        dma('g', outw16.ap().rearrange("p (k o) -> p k o", o=D),
            outwTd.rearrange("(k p) o -> p k o", p=128), g_w34)

        colsum116 = mem.alloc("colsum116_r", [1, D], F16)
        W12LD = [(g_w12.h, g_w12.n)]
        c1banks = []
        for j in range(2):
            bi, bw = BK.take()
            items = [(banks[bi].ap()[0:1, :], ones16.ap()[:, 0:1],
                      w1_16.ap()[:, D * k + TT * j:D * k + TT * (j + 1)],
                      k == 0, k == NCH - 1) for k in range(NCH)]
            t_group(items, waits=bw + W12LD)
            c1banks.append((bi, j, s_t.n))
        for (bi, j, tcnt) in c1banks:
            op('s', lambda e, b=banks[bi],
               d=colsum116.ap()[0:1, TT * j:TT * (j + 1)]:
               e.activation(d, b.ap()[0:1, :], AF.Copy), sm=s_s,
               waits=[(s_t.h, tcnt)])
            BK.set(bi, s_s, s_s.n)
        s_after_c1 = s_s.n
        u_rev = u_scan.ap()[64:128, ::-1]
        bks = [BK.take() for _ in range(NT)]
        bw = [w for _, ws in bks for w in ws]
        items = []
        for k in range(NCH):
            for t in range(NT):
                items.append((banks[bks[t][0]].ap()[:, :],
                              winT16.ap()[:, 128 * k:128 * (k + 1)],
                              xconv16.ap()[:, L * k + TT * t:L * k + TT * (t + 1)],
                              k == 0, k == NCH - 1))
        t_group(items, waits=bw + [(g_sm.h, g_sm.n), (s_s.h, s_after_pw)])
        tcnt = s_t.n
        for t in range(NT):
            bi = bks[t][0]
            op('v', lambda e, b=banks[bi],
               d=u_scan.ap()[0:64, TT * t:TT * (t + 1)]:
               e.tensor_scalar(d, b.ap()[0:64, :], prm.ap()[0:64, 3:4], None,
                               ALU.mult), sm=s_v, waits=[(s_t.h, tcnt)])
            op('v', lambda e, b=banks[bi],
               d=u_rev[:, TT * t:TT * (t + 1)]:
               e.tensor_scalar(d, b.ap()[64:128, :], prm.ap()[64:128, 3:4],
                               None, ALU.mult), sm=s_v)
            BK.set(bi, s_v, s_v.n)
        op('v', lambda e: e.tensor_scalar(abart[:], u_scan[:], 0.0,
                                          P_Abar, ALU.mult, ALU.add), sm=s_v)
        op('v', lambda e: e.tensor_tensor_scan(S_scan[:], abart[:], u_scan[:],
                                               0.0, ALU.mult, ALU.add), sm=s_v)
        op('s', lambda e: e.activation(S16.ap()[0:64, :], S_scan.ap()[0:64, :],
                                       AF.Copy), sm=s_s,
           waits=[(s_v.h, s_v.n)])
        s_after_s16f = s_s.n
        op('v', lambda e: e.tensor_copy(S16.ap()[64:128, :],
                                        S_scan.ap()[64:128, ::-1]), sm=s_v)
        v_after_scan = s_v.n
        dbg("uscan", u_scan[:], [128, L], F32, [(s_v.h, s_v.n)])
        dbg("sscan", S_scan[:], [128, L], F32, [(s_v.h, s_v.n)])
        dbg("prm", prm[:], [128, 12], F32, [(s_v.h, s_v.n)])

        mem.free("xconv16", "S_scan", "u_scan", "abart")
        h1buf = mem.alloc("h1buf", [128, NCH * TT], F16)
        y16all = mem.alloc("y16all", [128, NCH * L], F16)
        ybuf = mem.alloc("ybuf", [128, 4 * TT], F16)
        W34LD = [(g_w34.h, g_w34.n)]
        h1ev = {}
        h1grp0 = {}
        for o in range(NCH):
            bi, bw = BK.take()
            items = [(banks[bi].ap()[:, :],
                      w1_16.ap()[:, D * k + 128 * o:D * k + 128 * (o + 1)],
                      x16s_[0].ap()[:, TT * k:TT * (k + 1)],
                      k == 0, False) for k in range(NCH)]
            items.append((banks[bi].ap()[:, :],
                          colsum116.ap()[0:1, 128 * o:128 * (o + 1)],
                          negmu.ap()[0:1, 0:TT], False, True))
            t_group(items, waits=bw + [(g_x0.h, g_x0.n),
                                       (s_s.h, s_after_c1)])
            tcnt = s_t.n
            h1grp0[(0, o)] = tcnt
            op('v', lambda e, b=banks[bi], o=o,
               d=h1buf.ap()[:, TT * o:TT * (o + 1)],
               s1=s1b.ap()[:, 0:TT]:
               e.tensor_tensor(d, b.ap()[:, :], s1, ALU.mult),
               sm=s_v, waits=[(s_t.h, tcnt), (s_s.h, s_after_s1),
                              (s_s.h, s_after_s16f)])
            h1ev[(0, o)] = s_v.n
            BK.set(bi, s_v, s_v.n)

        jy = 0
        yslot_guard = {}
        for c in range(NCH):
            bks = [BK.take() for _ in range(NT)]
            bw = [w for _, ws in bks for w in ws]
            items = []
            for t in range(NT):
                items.append((banks[bks[t][0]].ap()[:, :],
                              wout16.ap()[:, 128 * c:128 * (c + 1)],
                              S16.ap()[:, TT * t:TT * (t + 1)], True, True))
            t_group(items, waits=bw + [(s_v.h, v_after_scan),
                                       (s_s.h, s_after_s16f)])
            tcnt = s_t.n
            for t in range(NT):
                bi = bks[t][0]
                dsl = y16all.ap()[:, L * c + TT * t:L * c + TT * (t + 1)]
                gsl = gate16.ap()[:, L * c + TT * t:L * c + TT * (t + 1)]
                if t % 2 == 0:
                    op('v', lambda e, b=banks[bi], d=dsl, g=gsl:
                       e.tensor_tensor(d, b.ap()[:, :], g, ALU.mult),
                       sm=s_v, waits=[(s_t.h, tcnt), (s_s.h, s_after_gate)])
                    BK.set(bi, s_v, s_v.n)
                else:
                    ysl = ybuf.ap()[:, TT * (jy % 4):TT * (jy % 4 + 1)]
                    sw = [(s_t.h, tcnt)]
                    if jy >= 4:
                        sw.append((s_g.h, yslot_guard[jy - 4]))
                    op('s', lambda e, b=banks[bi], d=ysl:
                       e.activation(d, b.ap()[:, :], AF.Copy),
                       sm=s_s, waits=sw)
                    BK.set(bi, s_s, s_s.n)
                    op('g', lambda e, d=dsl, y=ysl, g=gsl:
                       e.tensor_tensor(d, y, g, ALU.mult),
                       sm=s_g, waits=[(s_s.h, s_s.n), (s_s.h, s_after_gate)])
                    yslot_guard[jy] = s_g.n
                    jy += 1
        mem.free("gate16", "S16", "ybuf")
        g_after_y16 = s_g.n
        v_after_y16 = s_v.n
        dbg("y16all", y16all[:], [128, NCH * L], F16,
            [(s_v.h, v_after_y16), (s_g.h, g_after_y16)])

        alpha_b = mem.alloc("alpha_b", [128, L], F32)
        beta16 = mem.alloc("beta16_r", [1, L], F16)
        h2all = mem.alloc("h2all", [128, NCH * L], F16)

        BAR = barrier()
        op('v', lambda e: e.tensor_scalar(alpha_b[:], s1b[:], 1.0, None,
                                          ALU.add), sm=s_v, waits=BAR)
        op('v', lambda e: e.tensor_tensor(beta16[0:1, :], negmu.ap()[0:1, :],
                                          s1b.ap()[0:1, :], ALU.mult), sm=s_v)
        v_after_prep = s_v.n
        dbg("colsum1", colsum116[:], [1, D], F16, [(s_v.h, v_after_prep)])
        dbg("beta", beta16[:], [1, L], F16, [(s_v.h, v_after_prep)])

        h2grp = {}
        h1grp = dict(h1grp0)
        for t in range(NT):
            gx = (g_x0, g_x1)[t % 2]
            if t >= 2:
                dma('g', x16s_[t % 2].ap().rearrange("p (c w) -> p c w", w=TT),
                    xTd[:, TT * t:TT * (t + 1)]
                    .rearrange("(c p) w -> p c w", p=128),
                    gx, waits=[(s_t.h, h1grp[(t - 2, NCH - 1)]
                                if (t - 2, NCH - 1) in h1grp
                                else h2grp[(t - 2, NCH - 1)])])
            for o in (range(NCH) if t > 0 else ()):
                bi, bw = BK.take()
                items = [(banks[bi].ap()[:, :],
                          w1_16.ap()[:, D * k + 128 * o:D * k + 128 * (o + 1)],
                          x16s_[t % 2].ap()[:, TT * k:TT * (k + 1)],
                          k == 0, False) for k in range(NCH)]
                items.append((banks[bi].ap()[:, :],
                              colsum116.ap()[0:1, 128 * o:128 * (o + 1)],
                              negmu.ap()[0:1, TT * t:TT * (t + 1)],
                              False, True))
                t_group(items, waits=bw + [(gx.h, gx.n),
                                           (s_v.h, v_after_prep)])
                tcnt = s_t.n
                h1grp[(t, o)] = tcnt
                vw = [(s_t.h, tcnt), (s_t.h, h2grp[(t - 1, NCH - 1)])]
                op('v', lambda e, b=banks[bi], o=o, t=t,
                   d=h1buf.ap()[:, TT * o:TT * (o + 1)],
                   s1=s1b.ap()[:, TT * t:TT * (t + 1)]:
                   e.tensor_tensor(d, b.ap()[:, :], s1, ALU.mult),
                   sm=s_v, waits=vw)
                h1ev[(t, o)] = s_v.n
                BK.set(bi, s_v, s_v.n)
            for o in range(NCH):
                bi, bw = BK.take()
                items = [(banks[bi].ap()[:, :],
                          w2_16.ap()[:, D * k + 128 * o:D * k + 128 * (o + 1)],
                          h1buf.ap()[:, TT * k:TT * (k + 1)],
                          k == 0, k == NCH - 1) for k in range(NCH)]
                t_group(items, waits=bw + [(s_v.h, h1ev[(t, NCH - 1)])])
                h2grp[(t, o)] = s_t.n
                op('s', lambda e, b=banks[bi], o=o, t=t,
                   d=h2all.ap()[:, L * o + TT * t:L * o + TT * (t + 1)]:
                   e.activation(d, b.ap()[:, :], AF.Copy),
                   sm=s_s, waits=[(s_t.h, h2grp[(t, o)])])
                BK.set(bi, s_s, s_s.n)
        s_after_h2 = s_s.n
        dbg("h2all", h2all[:], [128, NCH * L], F16, [(s_s.h, s_after_h2)])

        mem.free("x16slot0", "x16slot1", "h1buf", "w1_16", "w2_16",
                 "negmu_r", "woutT2f_s")
        x32s_ = [mem.alloc(f"x32slot{i}", [128, L], F32) for i in range(3)]
        ftmp = mem.alloc("ftmp", [128, 4 * L], F32)
        vtmp = mem.alloc("vtmp", [128, 4 * TT], F32)
        outstage = mem.alloc("outstage", [128, 4 * TT], F32)

        BAR2 = barrier()
        st_q = (q_st0, q_st1, q_st2, q_st3)
        st_g = (None, g_st1, None, g_st3)
        slot_guard = {}
        j = 0
        fin_done = {}
        for o in range(NCH):
            qx = (q_x0, q_x1, q_x2)[o % 3]
            if o < 3:
                w = list(BAR2)
            else:
                fs, fv = fin_done[(o - 3, NT - 1)]
                w = [(fs.h, fv)]
            dma('q', x32s_[o % 3][:], xTd[128 * o:128 * (o + 1), :], qx,
                waits=w)
            bks = [BK.take() for _ in range(NT)]
            bw = [w2 for _, ws in bks for w2 in ws]
            items = []
            for k in range(NCH):
                for t in range(NT):
                    items.append((banks[bks[t][0]].ap()[:, :],
                                  w3_16.ap()[:, D * k + 128 * o:D * k + 128 * (o + 1)],
                                  h2all.ap()[:, L * k + TT * t:L * k + TT * (t + 1)],
                                  k == 0, k == NCH - 1))
            t_group(items, waits=bw + W34LD + [(s_s.h, s_after_h2)])
            h3cnt = s_t.n
            fsl_o = ftmp.ap()[:, L * (o % 4):L * (o % 4 + 1)]
            sw = [(s_t.h, h3cnt)]
            if o >= 4:
                fs, fv = fin_done[(o - 4, NT - 1)]
                sw.append((fs.h, fv))
            for t in range(NT):
                bi = bks[t][0]
                op('s', lambda e, b=banks[bi], o=o,
                   d=fsl_o[:, TT * t:TT * (t + 1)]:
                   e.activation(d, b.ap()[:, :], AF.Tanh, bias=fb_col(o)),
                   sm=s_s, waits=sw)
                sw = []
                BK.set(bi, s_s, s_s.n)
            s_tanh_o = s_s.n
            pre_done = {}
            for t in range(NT):
                jj = o * NT + t
                vsl = vtmp.ap()[:, TT * (jj % 4):TT * (jj % 4 + 1)]
                peng, psem = ('v', s_v) if t % 2 == 0 else ('g', s_g)
                pw_ = [(s_s.h, s_tanh_o), (qx.h, qx.n)]
                if jj >= 4:
                    fs2, fv2 = fin_done[divmod(jj - 4, NT)]
                    pw_.append((fs2.h, fv2))
                op(peng, lambda e, v2=vsl, o=o, t=t,
                   x=x32s_[o % 3].ap()[:, TT * t:TT * (t + 1)],
                   a=alpha_b.ap()[:, TT * t:TT * (t + 1)]:
                   e.tensor_tensor(v2, x, a, ALU.mult), sm=psem, waits=pw_)
                op(peng, lambda e, v2=vsl, f=fsl_o[:, TT * t:TT * (t + 1)]:
                   e.tensor_tensor(v2, v2, f, ALU.add), sm=psem,
                   waits=[(psem.h, psem.n)])
                pre_done[t] = (psem, psem.n)
            bks2 = [BK.take() for _ in range(NT)]
            bw2 = [w2 for _, ws in bks2 for w2 in ws]
            items = []
            for k in range(NCH):
                for t in range(NT):
                    items.append((banks[bks2[t][0]].ap()[:, :],
                                  outw16.ap()[:, D * k + 128 * o:D * k + 128 * (o + 1)],
                                  y16all.ap()[:, L * k + TT * t:L * k + TT * (t + 1)],
                                  k == 0, False))
            for t in range(NT):
                items.append((banks[bks2[t][0]].ap()[:, :], onesr16b[0:1, :],
                              beta16.ap()[0:1, TT * t:TT * (t + 1)],
                              False, True))
            t_group(items, waits=bw2 + [(s_v.h, v_after_y16),
                                        (s_g.h, g_after_y16)])
            opcnt = s_t.n
            for t in range(NT):
                bi2 = bks2[t][0]
                slot = j % 4
                osl = outstage.ap()[:, TT * slot:TT * (slot + 1)]
                ring = 'q' if (t % 2 == 0 or o == NCH - 1) else 'g'
                stsem = st_q[slot] if ring == 'q' else st_g[slot]
                sg = [slot_guard[slot]] if slot in slot_guard else []
                op('s', lambda e, b=banks[bi2], o=o, d=osl:
                   e.add(d, b.ap()[:, :], outb_col(o)),
                   sm=s_s, waits=[(s_t.h, opcnt)] + sg)
                BK.set(bi2, s_s, s_s.n)
                feng, fsem = ('g', s_g) \
                    if (t % 2 == 0 and o < NCH - 1) else ('v', s_v)
                fw = [(s_s.h, s_s.n), (pre_done[t][0].h, pre_done[t][1])]
                if feng == 'g':
                    fw.append((s_g.h, s_g.n))
                op(feng, lambda e, d=osl,
                   v2=vtmp.ap()[:, TT * (j % 4):TT * (j % 4 + 1)]:
                   e.tensor_tensor(d, d, v2, ALU.add), sm=fsem, waits=fw)
                fin_done[(o, t)] = (fsem, fsem.n)
                dma(ring, outd[t, o], osl, stsem,
                    waits=[(fin_done[(o, t)][0].h, fin_done[(o, t)][1])])
                slot_guard[slot] = (stsem.h, stsem.n)
                j += 1

        for eng in 'gtvs':
            emit(eng, lambda e, dh=done.h: e.nop().then_inc(dh, 1))

        dma_finals = [(s.h, s.n) for s in
                      [q_sm, q_sm2, q_x0, q_x1, q_x2, q_x3,
                       q_st0, q_st1, q_st2, q_st3,
                       g_st1, g_st3,
                       g_wx, g_wx2, g_wz,
                       g_pw, g_w12, g_w34, g_sm, g_x0, g_x1, q_dbg, q_xtb]
                      + q_xt + [s_t, s_v, s_s, s_g]]

        def fin(e, a=q_st0.n, b=q_st1.n, fins=tuple(dma_finals), dh=done.h,
                bh=boot.h):
            e.wait_ge(q_st0.h, a)
            e.wait_ge(q_st1.h, b)
            e.wait_ge(dh, 4)
            for (sh, v) in fins:
                if v > 0:
                    e.wait_ge(sh, v)
            e.sem_clear(bh)
        emit('q', fin)

        blk = es.enter_context(nc.Block())

        @blk.sync
        def _(eng):
            for f in ops['q']:
                f(eng)

        @blk.gpsimd
        def _(eng):
            for f in ops['g']:
                f(eng)

        @blk.tensor
        def _(eng):
            for f in ops['t']:
                f(eng)

        @blk.vector
        def _(eng):
            for f in ops['v']:
                f(eng)

        @blk.scalar
        def _(eng):
            for f in ops['s']:
                f(eng)

    return nc


_CACHE = {}


def host_prep(inputs):
    inp = {k: np.asarray(v) for k, v in inputs.items()}
    f32 = np.float32
    f16 = np.float16

    T = np.ascontiguousarray(inp["L0_attn_T"].astype(f32, copy=False))
    in_wT = np.ascontiguousarray(inp["L0_attn_in_w"].astype(f32).T)
    pw_wT = np.ascontiguousarray(inp["L0_attn_pw_w"].astype(f32).T)
    out_wT = np.ascontiguousarray(inp["L0_attn_out_w"].astype(f32).T)
    w1 = np.ascontiguousarray(inp["L0_ffn_weight"].astype(f32))
    w2 = np.ascontiguousarray(inp["L0_ffn_aux_weight1"].astype(f32))
    w3 = np.ascontiguousarray(inp["L0_ffn_aux_weight2"].astype(f32))
    winT2 = np.ascontiguousarray(
        np.concatenate([inp["L0_attn_f_win"], inp["L0_attn_b_win"]], axis=0)
        .astype(f32).T)
    woutT2 = np.ascontiguousarray(
        np.concatenate([inp["L0_attn_f_wout"].astype(f32).T,
                        inp["L0_attn_b_wout"].astype(f32).T], axis=0))
    ssm = np.stack([
        np.concatenate([inp["L0_attn_f_logA"], inp["L0_attn_b_logA"]]),
        np.concatenate([inp["L0_attn_f_B"], inp["L0_attn_b_B"]]),
        np.concatenate([inp["L0_attn_f_C"], inp["L0_attn_b_C"]]),
        np.concatenate([np.broadcast_to(inp["L0_attn_f_logdt"], (64,)),
                        np.broadcast_to(inp["L0_attn_b_logdt"], (64,))]),
    ], axis=1).astype(f32)
    ssm = np.ascontiguousarray(ssm)

    def cols8(v):
        return np.ascontiguousarray(v.astype(f32).reshape(8, 128).T)

    vec = np.concatenate([
        cols8(inp["L0_attn_norm_scale"]),
        cols8(inp["L0_attn_in_b"][:1024]),
        cols8(inp["L0_attn_in_b"][1024:]),
        cols8(inp["L0_attn_dw_b"]),
        cols8(inp["L0_attn_pw_b"]),
        cols8(inp["L0_ffn_bias"]),
        cols8(inp["L0_attn_out_b"]),
    ], axis=1)
    dww = (inp["L0_attn_dw_w"].astype(f32).reshape(8, 128, 3)
           .transpose(1, 2, 0).reshape(128, 24))
    pack = np.ascontiguousarray(np.concatenate([ssm, vec, dww], axis=1))

    shared = dict(
        in_wT=in_wT, pw_wT=pw_wT, out_wT=out_wT, w1=w1, w2=w2, w3=w3,
        winT2=winT2, woutT2=woutT2, pack=pack,
        onesr32=np.ones((128, 128), f32))

    return shared, T


def kernel(**inputs):
    shared, T = host_prep(inputs)

    if "nc" not in _CACHE:
        _CACHE["nc"] = build_program()
    nc = _CACHE["nc"]

    in_maps = []
    for b in range(8):
        m = dict(shared)
        m["xT"] = np.ascontiguousarray(T[b].T)
        in_maps.append(m)

    res = run_bass_kernel_spmd(nc, in_maps, core_ids=list(range(8)))

    out = np.empty((8, L, D), np.float32)
    for b in range(8):
        a = res.results[b]["outT"]
        out[b] = a.transpose(0, 3, 1, 2).reshape(L, D)
    return out


if __name__ == "__main__":
    pass
